# version 06
# baseline (speedup 1.0000x reference)
import numpy as np
from contextlib import ExitStack

import concourse.bass as bass
import concourse.mybir as mybir
from concourse.bass_utils import run_bass_kernel_spmd

F32 = mybir.dt.float32
BF16 = mybir.dt.bfloat16
ALU = mybir.AluOpType
AF = mybir.ActivationFunctionType
AX = mybir.AxisListType

C1 = 1.5 * 2.0**23
C2 = 1.5 * 2.0**52

ENGINES = ("sp", "pe", "act", "dve", "pool")


class Op:
    __slots__ = ("idx", "engine", "emit", "deps", "need_inc", "inc_val", "dma_ch",
                 "dma_pos", "group", "tag")

    def __init__(self, idx, engine, emit, dma_ch, group, tag):
        self.idx = idx
        self.engine = engine
        self.emit = emit
        self.deps = []
        self.need_inc = False
        self.inc_val = None
        self.dma_ch = dma_ch
        self.dma_pos = None
        self.group = group
        self.tag = tag


class Prog:
    def __init__(self):
        self.ops = []
        self.access = {}
        self.chan_counts = {}

    def op(self, engine, emit, reads=(), writes=(), dma_ch=None, group=None, tag=""):
        idx = len(self.ops)
        o = Op(idx, engine, emit, dma_ch, group, tag)
        if dma_ch is not None:
            self.chan_counts[dma_ch] = self.chan_counts.get(dma_ch, 0) + 1
            o.dma_pos = self.chan_counts[dma_ch]
        deps = set()
        for r in reads:
            self._scan(r, False, idx, deps, group)
        for w in writes:
            self._scan(w, True, idx, deps, group)
        o.deps = sorted(deps)
        self.ops.append(o)
        for r in reads:
            self._record(r, idx, False)
        for w in writes:
            self._record(w, idx, True)
        return idx

    @staticmethod
    def _overlap(a, b):
        return a[0] < b[1] and b[0] < a[1] and a[2] < b[3] and b[2] < a[3]

    def _scan(self, region, is_write, idx, deps, group):
        name, p0, p1, c0, c1 = region
        lst = self.access.get(name)
        if not lst:
            return
        me = (p0, p1, c0, c1)
        for rec in reversed(lst):
            oidx, w, q0, q1, d0, d1 = rec
            if not (w or is_write):
                continue
            if self._overlap(me, (q0, q1, d0, d1)):
                o = self.ops[oidx]
                if group is not None and o.group == group:
                    continue
                deps.add(oidx)

    def _record(self, region, idx, is_write):
        name, p0, p1, c0, c1 = region
        lst = self.access.setdefault(name, [])
        if is_write and p0 == 0 and c0 == 0:
            keep = []
            for rec in lst:
                if not (rec[2] >= p0 and rec[3] <= p1 and rec[4] >= c0 and rec[5] <= c1):
                    keep.append(rec)
            if len(keep) != len(lst):
                self.access[name] = lst = keep
        lst.append([idx, is_write, p0, p1, c0, c1])
        if len(lst) > 4000:
            del lst[: len(lst) - 3000]

    def emit(self, nc, es):
        ops = self.ops
        for o in ops:
            for d in o.deps:
                p = ops[d]
                if p.dma_ch is None:
                    p.need_inc = True
        counters = {e: 0 for e in ENGINES}
        for o in ops:
            if o.dma_ch is None and o.need_inc:
                counters[o.engine] += 1
                o.inc_val = counters[o.engine]

        eng_sems = {e: es.enter_context(nc.semaphore(f"es_{e}")) for e in ENGINES}
        ch_sems = {c: es.enter_context(nc.semaphore(f"ch_{c}")) for c in self.chan_counts}
        import bisect as _bisect
        chan_idxs = {}
        for o in ops:
            if o.dma_ch is not None:
                chan_idxs.setdefault(o.dma_ch, []).append(o.idx)

        per_engine = {e: [o for o in ops if o.engine == e] for e in ENGINES}
        emitted_ok = []

        def make_body(ename):
            def body(eng):
                watermark = {}
                for o in per_engine[ename]:
                    needs = {}
                    for d in o.deps:
                        p = ops[d]
                        if p.dma_ch is not None:
                            key = ("ch", p.dma_ch)
                            if p.dma_ch.startswith("const"):
                                val = 16 * self.chan_counts[p.dma_ch]
                            else:
                                val = 16 * _bisect.bisect_left(chan_idxs[p.dma_ch], o.idx)
                            if needs.get(key, 0) < val:
                                needs[key] = val
                        else:
                            key = ("e", p.engine)
                            if needs.get(key, 0) < p.inc_val:
                                needs[key] = p.inc_val
                    for key, val in needs.items():
                        if watermark.get(key, 0) < val:
                            sem = ch_sems[key[1]] if key[0] == "ch" else eng_sems[key[1]]
                            eng.wait_ge(sem, val)
                            watermark[key] = val
                            self.emit_stats[(ename, "wait_" + key[0])] += 1
                    try:
                        inst = o.emit(eng)
                    except Exception as ex:
                        raise RuntimeError(f"emit failed at op {o.idx} tag={o.tag!r}") from ex
                    if o.dma_ch is not None:
                        inst.then_inc(ch_sems[o.dma_ch], 16)
                    elif o.need_inc:
                        inst.then_inc(eng_sems[o.engine], 1)
                    emitted_ok.append(o.idx)
                done = set()
                for o in per_engine[ename]:
                    if o.dma_ch is not None and o.dma_ch not in done:
                        done.add(o.dma_ch)
                for c in done:
                    eng.wait_ge(ch_sems[c], 16 * self.chan_counts[c])
            return body

        import collections as _c
        self.emit_stats = _c.Counter()
        block = es.enter_context(nc.Block())
        block.sync(make_body("sp"))
        block.tensor(make_body("pe"))
        block.scalar(make_body("act"))
        block.vector(make_body("dve"))
        block.gpsimd(make_body("pool"))
        assert len(emitted_ok) == len(ops)


class TV:

    __slots__ = ("h", "name", "p0", "p1", "c0", "c1")

    def __init__(self, h, name, p0, p1, c0, c1):
        self.h, self.name, self.p0, self.p1, self.c0, self.c1 = h, name, p0, p1, c0, c1

    def ap(self):
        return self.h.ap()[self.p0:self.p1, self.c0:self.c1]

    def reg(self):
        return (self.name, self.p0, self.p1, self.c0, self.c1)

    def view(self, p0, p1, c0, c1):
        return TV(self.h, self.name, self.p0 + p0, self.p0 + p1,
                  self.c0 + c0, self.c0 + c1)


def _tv(h, name, P, C):
    return TV(h, name, 0, P, 0, C)


def default_cfg():
    return dict(
        T=4096,
        TT=512,
        D=1024,
        DI=2048,
        NST=16,
        DTR=64,
        FH=8192,
        FI=4096,
        H=4096,
        EPS=1e-5,
        stop_after=None,
        debug=(),
    )


def build_program(cfg):
    T, TT, D, DI = cfg["T"], cfg["TT"], cfg["D"], cfg["DI"]
    NST, DTR, FH, FI, H = cfg["NST"], cfg["DTR"], cfg["FH"], cfg["FI"], cfg["H"]
    EPS = cfg["EPS"]
    NTILE = T // TT
    NCH = TT // 128
    DB = DI // 128
    MB_D = D // 128
    QD = 4
    NQ = DB // QD

    nc = bass.Bass("TRN2", target_bir_lowering=False)
    p = Prog()

    def din(name, shape):
        return nc.dram_tensor(name, list(shape), F32, kind="ExternalInput").ap()

    x_in = din("x", (T, D))
    w_in = din("w_in", (D, 2 * DI))
    conv_w = din("conv_w", (DI, 4))
    conv_b = din("conv_b", (DI,))
    w_x = din("w_x", (DI, DTR + 2 * NST))
    w_dt = din("w_dt", (DTR, DI))
    b_dt = din("b_dt", (DI,))
    a_log = din("a_log", (DI, NST))
    d_skip = din("d_skip", (DI,))
    w_out = din("w_out", (DI, D))
    ln1_g, ln1_b = din("ln1_g", (D,)), din("ln1_b", (D,))
    ln2_g, ln2_b = din("ln2_g", (D,)), din("ln2_b", (D,))
    w_ff1 = din("w_ff1", (D, FH))
    b_ff1 = din("b_ff1", (FH,))
    w_ff2 = din("w_ff2", (FI, D))
    b_ff2 = din("b_ff2", (D,))
    fg1, fb1 = din("f_ln1_g", (D,)), din("f_ln1_b", (D,))
    fg2, fb2 = din("f_ln2_g", (H,)), din("f_ln2_b", (H,))
    w1 = din("w1", (H, D))
    b1 = din("b1", (H,))
    w2 = din("w2", (D, H))
    b2 = din("b2", (D,))
    ident_in = din("ident_hbm", (128, 128))
    eps40_in = din("eps40_hbm", (128, 2))

    out_hbm = nc.dram_tensor("out", [T, D], F32, kind="ExternalOutput").ap()

    scr_w1q = nc.dram_tensor("scr_w1q", [D, H], BF16, kind="Internal").ap()
    scr_w2q = nc.dram_tensor("scr_w2q", [H, D], BF16, kind="Internal").ap()
    scr_rows = nc.dram_tensor("scr_rows", [64, TT], F32, kind="Internal").ap()
    scr_ag = nc.dram_tensor("scr_ag", [FI, 2 * TT], BF16, kind="Internal").ap()
    scr_wz = nc.dram_tensor("scr_wz", [D, 2 * 2 * DI], BF16, kind="Internal").ap()
    scr_f1 = nc.dram_tensor("scr_f1", [D, 2 * FH], BF16, kind="Internal").ap()
    scr_f2 = nc.dram_tensor("scr_f2", [FI, 2 * D], BF16, kind="Internal").ap()
    scr_g1 = nc.dram_tensor("scr_g1", [H, TT], F32, kind="Internal").ap()
    scr_sw = nc.dram_tensor("scr_sw", [2], F32, kind="Internal").ap()

    dbg_out = {}
    for name in cfg["debug"]:
        pass

    es = ExitStack()

    def sb(name, P, C, dt=F32):
        h = es.enter_context(nc.sbuf_tensor(name, [P, C], dt))
        return _tv(h, name, P, C)

    def psum(name, P, C):
        h = es.enter_context(nc.psum_tensor(name, [P, C], F32))
        return _tv(h, name, P, C)

    with es:
        ident = sb("ident", 128, 128)
        eps40 = sb("eps40", 128, 2)
        ind_pp = sb("ind_pp", 128, 1)
        c_convw = sb("c_convw", 128, DB * 4)
        c_convb = sb("c_convb", 128, DB)
        c_bdt = sb("c_bdt", 128, DB)
        c_dskip = sb("c_dskip", 128, DB)
        c_Aneg = sb("c_Aneg", 128, DB * NST)
        c_ln = sb("c_ln", 128, 4 * (D // 128))
        c_fln2 = sb("c_fln2", 128, 2 * (H // 128))
        c_bff1 = sb("c_bff1", 128, FH // 128)
        c_bff2 = sb("c_bff2", 128, D // 128)
        c_b1 = sb("c_b1", 128, H // 128)
        c_b2 = sb("c_b2", 128, D // 128)
        c_sw = sb("c_sw", 128, 2)
        h_state = sb("h_state", 128, DB * NST)

        x_tm = sb("x_tm", 128, NCH * D)
        a_bf = sb("a_bf", 128, 2 * MB_D * TT, BF16)
        xcs = sb("xcs", 128, DB * (TT + 4))
        dt_fm = sb("dt_fm", 128, max(DB * TT, H))
        y_acc = sb("y_acc", 128, DB * TT)
        v_q = sb("v_q", 128, max(2 * TT, D))
        h_q = sb("h_q", 128, QD * TT)
        u_q = sb("u_q", 128, QD * TT)
        dA_t = sb("dA_t", 128, 2 * TT)
        bc_rep = sb("bc_rep", 128, 4 * TT)
        wsl = sb("wsl", 128, 2 * 1024)
        wslb = sb("wslb", 128, 2 * 2048, BF16)
        qbf = sb("qbf", 128, max(FH, 2 * D), BF16)
        srow = sb("srow", 128, 8)
        sr_rep = sb("sr_rep", 128, max(2 * TT, D))
        tiny = sb("tiny", 128, 16)

        ps_mm = [psum(f"ps{i}", 128, 512) for i in range(8)]

        def dma(ch, dst, src, engine="sp", tag="", slow=False):
            def e(eng, dst=dst, src=src, slow=slow):
                return eng.dma_start(out=dst.ap() if isinstance(dst, TV) else dst,
                                     in_=src.ap() if isinstance(src, TV) else src,
                                     allow_slow_non_contiguous=slow)
            reads = [src.reg()] if isinstance(src, TV) else []
            writes = [dst.reg()] if isinstance(dst, TV) else []
            p.op(engine, e, reads=reads, writes=writes, dma_ch=ch, tag=tag)

        def dma_hbm(ch, dst_ap, src, hbm_name, hr, engine="sp", tag=""):
            def e(eng, dst_ap=dst_ap, src=src):
                return eng.dma_start(out=dst_ap, in_=src.ap())
            p.op(engine, e, reads=[src.reg()],
                 writes=[(hbm_name,) + hr], dma_ch=ch, tag=tag)

        def dma_from_hbm(ch, dst, src_ap, hbm_name, hr, engine="sp", tag=""):
            def e(eng, dst=dst, src_ap=src_ap):
                return eng.dma_start(out=dst.ap(), in_=src_ap)
            p.op(engine, e, reads=[(hbm_name,) + hr], writes=[dst.reg()],
                 dma_ch=ch, tag=tag)

        def act(out, in_, func, scale=None, bias=None, extra_reads=(), eng="act", tag=""):
            def e(engine, out=out, in_=in_, func=func, scale=scale, bias=bias):
                kw = {}
                if scale is not None:
                    kw["scale"] = scale.ap() if isinstance(scale, TV) else scale
                if bias is not None:
                    kw["bias"] = bias.ap() if isinstance(bias, TV) else bias
                return engine.activation(out=out.ap(), in_=in_.ap(), func=func, **kw)
            reads = [in_.reg()]
            if isinstance(scale, TV):
                reads.append(scale.reg())
            if isinstance(bias, TV):
                reads.append(bias.reg())
            reads.extend(r.reg() for r in extra_reads)
            p.op(eng, e, reads=reads, writes=[out.reg()], tag=tag)

        def tt(out, in0, in1, op, eng="dve", tag=""):
            def e(engine, out=out, in0=in0, in1=in1, op=op):
                return engine.tensor_tensor(out=out.ap(), in0=in0.ap(), in1=in1.ap(), op=op)
            p.op(eng, e, reads=[in0.reg(), in1.reg()], writes=[out.reg()], tag=tag)

        def tt_bcast(out, in0, in1, op, reps, eng="dve", tag=""):
            def e(engine, out=out, in0=in0, in1=in1, op=op, reps=reps):
                w = in1.c1 - in1.c0
                ap1 = in1.ap().rearrange("p (r w) -> p r w", r=1).broadcast_to(
                    (128, reps, w))
                ap0 = in0.ap().rearrange("p (r w) -> p r w", r=reps)
                apo = out.ap().rearrange("p (r w) -> p r w", r=reps)
                return engine.tensor_tensor(out=apo, in0=ap0, in1=ap1, op=op)
            p.op(eng, e, reads=[in0.reg(), in1.reg()], writes=[out.reg()], tag=tag)

        def ts(out, in_, s1, op1, s2=None, op2=None, eng="dve", tag=""):
            def e(engine, out=out, in_=in_, s1=s1, op1=op1, s2=s2, op2=op2):
                a1 = s1.ap() if isinstance(s1, TV) else s1
                if s2 is None:
                    return engine.tensor_single_scalar(out=out.ap(), in_=in_.ap(),
                                                       scalar=a1, op=op1)
                a2 = s2.ap() if isinstance(s2, TV) else s2
                return engine.tensor_scalar(out=out.ap(), in0=in_.ap(), scalar1=a1,
                                            scalar2=a2, op0=op1, op1=op2)
            reads = [in_.reg()]
            for s in (s1, s2):
                if isinstance(s, TV):
                    reads.append(s.reg())
            p.op(eng, e, reads=reads, writes=[out.reg()], tag=tag)

        def stt(out, in0, scalar, in1, op0, op1, eng="dve", tag=""):
            def e(engine, out=out, in0=in0, scalar=scalar, in1=in1, op0=op0, op1=op1):
                s = scalar.ap() if isinstance(scalar, TV) else scalar
                return engine.scalar_tensor_tensor(out=out.ap(), in0=in0.ap(), scalar=s,
                                                   in1=in1.ap(), op0=op0, op1=op1)
            reads = [in0.reg(), in1.reg()]
            if isinstance(scalar, TV):
                reads.append(scalar.reg())
            p.op(eng, e, reads=reads, writes=[out.reg()], tag=tag)

        def scan_op(out, data0, data1, initial, tag=""):
            def e(engine, out=out, data0=data0, data1=data1, initial=initial):
                ini = initial.ap() if isinstance(initial, TV) else initial
                return engine.tensor_tensor_scan(out=out.ap(), data0=data0.ap(),
                                                 data1=data1.ap(), initial=ini,
                                                 op0=ALU.mult, op1=ALU.add)
            reads = [data0.reg(), data1.reg()]
            if isinstance(initial, TV):
                reads.append(initial.reg())
            p.op("dve", e, reads=reads, writes=[out.reg()], tag=tag)

        def reduce(out, in_, op, absval=False, eng="dve", tag=""):
            def e(engine, out=out, in_=in_, op=op, absval=absval):
                return engine.tensor_reduce(out=out.ap(), in_=in_.ap(), axis=AX.X,
                                            op=op, apply_absolute_value=absval)
            p.op(eng, e, reads=[in_.reg()], writes=[out.reg()], tag=tag)

        def recip(out, in_, tag=""):
            def e(engine, out=out, in_=in_):
                return engine.reciprocal(out=out.ap(), in_=in_.ap())
            p.op("dve", e, reads=[in_.reg()], writes=[out.reg()], tag=tag)

        def copy(out, in_, eng="act", tag=""):
            if eng == "act":
                act(out, in_, AF.Copy, tag=tag)
            else:
                def e(engine, out=out, in_=in_):
                    return engine.tensor_copy(out=out.ap(), in_=in_.ap())
                p.op(eng, e, reads=[in_.reg()], writes=[out.reg()], tag=tag)

        def mm(out, lhsT, rhs, start, stop, group, tag=""):
            def e(engine, out=out, lhsT=lhsT, rhs=rhs, start=start, stop=stop):
                return engine.matmul(out=out.ap(), lhsT=lhsT.ap(), rhs=rhs.ap(),
                                     start=start, stop=stop)
            p.op("pe", e, reads=[lhsT.reg(), rhs.reg()], writes=[out.reg()],
                 group=group, tag=tag)

        def transpose(out_ps, in_, tag=""):
            def e(engine, out_ps=out_ps, in_=in_):
                return engine.transpose(out=out_ps.ap(), in_=in_.ap(),
                                        identity=ident.ap())
            p.op("pe", e, reads=[in_.reg(), ident.reg()], writes=[out_ps.reg()],
                 tag=tag)

        def memset(buf, val, eng="dve", tag=""):
            def e(engine, buf=buf, val=val):
                return engine.memset(buf.ap(), val)
            p.op(eng, e, reads=[], writes=[buf.reg()], tag=tag)

        dbg_seq = [0]

        def debug_dump(name, tv_list, dt=F32):
            if name not in cfg["debug"]:
                return
            P = 128
            Ctot = sum(t.c1 - t.c0 for t in tv_list)
            d = nc.dram_tensor(f"dbg_{name}", [P, Ctot], dt, kind="ExternalOutput").ap()
            c = 0
            for t in tv_list:
                w = t.c1 - t.c0
                def e(eng, t=t, d=d, c=c, w=w):
                    return eng.dma_start(out=d[:, c:c + w], in_=t.ap())
                p.op("pool", e, reads=[t.reg()], writes=[], dma_ch="dbg", tag=f"dbg_{name}")
                c += w
            dbg_out[name] = d

        def debug_dump_hbm(name, src_name, src_hbm_ap, shape, dt=F32):
            if name not in cfg["debug"]:
                return
            d = nc.dram_tensor(f"dbg_{name}", list(shape), dt, kind="ExternalOutput").ap()
            def e(eng, d=d, src_hbm_ap=src_hbm_ap):
                return eng.dma_start(out=d, in_=src_hbm_ap)
            p.op("pool", e, reads=[(src_name, 0, shape[0], 0, shape[1])], writes=[],
                 dma_ch="dbg", tag=f"dbg_{name}")

        dma("const", ident, ident_in)
        dma("const", eps40, eps40_in)
        ts(ind_pp, eps40.view(0, 128, 0, 1), 1.0, ALU.add, 1.0, ALU.subtract, tag="ind1")
        ts(ind_pp, ind_pp, 2.0**40, ALU.mult, tag="ind2")

        def dma3(ch, dst, src3, a, b, tag=""):
            def e(eng, dst=dst, src3=src3, a=a, b=b):
                return eng.dma_start(
                    out=dst.ap().rearrange("p (a b) -> p a b", a=a),
                    in_=src3, allow_slow_non_contiguous=True)
            p.op("sp", e, reads=[], writes=[dst.reg()], dma_ch=ch, tag=tag)

        dma3("const", c_convw, conv_w.rearrange("(db p) k -> p db k", p=128),
             DB, 4)
        dma("const", c_convb, conv_b.rearrange("(db p) -> p db", p=128), slow=True)
        dma("const", c_bdt, b_dt.rearrange("(db p) -> p db", p=128), slow=True)
        dma("const", c_dskip, d_skip.rearrange("(db p) -> p db", p=128), slow=True)
        dma3("const", c_Aneg, a_log.rearrange("(db p) n -> p db n", p=128),
             DB, NST)
        act(c_Aneg, c_Aneg, AF.Exp, tag="Aexp")
        ts(c_Aneg, c_Aneg, -1.0, ALU.mult, tag="Aneg")
        for i, v in enumerate((ln1_g, ln1_b, ln2_g, ln2_b)):
            dma("const", c_ln.view(0, 128, i * MB_D, (i + 1) * MB_D),
                v.rearrange("(m p) -> p m", p=128), slow=True)
        dma("const", c_fln2.view(0, 128, 0, H // 128), fg2.rearrange("(m p) -> p m", p=128), slow=True)
        dma("const", c_fln2.view(0, 128, H // 128, 2 * H // 128),
            fb2.rearrange("(m p) -> p m", p=128), slow=True)
        dma("const", c_bff1, b_ff1.rearrange("(m p) -> p m", p=128), slow=True)
        dma("const", c_bff2, b_ff2.rearrange("(m p) -> p m", p=128), slow=True)
        dma("const", c_b1, b1.rearrange("(m p) -> p m", p=128), slow=True)
        dma("const", c_b2, b2.rearrange("(m p) -> p m", p=128), slow=True)
        memset(h_state, 0.0, tag="h0")

        def prep_w(w_hbm, name, rows, cols, scr, sw_col):
            RB = rows // 128
            CBLK = cols // 128
            acc = tiny.view(0, 128, 0, 1)
            t1 = tiny.view(0, 128, 1, 2)
            def get_slot(rb):
                if cols <= 1024:
                    return wsl.view(0, 128, (rb % 2) * cols, (rb % 2) * cols + cols)
                return dt_fm.view(0, 128, 0, cols)
            for rb in range(RB):
                slot = get_slot(rb)
                dma(f"pw_{name}{rb % 2}", slot,
                    w_hbm[rb * 128:(rb + 1) * 128, :], tag=f"prep_{name}_ld{rb}")
                reduce(t1, slot, ALU.add, absval=True, tag=f"prep_{name}_sum{rb}")
                if rb == 0:
                    copy(acc, t1, eng="dve")
                else:
                    tt(acc, acc, t1, ALU.add)
            tp = ps_mm[7].view(0, 1, 0, 128)
            transpose(tp, tiny.view(0, 128, 0, 1), tag=f"prep_{name}_tp")
            r0 = tiny.view(0, 1, 2, 3)
            reduce(r0, tp, ALU.add, tag=f"prep_{name}_tot")
            ts(r0, r0, 1.0 / (rows * cols), ALU.mult, 1e-5, ALU.max, eng="dve")
            dma_hbm("pw_sw", scr_sw[sw_col:sw_col + 1].rearrange("(a o) -> a o", o=1), r0,
                    "scr_sw", (0, 1, sw_col, sw_col + 1), engine="pool")
            dma_from_hbm(f"swbc{sw_col}", c_sw.view(0, 128, sw_col, sw_col + 1),
                         scr_sw[sw_col:sw_col + 1].rearrange("(o a) -> o a", o=1).broadcast_to((128, 1)),
                         "scr_sw", (0, 1, sw_col, sw_col + 1), engine="pool")
            swc = c_sw.view(0, 128, sw_col, sw_col + 1)
            for rb in range(RB):
                slot = get_slot(rb)
                dma(f"pw_{name}{rb % 2}", slot,
                    w_hbm[rb * 128:(rb + 1) * 128, :], tag=f"prep2_{name}_ld{rb}")
                qs = y_acc.view(0, 128, 0, cols)
                qs2 = xcs.view(0, 128, 0, cols)
                thr = tiny.view(0, 128, 12, 13)
                nthr = tiny.view(0, 128, 13, 14)
                if rb == 0:
                    ts(thr, swc, 0.5, ALU.mult, tag=f"prep_{name}_thr")
                    ts(nthr, thr, -1.0, ALU.mult, tag=f"prep_{name}_nthr")
                ts(qs2, slot, thr, ALU.is_ge, tag=f"prep_{name}_ge{rb}")
                ts(qs, slot, nthr, ALU.is_le, tag=f"prep_{name}_le{rb}")
                tt(qs, qs2, qs, ALU.subtract, tag=f"prep_{name}_q{rb}")
                for cb in range(CBLK):
                    tp2 = ps_mm[6].view(0, 128, 0, 128)
                    transpose(tp2, qs.view(0, 128, cb * 128, (cb + 1) * 128),
                              tag=f"prep_{name}_t{rb}_{cb}")
                    bslot = wslb.view(0, 128, 0, 128)
                    copy(bslot, tp2, eng="dve")
                    dma_hbm(f"pw_st_{name}", scr[cb * 128:(cb + 1) * 128,
                                                rb * 128:(rb + 1) * 128],
                            bslot, "scr_" + name,
                            (cb * 128, (cb + 1) * 128, rb * 128, (rb + 1) * 128),
                            engine="pool")

        prep_w(w1, "w1q", H, D, scr_w1q, 0)
        prep_w(w2, "w2q", D, H, scr_w2q, 1)

        def prep_split(w_hbm, name, rows, cols, scr):
            RB = rows // 128
            CH = max(1, cols // 4096)
            cw = cols // CH
            for rb in range(RB):
                for ch in range(CH):
                    slot = dt_fm.view(0, 128, 0, cw)
                    dma(f"psl_{name}", slot,
                        w_hbm[rb * 128:(rb + 1) * 128, ch * cw:(ch + 1) * cw],
                        tag=f"psl_{name}_{rb}_{ch}")
                    hi = TV(qbf.h, qbf.name, 0, 128, 0, cw)
                    lo = TV(qbf.h, qbf.name, 0, 128, cw, 2 * cw)
                    copy(hi, slot, eng="act", tag=f"psh_{name}_{rb}_{ch}")
                    stt(lo, hi, -1.0, slot, ALU.mult, ALU.add,
                        tag=f"pslo_{name}_{rb}_{ch}")
                    dma_hbm(f"pst_{name}", scr[rb * 128:(rb + 1) * 128,
                                               ch * cw:(ch + 1) * cw], hi,
                            "scr_" + name, (rb * 128, (rb + 1) * 128, ch * cw,
                                            (ch + 1) * cw), engine="pool",
                            tag=f"psth_{name}_{rb}_{ch}")
                    dma_hbm(f"pst_{name}", scr[rb * 128:(rb + 1) * 128,
                                               cols + ch * cw: cols + (ch + 1) * cw],
                            lo, "scr_" + name, (rb * 128, (rb + 1) * 128,
                                                cols + ch * cw, cols + (ch + 1) * cw),
                            engine="pool", tag=f"pstl_{name}_{rb}_{ch}")

        prep_split(w_in, "wz", D, 2 * DI, scr_wz)
        prep_split(w_ff1, "f1", D, FH, scr_f1)
        prep_split(w_ff2, "f2", FI, D, scr_f2)

        if cfg["stop_after"] == "prep":
            debug_dump("prep", [c_Aneg, c_sw])
            p.emit(nc, es)
            return nc, dbg_out

        for ti in range(NTILE):
            tb = ti * TT

            for c in range(NCH):
                dma(f"x{ti % 2}", x_tm.view(0, 128, c * D, (c + 1) * D),
                    x_in[tb + c * 128: tb + (c + 1) * 128, :], tag=f"ldx{ti}_{c}")

            def tm_ln_stats(src_tv, c, mu, r, tag):
                sums = tiny.view(0, 128, 4, 5)
                sq = tiny.view(0, 128, 5, 6)
                reduce(sums, src_tv, ALU.add, tag=f"{tag}_s")
                ts(mu, sums, 1.0 / (src_tv.c1 - src_tv.c0), ALU.mult)
                sqbuf = sr_rep.view(0, 128, 0, src_tv.c1 - src_tv.c0)
                act(sqbuf, src_tv, AF.Square, tag=f"{tag}_sq")
                reduce(sq, sqbuf, ALU.add, tag=f"{tag}_ss")
                ts(sq, sq, 1.0 / (src_tv.c1 - src_tv.c0), ALU.mult)
                mu2 = tiny.view(0, 128, 6, 7)
                tt(mu2, mu, mu, ALU.mult)
                var = tiny.view(0, 128, 7, 8)
                tt(var, sq, mu2, ALU.subtract)
                act(r, var, AF.Sqrt, bias=eps40.view(0, 128, 1, 2), tag=f"{tag}_r")
                recip(r, r, tag=f"{tag}_rr")

            def ln_tm_to_fm(src_tm, dst_bf, g_col, b_col, tag):
                for c in range(NCH):
                    mu = srow.view(0, 128, 0, 1)
                    r = srow.view(0, 128, 1, 2)
                    src = src_tm.view(0, 128, c * D, (c + 1) * D)
                    tm_ln_stats(src, c, mu, r, f"{tag}{c}")
                    xn = sr_rep.view(0, 128, 0, D)
                    ts(xn, src, mu, ALU.subtract, r, ALU.mult, tag=f"{tag}{c}_n")
                    for j in range(MB_D):
                        tp = ps_mm[6 + (j % 2)].view(0, 128, 0, 128)
                        transpose(tp, xn.view(0, 128, j * 128, (j + 1) * 128),
                                  tag=f"{tag}{c}_t{j}")
                        tmpf = v_q.view(0, 128, (j % 2) * 128, (j % 2) * 128 + 128)
                        ts(tmpf, tp, c_ln.view(0, 128, g_col + j, g_col + j + 1),
                           ALU.mult, c_ln.view(0, 128, b_col + j, b_col + j + 1),
                           ALU.add, tag=f"{tag}{c}_a{j}")
                        hi = TV(dst_bf.h, dst_bf.name, 0, 128,
                                2 * j * TT + c * 128, 2 * j * TT + (c + 1) * 128)
                        lo = TV(dst_bf.h, dst_bf.name, 0, 128,
                                (2 * j + 1) * TT + c * 128,
                                (2 * j + 1) * TT + (c + 1) * 128)
                        copy(hi, tmpf, eng="act", tag=f"{tag}{c}_h{j}")
                        stt(lo, hi, -1.0, tmpf, ALU.mult, ALU.add,
                            tag=f"{tag}{c}_l{j}")

            def abf_h(k):
                return TV(a_bf.h, a_bf.name, 0, 128, 2 * k * TT, 2 * k * TT + TT)

            def abf_l(k):
                return TV(a_bf.h, a_bf.name, 0, 128, (2 * k + 1) * TT,
                          (2 * k + 1) * TT + TT)

            ln_tm_to_fm(x_tm, a_bf, 0, MB_D, f"ln1_{ti}")

            if cfg["stop_after"] == "ln1":
                debug_dump("ln1", [a_bf], dt=BF16)
                break

            for mg in range(2):
                for k in range(MB_D):
                    wslot = wslb.view(0, 128, (k % 2) * 2048, (k % 2) * 2048 + 2048)
                    whi = wslot.view(0, 128, 0, 1024)
                    wlo = wslot.view(0, 128, 1024, 2048)
                    dma(f"wz{k % 2}", whi,
                        scr_wz[k * 128:(k + 1) * 128, mg * 1024:(mg + 1) * 1024],
                        tag=f"winh{ti}_{mg}_{k}")
                    dma(f"wz{k % 2}", wlo,
                        scr_wz[k * 128:(k + 1) * 128,
                               2 * DI + mg * 1024: 2 * DI + (mg + 1) * 1024],
                        tag=f"winl{ti}_{mg}_{k}")
                    for m8 in range(8):
                        m = mg * 8 + m8
                        pso = ps_mm[m8].view(0, 128, 0, TT)
                        mm(pso, whi.view(0, 128, m8 * 128, (m8 + 1) * 128),
                           abf_h(k), start=(k == 0), stop=False,
                           group=f"xz{ti}_{m}", tag=f"xzhh{ti}_{m}_{k}")
                        mm(pso, whi.view(0, 128, m8 * 128, (m8 + 1) * 128),
                           abf_l(k), start=False, stop=False,
                           group=f"xz{ti}_{m}", tag=f"xzhl{ti}_{m}_{k}")
                        mm(pso, wlo.view(0, 128, m8 * 128, (m8 + 1) * 128),
                           abf_h(k), start=False, stop=(k == MB_D - 1),
                           group=f"xz{ti}_{m}", tag=f"xzlh{ti}_{m}_{k}")
                for m8 in range(8):
                    m = mg * 8 + m8
                    dst = xcs.view(0, 128, m * (TT + 4) + 4, m * (TT + 4) + 4 + TT)
                    copy(dst, ps_mm[m8].view(0, 128, 0, TT), eng="act",
                         tag=f"xzev{ti}_{m}")

            if cfg["stop_after"] == "xz":
                debug_dump("xz", [xcs])
                break

            if ti == 0:
                for db in range(DB):
                    memset(xcs.view(0, 128, db * (TT + 4), db * (TT + 4) + 4), 0.0,
                           eng="pool", tag=f"halo0_{db}")
            for db in range(DB):
                base = db * (TT + 4)
                xc_t = lambda sh, db=db, base=base: xcs.view(0, 128, base + 4 - sh,
                                                             base + 4 + TT - sh)
                accv = v_q.view(0, 128, 0, TT)
                ts(accv, xc_t(3), c_convw.view(0, 128, db * 4, db * 4 + 1),
                   ALU.mult, tag=f"cv{ti}_{db}_0")
                for k in range(1, 4):
                    stt(accv, xc_t(3 - k),
                        c_convw.view(0, 128, db * 4 + k, db * 4 + k + 1),
                        accv, ALU.mult, ALU.add, tag=f"cv{ti}_{db}_{k}")
                if ti < NTILE - 1:
                    copy(xcs.view(0, 128, base, base + 4),
                         xcs.view(0, 128, base + TT, base + 4 + TT), eng="pool",
                         tag=f"halo{ti}_{db}")
                act(xcs.view(0, 128, base + 4, base + 4 + TT), accv, AF.Silu,
                    bias=c_convb.view(0, 128, db, db + 1), tag=f"silu{ti}_{db}")

            if cfg["stop_after"] == "conv":
                debug_dump("conv", [xcs])
                break

            NW = DTR + 2 * NST
            wxs = wsl.view(0, 128, 0, DB * NW)
            for k in range(DB):
                dma("wx", wxs.view(0, 128, k * NW, (k + 1) * NW),
                    w_x[k * 128:(k + 1) * 128, :], tag=f"wx{ti}_{k}")
            for k in range(DB):
                mm(ps_mm[5].view(0, 96, 0, TT), wxs.view(0, 128, k * NW, (k + 1) * NW).view(0, 128, 0, NW),
                   xcs.view(0, 128, k * (TT + 4) + 4, k * (TT + 4) + 4 + TT),
                   start=(k == 0), stop=(k == DB - 1), group=f"dbc{ti}",
                   tag=f"dbc{ti}_{k}")
            dbc_sb = sr_rep.view(0, 96, 0, TT)
            copy(dbc_sb, ps_mm[5].view(0, 96, 0, TT), eng="act", tag=f"dbcev{ti}")
            dma_hbm("bc_st", scr_rows[0:2 * NST, 0:TT],
                    dbc_sb.view(DTR, DTR + 2 * NST, 0, TT), "scr_rows",
                    (0, 2 * NST, 0, TT), engine="pool", tag=f"bcst{ti}")

            wdt_sb = wsl.view(0, 64, 0, DI)
            dma("wdt", wdt_sb, w_dt[:, :], tag=f"wdt{ti}")
            for db in range(DB):
                pdt = ps_mm[4].view(0, 128, 0, TT)
                mm(pdt, wdt_sb.view(0, 64, db * 128, (db + 1) * 128),
                   dbc_sb.view(0, DTR, 0, TT), start=True, stop=True,
                   group=f"dt{ti}_{db}", tag=f"mmdt{ti}_{db}")
                bcol = c_bdt.view(0, 128, db, db + 1)
                t_abs = v_q.view(0, 128, 0, TT)
                t_ln = v_q.view(0, 128, TT, 2 * TT)
                act(t_abs, pdt, AF.Abs, bias=bcol, tag=f"sp{ti}_{db}_abs")
                act(t_abs, t_abs, AF.Exp, scale=-1.0, tag=f"sp{ti}_{db}_exp")
                act(t_abs, t_abs, AF.Ln, bias=1.0, tag=f"sp{ti}_{db}_ln")
                act(t_ln, pdt, AF.Relu, bias=bcol, tag=f"sp{ti}_{db}_relu")
                tt(dt_fm.view(0, 128, db * TT, (db + 1) * TT), t_abs, t_ln, ALU.add,
                   tag=f"sp{ti}_{db}_add")

            if cfg["stop_after"] == "dt":
                debug_dump("dt", [dt_fm])
                break

            for q in range(DB // 2):
                db0 = q * 2
                vq_s = v_q.view(0, 128, 0, 2 * TT)
                for j in range(2):
                    db = db0 + j
                    tt(vq_s.view(0, 128, j * TT, (j + 1) * TT),
                       dt_fm.view(0, 128, db * TT, (db + 1) * TT),
                       xcs.view(0, 128, db * (TT + 4) + 4, db * (TT + 4) + 4 + TT),
                       ALU.mult, tag=f"v{ti}_{q}_{j}", eng="pool" if j else "dve")
                for n in range(NST):
                    par = n % 2
                    bn = bc_rep.view(0, 128, par * 2 * TT, par * 2 * TT + TT)
                    cn = bc_rep.view(0, 128, par * 2 * TT + TT, par * 2 * TT + 2 * TT)
                    def ebc(eng, n=n, par=par):
                        srcap = scr_rows[n::NST, 0:TT][0:2, :]
                        srcap = srcap.rearrange("(o a) b -> o a b", o=1)
                        srcap = srcap.broadcast_to((128, 2, TT))
                        dstap = bc_rep.ap()[:, par * 2 * TT:(par + 1) * 2 * TT]
                        dstap = dstap.rearrange("p (a b) -> p a b", a=2)
                        return eng.dma_start(out=dstap, in_=srcap)
                    p.op("sp", ebc, reads=[("scr_rows", n, NST + n + 1, 0, TT)],
                         writes=[(bc_rep.name, 0, 128, par * 2 * TT, (par + 1) * 2 * TT)],
                         dma_ch=f"bcrep{par}", tag=f"ldBC{ti}_{q}_{n}")
                    uq = u_q.view(0, 128, par * 2 * TT, (par + 1) * 2 * TT)
                    hq = h_q.view(0, 128, par * 2 * TT, (par + 1) * 2 * TT)
                    tt(uq.view(0, 128, 0, TT), vq_s.view(0, 128, 0, TT), bn,
                       ALU.mult, tag=f"u0{ti}_{q}_{n}")
                    tt(uq.view(0, 128, TT, 2 * TT), vq_s.view(0, 128, TT, 2 * TT),
                       bn, ALU.mult, eng="pool", tag=f"u1{ti}_{q}_{n}")
                    for j in range(2):
                        db = db0 + j
                        da = dA_t.view(0, 128, par * TT, (par + 1) * TT)
                        act(da, dt_fm.view(0, 128, db * TT, (db + 1) * TT), AF.Exp,
                            scale=c_Aneg.view(0, 128, db * NST + n, db * NST + n + 1),
                            tag=f"dA{ti}_{q}_{n}_{j}")
                        scan_op(hq.view(0, 128, j * TT, (j + 1) * TT), da,
                                uq.view(0, 128, j * TT, (j + 1) * TT),
                                h_state.view(0, 128, db * NST + n, db * NST + n + 1),
                                tag=f"scan{ti}_{q}_{n}_{j}")
                    def esv(eng, q=q, n=n, db0=db0, hq=hq):
                        src2 = hq.ap().rearrange("p (j t) -> p j t", j=2)[:, :, TT - 1:TT]
                        src2 = src2.rearrange("p j t -> p (j t)")
                        dst = h_state.ap().rearrange("p (db n) -> p db n", n=NST)[
                            :, db0:db0 + 2, n:n + 1].rearrange("p j t -> p (j t)")
                        return eng.activation(out=dst, in_=src2, func=AF.Copy)
                    p.op("act", esv, reads=[(h_q.name, 0, 128, hq.c0, hq.c1)],
                         writes=[(h_state.name, 0, 128, db0 * NST + n, (db0 + 1) * NST + n + 1)],
                         tag=f"hs{ti}_{q}_{n}")
                    tt(uq.view(0, 128, 0, TT), hq.view(0, 128, 0, TT), cn,
                       ALU.mult, tag=f"hc0{ti}_{q}_{n}")
                    tt(uq.view(0, 128, TT, 2 * TT), hq.view(0, 128, TT, 2 * TT), cn,
                       ALU.mult, eng="pool", tag=f"hc1{ti}_{q}_{n}")
                    ya0 = y_acc.view(0, 128, db0 * TT, (db0 + 1) * TT)
                    ya1 = y_acc.view(0, 128, (db0 + 1) * TT, (db0 + 2) * TT)
                    if n == 0:
                        copy(ya0, uq.view(0, 128, 0, TT), eng="pool", tag=f"y0a{ti}_{q}")
                        copy(ya1, uq.view(0, 128, TT, 2 * TT), eng="pool", tag=f"y0b{ti}_{q}")
                    else:
                        tt(ya0, ya0, uq.view(0, 128, 0, TT), ALU.add, eng="pool",
                           tag=f"yadda{ti}_{q}_{n}")
                        tt(ya1, ya1, uq.view(0, 128, TT, 2 * TT), ALU.add,
                           eng="pool" if n % 2 else "dve", tag=f"yaddb{ti}_{q}_{n}")
                zbank = 6 + (q % 2)
                for j in range(2):
                    db = db0 + j
                    zw = wslb.view(0, 128, (j % 2) * 2048, (j % 2) * 2048 + 2048)
                    zwh = zw.view(0, 128, 0, 1024)
                    zwl = zw.view(0, 128, 1024, 2048)
                    for half, dst, co in ((0, zwh, 2048 + db * 128),
                                          (1, zwl, 2 * DI + 2048 + db * 128)):
                        def ezw(eng, dst=dst, co=co):
                            srcap = scr_wz[:, co:co + 128]
                            srcap = srcap.rearrange("(k p) c -> p k c", p=128)
                            dstap = dst.ap().rearrange("p (k c) -> p k c", k=MB_D)
                            return eng.dma_start(out=dstap, in_=srcap)
                        p.op("sp", ezw, reads=[("scr_wz", 0, D, co, co + 128)],
                             writes=[dst.reg()], dma_ch=f"wz{j % 2}",
                             tag=f"zw{ti}_{db}_{half}")
                    psz = ps_mm[zbank].view(0, 128, 0, TT)
                    for k in range(MB_D):
                        mm(psz, zwh.view(0, 128, k * 128, (k + 1) * 128), abf_h(k),
                           start=(k == 0), stop=False,
                           group=f"z{ti}_{db}", tag=f"zhh{ti}_{db}_{k}")
                        mm(psz, zwh.view(0, 128, k * 128, (k + 1) * 128), abf_l(k),
                           start=False, stop=False,
                           group=f"z{ti}_{db}", tag=f"zhl{ti}_{db}_{k}")
                        mm(psz, zwl.view(0, 128, k * 128, (k + 1) * 128), abf_h(k),
                           start=False, stop=(k == MB_D - 1),
                           group=f"z{ti}_{db}", tag=f"zlh{ti}_{db}_{k}")
                    act(psz, psz, AF.Silu, tag=f"gz{ti}_{db}")
                    yv = y_acc.view(0, 128, db * TT, (db + 1) * TT)
                    stt(yv, xcs.view(0, 128, db * (TT + 4) + 4,
                                     db * (TT + 4) + 4 + TT),
                        c_dskip.view(0, 128, db, db + 1), yv, ALU.mult, ALU.add,
                        tag=f"gd{ti}_{db}")
                    tt(dt_fm.view(0, 128, db * TT, (db + 1) * TT), yv, psz, ALU.mult,
                       tag=f"gm{ti}_{db}")

            if cfg["stop_after"] == "scan":
                debug_dump("scan", [y_acc])
                break

            for k in range(DB):
                wslot = wsl.view(0, 128, (k % 2) * 1024, (k % 2) * 1024 + D)
                dma(f"wo{k % 2}", wslot, w_out[k * 128:(k + 1) * 128, :],
                    tag=f"wo{ti}_{k}")
                for m in range(MB_D):
                    mm(ps_mm[m].view(0, 128, 0, TT),
                       wslot.view(0, 128, m * 128, (m + 1) * 128),
                       dt_fm.view(0, 128, k * TT, (k + 1) * TT),
                       start=(k == 0), stop=(k == DB - 1), group=f"wo{ti}_{m}",
                       tag=f"mmo{ti}_{m}_{k}")
            for m in range(MB_D):
                stg = u_q.view(0, 128, (m % 4) * TT, (m % 4) * TT + TT)
                copy(stg, ps_mm[m].view(0, 128, 0, TT), eng="act", tag=f"woev{ti}_{m}")
                for c in range(NCH):
                    tbk = m if (m == 0 or c % 2 == 0) else m - 1
                    tp = ps_mm[tbk].view(0, 128, 0, 128)
                    transpose(tp, stg.view(0, 128, c * 128, (c + 1) * 128),
                              tag=f"brt{ti}_{c}_{m}")
                    xd = x_tm.view(0, 128, c * D + m * 128, c * D + (m + 1) * 128)
                    tt(xd, xd, tp, ALU.add, tag=f"bra{ti}_{c}_{m}")

            if cfg["stop_after"] == "wout":
                debug_dump("wout", [x_tm])
                break

            ln_tm_to_fm(x_tm, a_bf, 2 * MB_D, 3 * MB_D, f"ln2_{ti}")

            if cfg["stop_after"] == "ln2":
                debug_dump("ln2", [a_bf], dt=BF16)
                break

            for mg in range(8):
                for k in range(MB_D):
                    wslot = wslb.view(0, 128, (k % 2) * 2048, (k % 2) * 2048 + 2048)
                    for half, off in ((0, 0), (1, 1024)):
                        def eld(eng, k=k, mg=mg, wslot=wslot, half=half, off=off):
                            return eng.dma_start(
                                out=wslot.ap()[:, off:off + 1024].rearrange(
                                    "p (h x) -> p h x", h=2),
                                in_=scr_f1[k * 128:(k + 1) * 128,
                                           half * FH:(half + 1) * FH].rearrange(
                                    "p (h x) -> p h x", h=2)[:, :, mg * 512:(mg + 1) * 512])
                        p.op("sp", eld, reads=[("scr_f1", k * 128, (k + 1) * 128,
                                                half * FH + mg * 512,
                                                half * FH + FH)],
                             writes=[(wslot.name, 0, 128, wslot.c0 + off,
                                      wslot.c0 + off + 1024)], dma_ch=f"wf{k % 2}",
                             tag=f"ff1ld{ti}_{mg}_{k}_{half}")
                    for i in range(4):
                        for bank, woff, gtag in ((i, 0, "a"), (4 + i, 512, "g")):
                            pso = ps_mm[bank].view(0, 128, 0, TT)
                            grp = f"f1{gtag}{ti}_{mg}_{i}"
                            mm(pso, wslot.view(0, 128, woff + i * 128,
                                               woff + (i + 1) * 128),
                               abf_h(k), start=(k == 0), stop=False, group=grp,
                               tag=f"f1{gtag}hh{ti}_{mg}_{i}_{k}")
                            mm(pso, wslot.view(0, 128, woff + i * 128,
                                               woff + (i + 1) * 128),
                               abf_l(k), start=False, stop=False, group=grp,
                               tag=f"f1{gtag}hl{ti}_{mg}_{i}_{k}")
                            mm(pso, wslot.view(0, 128, 1024 + woff + i * 128,
                                               1024 + woff + (i + 1) * 128),
                               abf_h(k), start=False, stop=(k == MB_D - 1), group=grp,
                               tag=f"f1{gtag}lh{ti}_{mg}_{i}_{k}")
                if cfg["stop_after"] == "ff1w" and mg == 1:
                    break
                for i in range(4):
                    ma = mg * 4 + i
                    aev = v_q.view(0, 128, (i % 2) * TT, (i % 2) * TT + TT)
                    gev = u_q.view(0, 128, (i % 2) * TT, (i % 2) * TT + TT)
                    act(aev, ps_mm[i].view(0, 128, 0, TT), AF.Identity,
                        bias=c_bff1.view(0, 128, ma, ma + 1), tag=f"f1ae{ti}_{ma}")
                    act(gev, ps_mm[4 + i].view(0, 128, 0, TT), AF.Gelu_apprx_tanh,
                        bias=c_bff1.view(0, 128, 32 + ma, 32 + ma + 1),
                        tag=f"f1ge{ti}_{ma}")
                    tt(aev, aev, gev, ALU.mult, tag=f"f1m{ti}_{ma}")
                    agh = TV(qbf.h, qbf.name, 0, 128, (i % 2) * TT, (i % 2) * TT + TT)
                    agl = TV(qbf.h, qbf.name, 0, 128, 2 * TT + (i % 2) * TT,
                             2 * TT + (i % 2) * TT + TT)
                    copy(agh, aev, eng="act", tag=f"agh{ti}_{ma}")
                    stt(agl, agh, -1.0, aev, ALU.mult, ALU.add, tag=f"agl{ti}_{ma}")
                    dma_hbm(f"agst{i % 2}", scr_ag[ma * 128:(ma + 1) * 128, 0:TT],
                            agh, "scr_ag", (ma * 128, (ma + 1) * 128, 0, TT),
                            engine="pool", tag=f"agsth{ti}_{ma}")
                    dma_hbm(f"agst{i % 2}", scr_ag[ma * 128:(ma + 1) * 128, TT:2 * TT],
                            agl, "scr_ag", (ma * 128, (ma + 1) * 128, TT, 2 * TT),
                            engine="pool", tag=f"agstl{ti}_{ma}")
            if cfg["stop_after"] == "ff1w":
                break
            if cfg["stop_after"] == "ff1":
                debug_dump_hbm("ff1", "scr_ag", scr_ag[:, :], (FI, 2 * TT), dt=BF16)
                break
            for k in range(FI // 128):
                wslot = wslb.view(0, 128, (k % 2) * 2048, (k % 2) * 2048 + 2 * D)
                dma(f"wf{k % 2}", wslot, scr_f2[k * 128:(k + 1) * 128, :],
                    tag=f"ff2ld{ti}_{k}")
                agk = TV(qbf.h, qbf.name, 0, 128, 4 * TT + (k % 2) * 2 * TT,
                         4 * TT + (k % 2) * 2 * TT + 2 * TT)
                dma_from_hbm(f"agld{k % 2}", agk, scr_ag[k * 128:(k + 1) * 128, :],
                             "scr_ag", (k * 128, (k + 1) * 128, 0, 2 * TT),
                             engine="sp", tag=f"agld{ti}_{k}")
                aghk = agk.view(0, 128, 0, TT)
                aglk = agk.view(0, 128, TT, 2 * TT)
                for m in range(MB_D):
                    pso = ps_mm[m].view(0, 128, 0, TT)
                    grp = f"f2{ti}_{m}"
                    KL = FI // 128 - 1
                    mm(pso, wslot.view(0, 128, m * 128, (m + 1) * 128), aghk,
                       start=(k == 0), stop=False, group=grp, tag=f"f2hh{ti}_{m}_{k}")
                    mm(pso, wslot.view(0, 128, m * 128, (m + 1) * 128), aglk,
                       start=False, stop=False, group=grp, tag=f"f2hl{ti}_{m}_{k}")
                    mm(pso, wslot.view(0, 128, D + m * 128, D + (m + 1) * 128), aghk,
                       start=False, stop=(k == KL), group=grp, tag=f"f2lh{ti}_{m}_{k}")
            for m in range(MB_D):
                stg = u_q.view(0, 128, (m % 4) * TT, (m % 4) * TT + TT)
                act(stg, ps_mm[m].view(0, 128, 0, TT), AF.Identity,
                    bias=c_bff2.view(0, 128, m, m + 1), tag=f"f2ev{ti}_{m}")
                for c in range(NCH):
                    tbk = m if (m == 0 or c % 2 == 0) else m - 1
                    tp = ps_mm[tbk].view(0, 128, 0, 128)
                    transpose(tp, stg.view(0, 128, c * 128, (c + 1) * 128),
                              tag=f"fft{ti}_{c}_{m}")
                    xd = x_tm.view(0, 128, c * D + m * 128, c * D + (m + 1) * 128)
                    tt(xd, xd, tp, ALU.add, tag=f"ffa{ti}_{c}_{m}")

            for c in range(NCH):
                x0c = v_q.view(0, 128, 0, D)
                dma(f"xr{c % 2}", x0c, x_in[tb + c * 128: tb + (c + 1) * 128, :],
                    tag=f"ldxr{ti}_{c}")
                xd = x_tm.view(0, 128, c * D, (c + 1) * D)
                tt(xd, xd, x0c, ALU.add, tag=f"pl1a{ti}_{c}")
                mu = srow.view(0, 128, 0, 1)
                r = srow.view(0, 128, 1, 2)
                tm_ln_stats(xd, c, mu, r, f"pl1_{ti}{c}")
                ts(xd, xd, mu, ALU.subtract, r, ALU.mult, tag=f"pl1n{ti}_{c}")

            if cfg["stop_after"] == "postln1":
                debug_dump("postln1", [x_tm])
                break

            HB = H // 128
            fg1rep = dA_t.view(0, 128, 0, D)
            fb1rep = bc_rep.view(0, 128, 1024, 1024 + D)
            dma("fgrep", fg1rep,
                fg1.rearrange("(o d) -> o d", o=1).broadcast_to((128, D)),
                tag=f"fg1l{ti}")
            dma("fbrep", fb1rep,
                fb1.rearrange("(o d) -> o d", o=1).broadcast_to((128, D)),
                tag=f"fb1l{ti}")
            for c in range(NCH):
                srcx = x_tm.view(0, 128, c * D, (c + 1) * D)
                mu = srow.view(0, 128, 0, 1)
                r = srow.view(0, 128, 1, 2)
                tm_ln_stats(srcx, c, mu, r, f"fl1_{ti}{c}")
                xa = sr_rep.view(0, 128, 0, D)
                ts(xa, srcx, mu, ALU.subtract, r, ALU.mult, tag=f"fl1n{ti}_{c}")
                tt(xa, xa, fg1rep, ALU.mult, tag=f"fl1g{ti}_{c}")
                tt(xa, xa, fb1rep, ALU.add, tag=f"fl1b{ti}_{c}")
                am = srow.view(0, 128, 3, 4)
                reduce(am, xa, ALU.max, absval=True, tag=f"q1m{ti}_{c}")
                ts(am, am, 1e-5, ALU.max, tag=f"q1c{ti}_{c}")
                sc = srow.view(0, 128, 4, 5)
                recip(sc, am, tag=f"q1r{ti}_{c}")
                ts(sc, sc, 127.0, ALU.mult, tag=f"q1s{ti}_{c}")
                iv = srow.view(0, 128, 5, 6)
                ts(iv, am, 1.0 / 127.0, ALU.mult, tag=f"q1i{ti}_{c}")
                dma_hbm("q1iv", scr_rows[32 + c:33 + c, 0:128].rearrange("a b -> b a"),
                        iv, "scr_rows", (32 + c, 33 + c, 0, 128), engine="pool",
                        tag=f"q1ivs{ti}_{c}")
                yq = xa
                ts(yq, xa, sc, ALU.mult, tag=f"q1y{ti}_{c}")
                AA = u_q.view(0, 128, 0, D)
                BB = u_q.view(0, 128, D, 2 * D)
                ts(AA, yq, C1, ALU.add, C1, ALU.subtract, tag=f"q1A{ti}_{c}")
                ts(BB, yq, C2, ALU.add, C2, ALU.subtract, tag=f"q1B{ti}_{c}")
                tt(BB, BB, AA, ALU.subtract, tag=f"q1t{ti}_{c}")
                stt(yq, BB, ind_pp, AA, ALU.mult, ALU.add, tag=f"q1b{ti}_{c}")
                ts(yq, yq, -128.0, ALU.max, 127.0, ALU.min, tag=f"q1cl{ti}_{c}")
                for j in range(MB_D):
                    tp = ps_mm[6 + (j % 2)].view(0, 128, 0, 128)
                    transpose(tp, yq.view(0, 128, j * 128, (j + 1) * 128),
                              tag=f"q1t{ti}_{c}_{j}")
                    copy(TV(qbf.h, qbf.name, 0, 128, j * TT + c * 128,
                            j * TT + (c + 1) * 128), tp, eng="dve",
                         tag=f"q1e{ti}_{c}_{j}")
            if cfg["stop_after"] == "q1":
                debug_dump("q1", [TV(qbf.h, qbf.name, 0, 128, 0, MB_D * TT)], dt=BF16)
                break
            iv_rep_f = dA_t.view(0, 128, 0, TT)
            def eivl(eng, iv_rep_f=iv_rep_f):
                srcap = scr_rows[32:32 + NCH, 0:128]
                srcap = srcap.rearrange("(o a) b -> o a b", o=1)
                srcap = srcap.broadcast_to((128, NCH, 128))
                dstap = iv_rep_f.ap().rearrange("p (a b) -> p a b", a=NCH)
                return eng.dma_start(out=dstap, in_=srcap)
            p.op("sp", eivl, reads=[("scr_rows", 32, 32 + NCH, 0, 128)],
                 writes=[iv_rep_f.reg()], dma_ch="ivrep", tag=f"q1ivl{ti}")
            if ti == 0:
                memset(tiny.view(0, 128, 8, 9), 1.0, eng="pool", tag="ones")
            onescol = tiny.view(0, 128, 8, 9)

            psS = ps_mm[6].view(0, 1, 0, TT)
            psQ = ps_mm[7].view(0, 1, 0, TT)
            for mg in range(HB // 4):
                for k in range(MB_D):
                    wslot = wslb.view(0, 128, (k % 2) * 1024, (k % 2) * 1024 + 512)
                    dma(f"w1q{k % 2}", wslot,
                        scr_w1q[k * 128:(k + 1) * 128, mg * 512:(mg + 1) * 512],
                        tag=f"w1l{ti}_{mg}_{k}")
                    for i in range(4):
                        mm(ps_mm[i].view(0, 128, 0, TT),
                           wslot.view(0, 128, i * 128, (i + 1) * 128),
                           TV(qbf.h, qbf.name, 0, 128, k * TT, (k + 1) * TT),
                           start=(k == 0), stop=(k == MB_D - 1),
                           group=f"m1{ti}_{mg}_{i}", tag=f"m1{ti}_{mg}_{i}_{k}")
                for i in range(4):
                    m = mg * 4 + i
                    g1m = v_q.view(0, 128, (i % 2) * TT, (i % 2) * TT + TT)
                    tt(g1m, ps_mm[i].view(0, 128, 0, TT), iv_rep_f, ALU.mult,
                       tag=f"m1e{ti}_{m}")
                    act(g1m, g1m, AF.Gelu, scale=c_sw.view(0, 128, 0, 1),
                        bias=c_b1.view(0, 128, m, m + 1), tag=f"m1g{ti}_{m}")
                    sq = u_q.view(0, 128, (i % 2) * TT, (i % 2) * TT + TT)
                    act(sq, g1m, AF.Square, tag=f"m1sq{ti}_{m}")
                    mm(psS, onescol, g1m, start=(m == 0), stop=(m == HB - 1),
                       group=f"lnS{ti}", tag=f"lnS{ti}_{m}")
                    mm(psQ, onescol, sq, start=(m == 0), stop=(m == HB - 1),
                       group=f"lnQ{ti}", tag=f"lnQm{ti}_{m}")
                    dma_hbm(f"g1st{i % 2}", scr_g1[m * 128:(m + 1) * 128, :], g1m,
                            "scr_g1", (m * 128, (m + 1) * 128, 0, TT),
                            engine="pool", tag=f"g1st{ti}_{m}")
            if cfg["stop_after"] == "mm1":
                debug_dump_hbm("mm1", "scr_g1", scr_g1[:, :], (H, TT))
                break
            mu1 = v_q.view(0, 1, 0, TT)
            vv = v_q.view(0, 1, TT, 2 * TT)
            m2r = u_q.view(0, 1, 0, TT)
            act(mu1, psS, AF.Identity, scale=1.0 / H, tag=f"lnmu{ti}")
            act(vv, psQ, AF.Identity, scale=1.0 / H, tag=f"lnq{ti}")
            tt(m2r, mu1, mu1, ALU.mult, tag=f"lnm2{ti}")
            tt(vv, vv, m2r, ALU.subtract, tag=f"lnv{ti}")
            act(vv, vv, AF.Sqrt, bias=eps40.view(0, 1, 1, 2), tag=f"lnr{ti}")
            recip(vv, vv, tag=f"lnrr{ti}")
            dma_hbm("lnmu", scr_rows[36:37, 0:TT], mu1, "scr_rows",
                    (36, 37, 0, TT), engine="pool", tag=f"lnmus{ti}")
            dma_hbm("lnr", scr_rows[37:38, 0:TT], vv, "scr_rows",
                    (37, 38, 0, TT), engine="pool", tag=f"lnrs{ti}")
            mu_rep = sr_rep.view(0, 128, 0, TT)
            r_rep = sr_rep.view(0, 128, TT, 2 * TT)
            dma_from_hbm("murep", mu_rep,
                         scr_rows[36:37, 0:TT].broadcast_to((128, TT)),
                         "scr_rows", (36, 37, 0, TT), engine="sp", tag=f"lnmul{ti}")
            dma_from_hbm("rrep", r_rep,
                         scr_rows[37:38, 0:TT].broadcast_to((128, TT)),
                         "scr_rows", (37, 38, 0, TT), engine="sp", tag=f"lnrl{ti}")

            mx = dA_t
            mx = u_q.view(0, 128, TT, 2 * TT)
            for grp in range(HB // 4):
                gchunk = y_acc.view(0, 128, (grp % 2) * 4 * TT, ((grp % 2) * 4 + 4) * TT)
                def eg1ldA(eng, grp=grp, gchunk=gchunk):
                    srcap = scr_g1[grp * 512:(grp + 1) * 512, :].rearrange(
                        "(j p) t -> p j t", p=128)
                    dstap = gchunk.ap().rearrange("p (j t) -> p j t", j=4)
                    return eng.dma_start(out=dstap, in_=srcap)
                p.op("sp", eg1ldA, reads=[("scr_g1", grp * 512, (grp + 1) * 512, 0, TT)],
                     writes=[gchunk.reg()], dma_ch=f"g1ldA{grp % 2}",
                     tag=f"g1ldA{ti}_{grp}")
                tt_bcast(gchunk, gchunk, mu_rep, ALU.subtract, 4, tag=f"pAn{ti}_{grp}")
                tt_bcast(gchunk, gchunk, r_rep, ALU.mult, 4, tag=f"pAr{ti}_{grp}")
                for i in range(4):
                    j = grp * 4 + i
                    gj = gchunk.view(0, 128, i * TT, (i + 1) * TT)
                    ts(gj, gj, c_fln2.view(0, 128, j, j + 1), ALU.mult,
                       c_fln2.view(0, 128, HB + j, HB + j + 1), ALU.add,
                       tag=f"pAa{ti}_{grp}_{i}")
                gm = u_q.view(0, 128, 0, TT)
                def ered(eng, gchunk=gchunk, gm=gm):
                    src3 = gchunk.ap().rearrange("p (j t) -> p t j", j=4)
                    return eng.tensor_reduce(out=gm.ap(), in_=src3, axis=AX.X,
                                             op=ALU.max, apply_absolute_value=True)
                p.op("dve", ered, reads=[gchunk.reg()], writes=[gm.reg()],
                     tag=f"pAm{ti}_{grp}")
                if grp == 0:
                    copy(mx, gm, eng="dve", tag=f"pAc{ti}_{grp}")
                else:
                    tt(mx, mx, gm, ALU.max, tag=f"pAx{ti}_{grp}")
            for c in range(NCH):
                tp = ps_mm[6].view(0, 128, 0, 128)
                transpose(tp, mx.view(0, 128, c * 128, (c + 1) * 128),
                          tag=f"q2tp{ti}_{c}")
                mcol = tiny.view(0, 128, 9, 10)
                reduce(mcol, tp.view(0, 128, 0, 128), ALU.max, tag=f"q2rd{ti}_{c}")
                ts(mcol, mcol, 1e-5, ALU.max, tag=f"q2cl{ti}_{c}")
                iv2 = tiny.view(0, 128, 10, 11)
                ts(iv2, mcol, 1.0 / 127.0, ALU.mult, tag=f"q2iv{ti}_{c}")
                sc2 = tiny.view(0, 128, 11, 12)
                recip(sc2, mcol, tag=f"q2rc{ti}_{c}")
                ts(sc2, sc2, 127.0, ALU.mult, tag=f"q2sc{ti}_{c}")
                cb = c * 128
                dma_hbm("q2s", scr_rows[38:39, cb:cb + 128].rearrange("a b -> b a"),
                        sc2, "scr_rows", (38, 39, cb, cb + 128),
                        engine="pool", tag=f"q2ss{ti}_{c}")
                dma_hbm("q2i", scr_rows[39:40, cb:cb + 128].rearrange("a b -> b a"),
                        iv2, "scr_rows", (39, 40, cb, cb + 128),
                        engine="pool", tag=f"q2is{ti}_{c}")
            s_rep = bc_rep.view(0, 128, 0, TT)
            i_rep2 = bc_rep.view(0, 128, TT, 2 * TT)
            dma_from_hbm("q2sr", s_rep,
                         scr_rows[38:39, 0:TT].broadcast_to((128, TT)),
                         "scr_rows", (38, 39, 0, TT), engine="sp", tag=f"q2srl{ti}")
            dma_from_hbm("q2ir", i_rep2,
                         scr_rows[39:40, 0:TT].broadcast_to((128, TT)),
                         "scr_rows", (39, 40, 0, TT), engine="sp", tag=f"q2irl{ti}")

            if cfg["stop_after"] == "q2m":
                debug_dump("q2m", [sr_rep.view(0, 128, 0, 2 * TT),
                                   bc_rep.view(0, 128, 0, 2 * TT)])
                break
            for grp in range(HB // 4):
                gchunk = y_acc.view(0, 128, (grp % 2) * 4 * TT, ((grp % 2) * 4 + 4) * TT)
                def eg1ldB(eng, grp=grp, gchunk=gchunk):
                    srcap = scr_g1[grp * 512:(grp + 1) * 512, :].rearrange(
                        "(j p) t -> p j t", p=128)
                    dstap = gchunk.ap().rearrange("p (j t) -> p j t", j=4)
                    return eng.dma_start(out=dstap, in_=srcap)
                p.op("sp", eg1ldB, reads=[("scr_g1", grp * 512, (grp + 1) * 512, 0, TT)],
                     writes=[gchunk.reg()], dma_ch=f"g1ldB{grp % 2}",
                     tag=f"g1ldB{ti}_{grp}")
                tt_bcast(gchunk, gchunk, mu_rep, ALU.subtract, 4, tag=f"pBn{ti}_{grp}")
                tt_bcast(gchunk, gchunk, r_rep, ALU.mult, 4, tag=f"pBr{ti}_{grp}")
                for i in range(4):
                    j = grp * 4 + i
                    gj = gchunk.view(0, 128, i * TT, (i + 1) * TT)
                    ts(gj, gj, c_fln2.view(0, 128, j, j + 1), ALU.mult,
                       c_fln2.view(0, 128, HB + j, HB + j + 1), ALU.add,
                       tag=f"pBa{ti}_{grp}_{i}")
                tt_bcast(gchunk, gchunk, s_rep, ALU.mult, 4, tag=f"q2y{ti}_{grp}")
                AA = u_q.view(0, 128, 0, 2 * TT)
                BB = u_q.view(0, 128, 2 * TT, 4 * TT)
                for hf in range(2):
                    gh = gchunk.view(0, 128, hf * 2 * TT, (hf + 1) * 2 * TT)
                    ts(AA, gh, C1, ALU.add, C1, ALU.subtract, tag=f"q2A{ti}_{grp}_{hf}")
                    ts(BB, gh, C2, ALU.add, C2, ALU.subtract, tag=f"q2B{ti}_{grp}_{hf}")
                    tt(BB, BB, AA, ALU.subtract, tag=f"q2d{ti}_{grp}_{hf}")
                    stt(gh, BB, ind_pp, AA, ALU.mult, ALU.add, tag=f"q2bl{ti}_{grp}_{hf}")
                    ts(gh, gh, -128.0, ALU.max, 127.0, ALU.min, tag=f"q2c{ti}_{grp}_{hf}")
                q2c = TV(qbf.h, qbf.name, 0, 128, MB_D * TT + (grp % 2) * 4 * TT,
                         MB_D * TT + (grp % 2) * 4 * TT + 4 * TT)
                copy(q2c, gchunk, eng="dve", tag=f"q2cast{ti}_{grp}")
                if grp == 0:
                    debug_dump("q2grp0", [gchunk])
                for kk in range(4):
                    k = grp * 4 + kk
                    wslot = wslb.view(0, 128, (k % 2) * 1024, (k % 2) * 1024 + D)
                    dma(f"w2q{k % 2}", wslot, scr_w2q[k * 128:(k + 1) * 128, :],
                        tag=f"w2l{ti}_{k}")
                    for m in range(MB_D):
                        mm(ps_mm[m].view(0, 128, 0, TT),
                           wslot.view(0, 128, m * 128, (m + 1) * 128),
                           q2c.view(0, 128, kk * TT, (kk + 1) * TT),
                           start=(k == 0), stop=(k == HB - 1),
                           group=f"m2{ti}_{m}", tag=f"m2{ti}_{m}_{k}")
            if cfg["stop_after"] == "mm2":
                pass

            for m in range(MB_D):
                ev = v_q.view(0, 128, 0, TT)
                tt(ev, ps_mm[m].view(0, 128, 0, TT), i_rep2, ALU.mult,
                   tag=f"m2e{ti}_{m}")
                stg = u_q.view(0, 128, (m % 2) * TT, (m % 2) * TT + TT)
                act(stg, ev, AF.Identity,
                    scale=c_sw.view(0, 128, 1, 2), bias=c_b2.view(0, 128, m, m + 1),
                    tag=f"m2b{ti}_{m}")
                for c in range(NCH):
                    tbk = m if (m == 0 or c % 2 == 0) else m - 1
                    tp = ps_mm[tbk].view(0, 128, 0, 128)
                    transpose(tp, stg.view(0, 128, c * 128, (c + 1) * 128),
                              tag=f"m2t{ti}_{c}_{m}")
                    xd = x_tm.view(0, 128, c * D + m * 128, c * D + (m + 1) * 128)
                    tt(xd, xd, tp, ALU.add, tag=f"m2a{ti}_{c}_{m}")
            for c in range(NCH):
                xd = x_tm.view(0, 128, c * D, (c + 1) * D)
                mu = srow.view(0, 128, 0, 1)
                r = srow.view(0, 128, 1, 2)
                tm_ln_stats(xd, c, mu, r, f"pl2_{ti}{c}")
                ts(xd, xd, mu, ALU.subtract, r, ALU.mult, tag=f"pl2n{ti}_{c}")
                dma_hbm(f"out{ti % 2}", out_hbm[tb + c * 128: tb + (c + 1) * 128, :],
                        xd, "out", (tb + c * 128, tb + (c + 1) * 128, 0, D),
                        engine="sp", tag=f"st{ti}_{c}")

        p.emit(nc, es)
    return nc, dbg_out


def kernel(**inputs):
    cfg = default_cfg()
    nc, _ = build_program(cfg)

    x_full = np.ascontiguousarray(np.asarray(inputs["L0_attn_x"], np.float32))
    B = x_full.shape[0]
    base = {
        "w_in": inputs["L0_attn_w_in"], "conv_w": inputs["L0_attn_conv_w"],
        "conv_b": inputs["L0_attn_conv_b"], "w_x": inputs["L0_attn_w_x"],
        "w_dt": inputs["L0_attn_w_dt"], "b_dt": inputs["L0_attn_b_dt"],
        "a_log": inputs["L0_attn_a_log"], "d_skip": inputs["L0_attn_d_skip"],
        "w_out": inputs["L0_attn_w_out"], "ln1_g": inputs["L0_attn_ln1_g"],
        "ln1_b": inputs["L0_attn_ln1_b"], "ln2_g": inputs["L0_attn_ln2_g"],
        "ln2_b": inputs["L0_attn_ln2_b"], "w_ff1": inputs["L0_attn_w_ff1"],
        "b_ff1": inputs["L0_attn_b_ff1"], "w_ff2": inputs["L0_attn_w_ff2"],
        "b_ff2": inputs["L0_attn_b_ff2"], "f_ln1_g": inputs["L0_ffn_ln1_g"],
        "f_ln1_b": inputs["L0_ffn_ln1_b"], "w1": inputs["L0_ffn_w1"],
        "b1": inputs["L0_ffn_b1"], "f_ln2_g": inputs["L0_ffn_ln2_g"],
        "f_ln2_b": inputs["L0_ffn_ln2_b"], "w2": inputs["L0_ffn_w2"],
        "b2": inputs["L0_ffn_b2"],
        "ident_hbm": np.eye(128, dtype=np.float32),
        "eps40_hbm": np.concatenate([np.full((128, 1), 2.0**-40, np.float32), np.full((128, 1), 1e-5, np.float32)], axis=1),
    }
    base = {k: np.ascontiguousarray(np.asarray(v, np.float32)) for k, v in base.items()}
    in_maps = []
    for b in range(B):
        m = dict(base)
        m["x"] = np.ascontiguousarray(x_full[b])
        in_maps.append(m)
    res = run_bass_kernel_spmd(nc, in_maps, core_ids=list(range(B)))
    out = np.stack([np.asarray(r["out"], np.float32) for r in res.results], axis=0)
    return out


# version 07
# speedup vs baseline: 1.0574x; 1.0574x over previous
import numpy as np
from contextlib import ExitStack

import concourse.bass as bass
import concourse.mybir as mybir
from concourse.bass_utils import run_bass_kernel_spmd

F32 = mybir.dt.float32
BF16 = mybir.dt.bfloat16
ALU = mybir.AluOpType
AF = mybir.ActivationFunctionType
AX = mybir.AxisListType

C1 = 1.5 * 2.0**23
C2 = 1.5 * 2.0**52

ENGINES = ("sp", "pe", "act", "dve", "pool")


class Op:
    __slots__ = ("idx", "engine", "emit", "deps", "need_inc", "inc_val", "dma_ch",
                 "dma_pos", "group", "tag")

    def __init__(self, idx, engine, emit, dma_ch, group, tag):
        self.idx = idx
        self.engine = engine
        self.emit = emit
        self.deps = []
        self.need_inc = False
        self.inc_val = None
        self.dma_ch = dma_ch
        self.dma_pos = None
        self.group = group
        self.tag = tag


class Prog:
    def __init__(self):
        self.ops = []
        self.access = {}
        self.chan_counts = {}

    def op(self, engine, emit, reads=(), writes=(), dma_ch=None, group=None, tag=""):
        idx = len(self.ops)
        o = Op(idx, engine, emit, dma_ch, group, tag)
        if dma_ch is not None:
            self.chan_counts[dma_ch] = self.chan_counts.get(dma_ch, 0) + 1
            o.dma_pos = self.chan_counts[dma_ch]
        deps = set()
        for r in reads:
            self._scan(r, False, idx, deps, group)
        for w in writes:
            self._scan(w, True, idx, deps, group)
        o.deps = sorted(deps)
        self.ops.append(o)
        for r in reads:
            self._record(r, idx, False)
        for w in writes:
            self._record(w, idx, True)
        return idx

    @staticmethod
    def _overlap(a, b):
        return a[0] < b[1] and b[0] < a[1] and a[2] < b[3] and b[2] < a[3]

    def _scan(self, region, is_write, idx, deps, group):
        name, p0, p1, c0, c1 = region
        lst = self.access.get(name)
        if not lst:
            return
        me = (p0, p1, c0, c1)
        for rec in reversed(lst):
            oidx, w, q0, q1, d0, d1 = rec
            if not (w or is_write):
                continue
            if self._overlap(me, (q0, q1, d0, d1)):
                o = self.ops[oidx]
                if group is not None and o.group == group:
                    continue
                deps.add(oidx)

    def _record(self, region, idx, is_write):
        name, p0, p1, c0, c1 = region
        lst = self.access.setdefault(name, [])
        if is_write and p0 == 0 and c0 == 0:
            keep = []
            for rec in lst:
                if not (rec[2] >= p0 and rec[3] <= p1 and rec[4] >= c0 and rec[5] <= c1):
                    keep.append(rec)
            if len(keep) != len(lst):
                self.access[name] = lst = keep
        lst.append([idx, is_write, p0, p1, c0, c1])
        if len(lst) > 4000:
            del lst[: len(lst) - 3000]

    def emit(self, nc, es):
        ops = self.ops
        for o in ops:
            for d in o.deps:
                p = ops[d]
                if p.dma_ch is None:
                    p.need_inc = True
        counters = {e: 0 for e in ENGINES}
        for o in ops:
            if o.dma_ch is None and o.need_inc:
                counters[o.engine] += 1
                o.inc_val = counters[o.engine]

        eng_sems = {e: es.enter_context(nc.semaphore(f"es_{e}")) for e in ENGINES}
        ch_sems = {c: es.enter_context(nc.semaphore(f"ch_{c}")) for c in self.chan_counts}
        import bisect as _bisect
        chan_idxs = {}
        for o in ops:
            if o.dma_ch is not None:
                chan_idxs.setdefault(o.dma_ch, []).append(o.idx)

        per_engine = {e: [o for o in ops if o.engine == e] for e in ENGINES}
        emitted_ok = []

        def make_body(ename):
            def body(eng):
                watermark = {}
                for o in per_engine[ename]:
                    needs = {}
                    for d in o.deps:
                        p = ops[d]
                        if p.dma_ch is not None:
                            key = ("ch", p.dma_ch)
                            if p.dma_ch.startswith("const"):
                                val = 16 * self.chan_counts[p.dma_ch]
                            else:
                                val = 16 * _bisect.bisect_left(chan_idxs[p.dma_ch], o.idx)
                            if needs.get(key, 0) < val:
                                needs[key] = val
                        else:
                            key = ("e", p.engine)
                            if needs.get(key, 0) < p.inc_val:
                                needs[key] = p.inc_val
                    for key, val in needs.items():
                        if watermark.get(key, 0) < val:
                            sem = ch_sems[key[1]] if key[0] == "ch" else eng_sems[key[1]]
                            eng.wait_ge(sem, val)
                            watermark[key] = val
                            self.emit_stats[(ename, "wait_" + key[0])] += 1
                    try:
                        inst = o.emit(eng)
                    except Exception as ex:
                        raise RuntimeError(f"emit failed at op {o.idx} tag={o.tag!r}") from ex
                    if o.dma_ch is not None:
                        inst.then_inc(ch_sems[o.dma_ch], 16)
                    elif o.need_inc:
                        inst.then_inc(eng_sems[o.engine], 1)
                    emitted_ok.append(o.idx)
                done = set()
                for o in per_engine[ename]:
                    if o.dma_ch is not None and o.dma_ch not in done:
                        done.add(o.dma_ch)
                for c in done:
                    eng.wait_ge(ch_sems[c], 16 * self.chan_counts[c])
            return body

        import collections as _c
        self.emit_stats = _c.Counter()
        block = es.enter_context(nc.Block())
        block.sync(make_body("sp"))
        block.tensor(make_body("pe"))
        block.scalar(make_body("act"))
        block.vector(make_body("dve"))
        block.gpsimd(make_body("pool"))
        assert len(emitted_ok) == len(ops)


class TV:

    __slots__ = ("h", "name", "p0", "p1", "c0", "c1")

    def __init__(self, h, name, p0, p1, c0, c1):
        self.h, self.name, self.p0, self.p1, self.c0, self.c1 = h, name, p0, p1, c0, c1

    def ap(self):
        return self.h.ap()[self.p0:self.p1, self.c0:self.c1]

    def reg(self):
        return (self.name, self.p0, self.p1, self.c0, self.c1)

    def view(self, p0, p1, c0, c1):
        return TV(self.h, self.name, self.p0 + p0, self.p0 + p1,
                  self.c0 + c0, self.c0 + c1)


def _tv(h, name, P, C):
    return TV(h, name, 0, P, 0, C)


def default_cfg():
    return dict(
        T=4096,
        TT=512,
        D=1024,
        DI=2048,
        NST=16,
        DTR=64,
        FH=8192,
        FI=4096,
        H=4096,
        EPS=1e-5,
        stop_after=None,
        debug=(),
    )


def build_program(cfg):
    T, TT, D, DI = cfg["T"], cfg["TT"], cfg["D"], cfg["DI"]
    NST, DTR, FH, FI, H = cfg["NST"], cfg["DTR"], cfg["FH"], cfg["FI"], cfg["H"]
    EPS = cfg["EPS"]
    NTILE = T // TT
    NCH = TT // 128
    DB = DI // 128
    MB_D = D // 128
    QD = 4
    NQ = DB // QD

    nc = bass.Bass("TRN2", target_bir_lowering=False)
    p = Prog()

    def din(name, shape):
        return nc.dram_tensor(name, list(shape), F32, kind="ExternalInput").ap()

    x_in = din("x", (T, D))
    w_in = din("w_in", (D, 2 * DI))
    conv_w = din("conv_w", (DI, 4))
    conv_b = din("conv_b", (DI,))
    w_x = din("w_x", (DI, DTR + 2 * NST))
    w_dt = din("w_dt", (DTR, DI))
    b_dt = din("b_dt", (DI,))
    a_log = din("a_log", (DI, NST))
    d_skip = din("d_skip", (DI,))
    w_out = din("w_out", (DI, D))
    ln1_g, ln1_b = din("ln1_g", (D,)), din("ln1_b", (D,))
    ln2_g, ln2_b = din("ln2_g", (D,)), din("ln2_b", (D,))
    w_ff1 = din("w_ff1", (D, FH))
    b_ff1 = din("b_ff1", (FH,))
    w_ff2 = din("w_ff2", (FI, D))
    b_ff2 = din("b_ff2", (D,))
    fg1, fb1 = din("f_ln1_g", (D,)), din("f_ln1_b", (D,))
    fg2, fb2 = din("f_ln2_g", (H,)), din("f_ln2_b", (H,))
    w1 = din("w1", (H, D))
    b1 = din("b1", (H,))
    w2 = din("w2", (D, H))
    b2 = din("b2", (D,))
    ident_in = din("ident_hbm", (128, 128))
    eps40_in = din("eps40_hbm", (128, 2))

    out_hbm = nc.dram_tensor("out", [T, D], F32, kind="ExternalOutput").ap()

    scr_w1q = nc.dram_tensor("scr_w1q", [D, H], BF16, kind="Internal").ap()
    scr_w2q = nc.dram_tensor("scr_w2q", [H, D], BF16, kind="Internal").ap()
    scr_rows = nc.dram_tensor("scr_rows", [64, TT], F32, kind="Internal").ap()
    scr_ag = nc.dram_tensor("scr_ag", [FI, 2 * TT], BF16, kind="Internal").ap()
    scr_wz = nc.dram_tensor("scr_wz", [D, 2 * 2 * DI], BF16, kind="Internal").ap()
    scr_f1 = nc.dram_tensor("scr_f1", [D, 2 * FH], BF16, kind="Internal").ap()
    scr_f2 = nc.dram_tensor("scr_f2", [FI, 2 * D], BF16, kind="Internal").ap()
    scr_g1 = nc.dram_tensor("scr_g1", [H, TT], F32, kind="Internal").ap()
    scr_sw = nc.dram_tensor("scr_sw", [2], F32, kind="Internal").ap()

    dbg_out = {}
    for name in cfg["debug"]:
        pass

    es = ExitStack()

    def sb(name, P, C, dt=F32):
        h = es.enter_context(nc.sbuf_tensor(name, [P, C], dt))
        return _tv(h, name, P, C)

    def psum(name, P, C):
        h = es.enter_context(nc.psum_tensor(name, [P, C], F32))
        return _tv(h, name, P, C)

    with es:
        ident = sb("ident", 128, 128)
        eps40 = sb("eps40", 128, 2)
        ind_pp = sb("ind_pp", 128, 1)
        c_convw = sb("c_convw", 128, DB * 4)
        c_convb = sb("c_convb", 128, DB)
        c_bdt = sb("c_bdt", 128, DB)
        c_dskip = sb("c_dskip", 128, DB)
        c_Aneg = sb("c_Aneg", 128, DB * NST)
        c_ln = sb("c_ln", 128, 4 * (D // 128))
        c_fln2 = sb("c_fln2", 128, 2 * (H // 128))
        c_bff1 = sb("c_bff1", 128, FH // 128)
        c_bff2 = sb("c_bff2", 128, D // 128)
        c_b1 = sb("c_b1", 128, H // 128)
        c_b2 = sb("c_b2", 128, D // 128)
        c_sw = sb("c_sw", 128, 2)
        h_state = sb("h_state", 128, DB * NST)

        x_tm = sb("x_tm", 128, NCH * D)
        a_bf = sb("a_bf", 128, 2 * MB_D * TT, BF16)
        xcs = sb("xcs", 128, DB * (TT + 4))
        dt_fm = sb("dt_fm", 128, max(DB * TT, H))
        y_acc = sb("y_acc", 128, DB * TT)
        v_q = sb("v_q", 128, max(2 * TT, D))
        h_q = sb("h_q", 128, QD * TT)
        u_q = sb("u_q", 128, QD * TT)
        dA_t = sb("dA_t", 128, 2 * TT)
        bc_rep = sb("bc_rep", 128, 4 * TT)
        wsl = sb("wsl", 128, 2 * 1024)
        wslb = sb("wslb", 128, 2 * 2048, BF16)
        qbf = sb("qbf", 128, max(FH, 2 * D), BF16)
        srow = sb("srow", 128, 8)
        sr_rep = sb("sr_rep", 128, max(2 * TT, D))
        tiny = sb("tiny", 128, 16)

        ps_mm = [psum(f"ps{i}", 128, 512) for i in range(8)]

        def dma(ch, dst, src, engine="sp", tag="", slow=False):
            def e(eng, dst=dst, src=src, slow=slow):
                return eng.dma_start(out=dst.ap() if isinstance(dst, TV) else dst,
                                     in_=src.ap() if isinstance(src, TV) else src,
                                     allow_slow_non_contiguous=slow)
            reads = [src.reg()] if isinstance(src, TV) else []
            writes = [dst.reg()] if isinstance(dst, TV) else []
            p.op(engine, e, reads=reads, writes=writes, dma_ch=ch, tag=tag)

        def dma_hbm(ch, dst_ap, src, hbm_name, hr, engine="sp", tag=""):
            def e(eng, dst_ap=dst_ap, src=src):
                return eng.dma_start(out=dst_ap, in_=src.ap())
            p.op(engine, e, reads=[src.reg()],
                 writes=[(hbm_name,) + hr], dma_ch=ch, tag=tag)

        def dma_from_hbm(ch, dst, src_ap, hbm_name, hr, engine="sp", tag=""):
            def e(eng, dst=dst, src_ap=src_ap):
                return eng.dma_start(out=dst.ap(), in_=src_ap)
            p.op(engine, e, reads=[(hbm_name,) + hr], writes=[dst.reg()],
                 dma_ch=ch, tag=tag)

        def act(out, in_, func, scale=None, bias=None, extra_reads=(), eng="act", tag=""):
            def e(engine, out=out, in_=in_, func=func, scale=scale, bias=bias):
                kw = {}
                if scale is not None:
                    kw["scale"] = scale.ap() if isinstance(scale, TV) else scale
                if bias is not None:
                    kw["bias"] = bias.ap() if isinstance(bias, TV) else bias
                return engine.activation(out=out.ap(), in_=in_.ap(), func=func, **kw)
            reads = [in_.reg()]
            if isinstance(scale, TV):
                reads.append(scale.reg())
            if isinstance(bias, TV):
                reads.append(bias.reg())
            reads.extend(r.reg() for r in extra_reads)
            p.op(eng, e, reads=reads, writes=[out.reg()], tag=tag)

        def tt(out, in0, in1, op, eng="dve", tag=""):
            def e(engine, out=out, in0=in0, in1=in1, op=op):
                return engine.tensor_tensor(out=out.ap(), in0=in0.ap(), in1=in1.ap(), op=op)
            p.op(eng, e, reads=[in0.reg(), in1.reg()], writes=[out.reg()], tag=tag)

        def tt_bcast(out, in0, in1, op, reps, eng="dve", tag=""):
            def e(engine, out=out, in0=in0, in1=in1, op=op, reps=reps):
                w = in1.c1 - in1.c0
                ap1 = in1.ap().rearrange("p (r w) -> p r w", r=1).broadcast_to(
                    (128, reps, w))
                ap0 = in0.ap().rearrange("p (r w) -> p r w", r=reps)
                apo = out.ap().rearrange("p (r w) -> p r w", r=reps)
                return engine.tensor_tensor(out=apo, in0=ap0, in1=ap1, op=op)
            p.op(eng, e, reads=[in0.reg(), in1.reg()], writes=[out.reg()], tag=tag)

        def ts(out, in_, s1, op1, s2=None, op2=None, eng="dve", tag=""):
            def e(engine, out=out, in_=in_, s1=s1, op1=op1, s2=s2, op2=op2):
                a1 = s1.ap() if isinstance(s1, TV) else s1
                if s2 is None:
                    return engine.tensor_single_scalar(out=out.ap(), in_=in_.ap(),
                                                       scalar=a1, op=op1)
                a2 = s2.ap() if isinstance(s2, TV) else s2
                return engine.tensor_scalar(out=out.ap(), in0=in_.ap(), scalar1=a1,
                                            scalar2=a2, op0=op1, op1=op2)
            reads = [in_.reg()]
            for s in (s1, s2):
                if isinstance(s, TV):
                    reads.append(s.reg())
            p.op(eng, e, reads=reads, writes=[out.reg()], tag=tag)

        def stt(out, in0, scalar, in1, op0, op1, eng="dve", tag=""):
            def e(engine, out=out, in0=in0, scalar=scalar, in1=in1, op0=op0, op1=op1):
                s = scalar.ap() if isinstance(scalar, TV) else scalar
                return engine.scalar_tensor_tensor(out=out.ap(), in0=in0.ap(), scalar=s,
                                                   in1=in1.ap(), op0=op0, op1=op1)
            reads = [in0.reg(), in1.reg()]
            if isinstance(scalar, TV):
                reads.append(scalar.reg())
            p.op(eng, e, reads=reads, writes=[out.reg()], tag=tag)

        def scan_op(out, data0, data1, initial, tag=""):
            def e(engine, out=out, data0=data0, data1=data1, initial=initial):
                ini = initial.ap() if isinstance(initial, TV) else initial
                return engine.tensor_tensor_scan(out=out.ap(), data0=data0.ap(),
                                                 data1=data1.ap(), initial=ini,
                                                 op0=ALU.mult, op1=ALU.add)
            reads = [data0.reg(), data1.reg()]
            if isinstance(initial, TV):
                reads.append(initial.reg())
            p.op("dve", e, reads=reads, writes=[out.reg()], tag=tag)

        def reduce(out, in_, op, absval=False, eng="dve", tag=""):
            def e(engine, out=out, in_=in_, op=op, absval=absval):
                return engine.tensor_reduce(out=out.ap(), in_=in_.ap(), axis=AX.X,
                                            op=op, apply_absolute_value=absval)
            p.op(eng, e, reads=[in_.reg()], writes=[out.reg()], tag=tag)

        def recip(out, in_, tag=""):
            def e(engine, out=out, in_=in_):
                return engine.reciprocal(out=out.ap(), in_=in_.ap())
            p.op("dve", e, reads=[in_.reg()], writes=[out.reg()], tag=tag)

        def copy(out, in_, eng="act", tag=""):
            if eng == "act":
                act(out, in_, AF.Copy, tag=tag)
            else:
                def e(engine, out=out, in_=in_):
                    return engine.tensor_copy(out=out.ap(), in_=in_.ap())
                p.op(eng, e, reads=[in_.reg()], writes=[out.reg()], tag=tag)

        def mm(out, lhsT, rhs, start, stop, group, tag=""):
            def e(engine, out=out, lhsT=lhsT, rhs=rhs, start=start, stop=stop):
                return engine.matmul(out=out.ap(), lhsT=lhsT.ap(), rhs=rhs.ap(),
                                     start=start, stop=stop)
            p.op("pe", e, reads=[lhsT.reg(), rhs.reg()], writes=[out.reg()],
                 group=group, tag=tag)

        def transpose(out_ps, in_, tag=""):
            def e(engine, out_ps=out_ps, in_=in_):
                return engine.transpose(out=out_ps.ap(), in_=in_.ap(),
                                        identity=ident.ap())
            p.op("pe", e, reads=[in_.reg(), ident.reg()], writes=[out_ps.reg()],
                 tag=tag)

        def memset(buf, val, eng="dve", tag=""):
            def e(engine, buf=buf, val=val):
                return engine.memset(buf.ap(), val)
            p.op(eng, e, reads=[], writes=[buf.reg()], tag=tag)

        dbg_seq = [0]

        def debug_dump(name, tv_list, dt=F32):
            if name not in cfg["debug"]:
                return
            P = 128
            Ctot = sum(t.c1 - t.c0 for t in tv_list)
            d = nc.dram_tensor(f"dbg_{name}", [P, Ctot], dt, kind="ExternalOutput").ap()
            c = 0
            for t in tv_list:
                w = t.c1 - t.c0
                def e(eng, t=t, d=d, c=c, w=w):
                    return eng.dma_start(out=d[:, c:c + w], in_=t.ap())
                p.op("pool", e, reads=[t.reg()], writes=[], dma_ch="dbg", tag=f"dbg_{name}")
                c += w
            dbg_out[name] = d

        def debug_dump_hbm(name, src_name, src_hbm_ap, shape, dt=F32):
            if name not in cfg["debug"]:
                return
            d = nc.dram_tensor(f"dbg_{name}", list(shape), dt, kind="ExternalOutput").ap()
            def e(eng, d=d, src_hbm_ap=src_hbm_ap):
                return eng.dma_start(out=d, in_=src_hbm_ap)
            p.op("pool", e, reads=[(src_name, 0, shape[0], 0, shape[1])], writes=[],
                 dma_ch="dbg", tag=f"dbg_{name}")

        dma("const", ident, ident_in)
        dma("const", eps40, eps40_in)
        ts(ind_pp, eps40.view(0, 128, 0, 1), 1.0, ALU.add, 1.0, ALU.subtract, tag="ind1")
        ts(ind_pp, ind_pp, 2.0**40, ALU.mult, tag="ind2")

        def dma3(ch, dst, src3, a, b, tag=""):
            def e(eng, dst=dst, src3=src3, a=a, b=b):
                return eng.dma_start(
                    out=dst.ap().rearrange("p (a b) -> p a b", a=a),
                    in_=src3, allow_slow_non_contiguous=True)
            p.op("sp", e, reads=[], writes=[dst.reg()], dma_ch=ch, tag=tag)

        dma3("const", c_convw, conv_w.rearrange("(db p) k -> p db k", p=128),
             DB, 4)
        dma("const", c_convb, conv_b.rearrange("(db p) -> p db", p=128), slow=True)
        dma("const", c_bdt, b_dt.rearrange("(db p) -> p db", p=128), slow=True)
        dma("const", c_dskip, d_skip.rearrange("(db p) -> p db", p=128), slow=True)
        dma3("const", c_Aneg, a_log.rearrange("(db p) n -> p db n", p=128),
             DB, NST)
        act(c_Aneg, c_Aneg, AF.Exp, tag="Aexp")
        ts(c_Aneg, c_Aneg, -1.0, ALU.mult, tag="Aneg")
        for i, v in enumerate((ln1_g, ln1_b, ln2_g, ln2_b)):
            dma("const", c_ln.view(0, 128, i * MB_D, (i + 1) * MB_D),
                v.rearrange("(m p) -> p m", p=128), slow=True)
        dma("const", c_fln2.view(0, 128, 0, H // 128), fg2.rearrange("(m p) -> p m", p=128), slow=True)
        dma("const", c_fln2.view(0, 128, H // 128, 2 * H // 128),
            fb2.rearrange("(m p) -> p m", p=128), slow=True)
        dma("const", c_bff1, b_ff1.rearrange("(m p) -> p m", p=128), slow=True)
        dma("const", c_bff2, b_ff2.rearrange("(m p) -> p m", p=128), slow=True)
        dma("const", c_b1, b1.rearrange("(m p) -> p m", p=128), slow=True)
        dma("const", c_b2, b2.rearrange("(m p) -> p m", p=128), slow=True)
        memset(h_state, 0.0, tag="h0")

        def prep_w(w_hbm, name, rows, cols, scr, sw_col):
            RB = rows // 128
            CBLK = cols // 128
            acc = tiny.view(0, 128, 0, 1)
            t1 = tiny.view(0, 128, 1, 2)
            def get_slot(rb):
                if cols <= 1024:
                    return wsl.view(0, 128, (rb % 2) * cols, (rb % 2) * cols + cols)
                return dt_fm.view(0, 128, 0, cols)
            for rb in range(RB):
                slot = get_slot(rb)
                dma(f"pw_{name}{rb % 2}", slot,
                    w_hbm[rb * 128:(rb + 1) * 128, :], tag=f"prep_{name}_ld{rb}")
                reduce(t1, slot, ALU.add, absval=True, tag=f"prep_{name}_sum{rb}")
                if rb == 0:
                    copy(acc, t1, eng="dve")
                else:
                    tt(acc, acc, t1, ALU.add)
            tp = ps_mm[7].view(0, 1, 0, 128)
            transpose(tp, tiny.view(0, 128, 0, 1), tag=f"prep_{name}_tp")
            r0 = tiny.view(0, 1, 2, 3)
            reduce(r0, tp, ALU.add, tag=f"prep_{name}_tot")
            ts(r0, r0, 1.0 / (rows * cols), ALU.mult, 1e-5, ALU.max, eng="dve")
            dma_hbm("pw_sw", scr_sw[sw_col:sw_col + 1].rearrange("(a o) -> a o", o=1), r0,
                    "scr_sw", (0, 1, sw_col, sw_col + 1), engine="pool")
            dma_from_hbm(f"swbc{sw_col}", c_sw.view(0, 128, sw_col, sw_col + 1),
                         scr_sw[sw_col:sw_col + 1].rearrange("(o a) -> o a", o=1).broadcast_to((128, 1)),
                         "scr_sw", (0, 1, sw_col, sw_col + 1), engine="pool")
            swc = c_sw.view(0, 128, sw_col, sw_col + 1)
            for rb in range(RB):
                slot = get_slot(rb)
                dma(f"pw_{name}{rb % 2}", slot,
                    w_hbm[rb * 128:(rb + 1) * 128, :], tag=f"prep2_{name}_ld{rb}")
                qs = y_acc.view(0, 128, 0, cols)
                qs2 = xcs.view(0, 128, 0, cols)
                thr = tiny.view(0, 128, 12, 13)
                nthr = tiny.view(0, 128, 13, 14)
                if rb == 0:
                    ts(thr, swc, 0.5, ALU.mult, tag=f"prep_{name}_thr")
                    ts(nthr, thr, -1.0, ALU.mult, tag=f"prep_{name}_nthr")
                ts(qs2, slot, thr, ALU.is_ge, tag=f"prep_{name}_ge{rb}")
                ts(qs, slot, nthr, ALU.is_le, tag=f"prep_{name}_le{rb}")
                tt(qs, qs2, qs, ALU.subtract, tag=f"prep_{name}_q{rb}")
                for cb in range(CBLK):
                    tp2 = ps_mm[6].view(0, 128, 0, 128)
                    transpose(tp2, qs.view(0, 128, cb * 128, (cb + 1) * 128),
                              tag=f"prep_{name}_t{rb}_{cb}")
                    bslot = wslb.view(0, 128, 0, 128)
                    copy(bslot, tp2, eng="dve")
                    dma_hbm(f"pw_st_{name}", scr[cb * 128:(cb + 1) * 128,
                                                rb * 128:(rb + 1) * 128],
                            bslot, "scr_" + name,
                            (cb * 128, (cb + 1) * 128, rb * 128, (rb + 1) * 128),
                            engine="pool")

        prep_w(w1, "w1q", H, D, scr_w1q, 0)
        prep_w(w2, "w2q", D, H, scr_w2q, 1)

        def prep_split(w_hbm, name, rows, cols, scr):
            RB = rows // 128
            CH = max(1, cols // 4096)
            cw = cols // CH
            for rb in range(RB):
                for ch in range(CH):
                    slot = dt_fm.view(0, 128, 0, cw)
                    dma(f"psl_{name}", slot,
                        w_hbm[rb * 128:(rb + 1) * 128, ch * cw:(ch + 1) * cw],
                        tag=f"psl_{name}_{rb}_{ch}")
                    hi = TV(qbf.h, qbf.name, 0, 128, 0, cw)
                    lo = TV(qbf.h, qbf.name, 0, 128, cw, 2 * cw)
                    copy(hi, slot, eng="act", tag=f"psh_{name}_{rb}_{ch}")
                    stt(lo, hi, -1.0, slot, ALU.mult, ALU.add,
                        tag=f"pslo_{name}_{rb}_{ch}")
                    dma_hbm(f"pst_{name}", scr[rb * 128:(rb + 1) * 128,
                                               ch * cw:(ch + 1) * cw], hi,
                            "scr_" + name, (rb * 128, (rb + 1) * 128, ch * cw,
                                            (ch + 1) * cw), engine="pool",
                            tag=f"psth_{name}_{rb}_{ch}")
                    dma_hbm(f"pst_{name}", scr[rb * 128:(rb + 1) * 128,
                                               cols + ch * cw: cols + (ch + 1) * cw],
                            lo, "scr_" + name, (rb * 128, (rb + 1) * 128,
                                                cols + ch * cw, cols + (ch + 1) * cw),
                            engine="pool", tag=f"pstl_{name}_{rb}_{ch}")

        prep_split(w_in, "wz", D, 2 * DI, scr_wz)
        prep_split(w_ff1, "f1", D, FH, scr_f1)
        prep_split(w_ff2, "f2", FI, D, scr_f2)

        if cfg["stop_after"] == "prep":
            debug_dump("prep", [c_Aneg, c_sw])
            p.emit(nc, es)
            return nc, dbg_out

        for ti in range(NTILE):
            tb = ti * TT

            for c in range(NCH):
                dma(f"x{ti % 2}", x_tm.view(0, 128, c * D, (c + 1) * D),
                    x_in[tb + c * 128: tb + (c + 1) * 128, :], tag=f"ldx{ti}_{c}")

            def tm_ln_stats(src_tv, c, mu, r, tag):
                sums = tiny.view(0, 128, 4, 5)
                sq = tiny.view(0, 128, 5, 6)
                reduce(sums, src_tv, ALU.add, tag=f"{tag}_s")
                ts(mu, sums, 1.0 / (src_tv.c1 - src_tv.c0), ALU.mult)
                sqbuf = sr_rep.view(0, 128, 0, src_tv.c1 - src_tv.c0)
                act(sqbuf, src_tv, AF.Square, tag=f"{tag}_sq")
                reduce(sq, sqbuf, ALU.add, tag=f"{tag}_ss")
                ts(sq, sq, 1.0 / (src_tv.c1 - src_tv.c0), ALU.mult)
                mu2 = tiny.view(0, 128, 6, 7)
                tt(mu2, mu, mu, ALU.mult)
                var = tiny.view(0, 128, 7, 8)
                tt(var, sq, mu2, ALU.subtract)
                act(r, var, AF.Sqrt, bias=eps40.view(0, 128, 1, 2), tag=f"{tag}_r")
                recip(r, r, tag=f"{tag}_rr")

            def ln_tm_to_fm(src_tm, dst_bf, g_col, b_col, tag):
                for c in range(NCH):
                    mu = srow.view(0, 128, 0, 1)
                    r = srow.view(0, 128, 1, 2)
                    src = src_tm.view(0, 128, c * D, (c + 1) * D)
                    tm_ln_stats(src, c, mu, r, f"{tag}{c}")
                    xn = sr_rep.view(0, 128, 0, D)
                    ts(xn, src, mu, ALU.subtract, r, ALU.mult, tag=f"{tag}{c}_n")
                    for j in range(MB_D):
                        tp = ps_mm[6 + (j % 2)].view(0, 128, 0, 128)
                        transpose(tp, xn.view(0, 128, j * 128, (j + 1) * 128),
                                  tag=f"{tag}{c}_t{j}")
                        tmpf = v_q.view(0, 128, (j % 2) * 128, (j % 2) * 128 + 128)
                        ts(tmpf, tp, c_ln.view(0, 128, g_col + j, g_col + j + 1),
                           ALU.mult, c_ln.view(0, 128, b_col + j, b_col + j + 1),
                           ALU.add, tag=f"{tag}{c}_a{j}")
                        hi = TV(dst_bf.h, dst_bf.name, 0, 128,
                                2 * j * TT + c * 128, 2 * j * TT + (c + 1) * 128)
                        lo = TV(dst_bf.h, dst_bf.name, 0, 128,
                                (2 * j + 1) * TT + c * 128,
                                (2 * j + 1) * TT + (c + 1) * 128)
                        copy(hi, tmpf, eng="act", tag=f"{tag}{c}_h{j}")
                        stt(lo, hi, -1.0, tmpf, ALU.mult, ALU.add,
                            tag=f"{tag}{c}_l{j}")

            def abf_h(k):
                return TV(a_bf.h, a_bf.name, 0, 128, 2 * k * TT, 2 * k * TT + TT)

            def abf_l(k):
                return TV(a_bf.h, a_bf.name, 0, 128, (2 * k + 1) * TT,
                          (2 * k + 1) * TT + TT)

            ln_tm_to_fm(x_tm, a_bf, 0, MB_D, f"ln1_{ti}")

            if cfg["stop_after"] == "ln1":
                debug_dump("ln1", [a_bf], dt=BF16)
                break

            for mg in range(2):
                for k in range(MB_D):
                    wslot = wslb.view(0, 128, (k % 2) * 2048, (k % 2) * 2048 + 2048)
                    whi = wslot.view(0, 128, 0, 1024)
                    wlo = wslot.view(0, 128, 1024, 2048)
                    dma(f"wz{k % 2}", whi,
                        scr_wz[k * 128:(k + 1) * 128, mg * 1024:(mg + 1) * 1024],
                        tag=f"winh{ti}_{mg}_{k}")
                    dma(f"wz{k % 2}", wlo,
                        scr_wz[k * 128:(k + 1) * 128,
                               2 * DI + mg * 1024: 2 * DI + (mg + 1) * 1024],
                        tag=f"winl{ti}_{mg}_{k}")
                    for m8 in range(8):
                        m = mg * 8 + m8
                        pso = ps_mm[m8].view(0, 128, 0, TT)
                        mm(pso, whi.view(0, 128, m8 * 128, (m8 + 1) * 128),
                           abf_h(k), start=(k == 0), stop=False,
                           group=f"xz{ti}_{m}", tag=f"xzhh{ti}_{m}_{k}")
                        mm(pso, whi.view(0, 128, m8 * 128, (m8 + 1) * 128),
                           abf_l(k), start=False, stop=False,
                           group=f"xz{ti}_{m}", tag=f"xzhl{ti}_{m}_{k}")
                        mm(pso, wlo.view(0, 128, m8 * 128, (m8 + 1) * 128),
                           abf_h(k), start=False, stop=(k == MB_D - 1),
                           group=f"xz{ti}_{m}", tag=f"xzlh{ti}_{m}_{k}")
                for m8 in range(8):
                    m = mg * 8 + m8
                    dst = xcs.view(0, 128, m * (TT + 4) + 4, m * (TT + 4) + 4 + TT)
                    copy(dst, ps_mm[m8].view(0, 128, 0, TT), eng="act",
                         tag=f"xzev{ti}_{m}")

            if cfg["stop_after"] == "xz":
                debug_dump("xz", [xcs])
                break

            if ti == 0:
                for db in range(DB):
                    memset(xcs.view(0, 128, db * (TT + 4), db * (TT + 4) + 4), 0.0,
                           eng="pool", tag=f"halo0_{db}")
            for db in range(DB):
                base = db * (TT + 4)
                xc_t = lambda sh, db=db, base=base: xcs.view(0, 128, base + 4 - sh,
                                                             base + 4 + TT - sh)
                accv = v_q.view(0, 128, 0, TT)
                ts(accv, xc_t(3), c_convw.view(0, 128, db * 4, db * 4 + 1),
                   ALU.mult, tag=f"cv{ti}_{db}_0")
                for k in range(1, 4):
                    stt(accv, xc_t(3 - k),
                        c_convw.view(0, 128, db * 4 + k, db * 4 + k + 1),
                        accv, ALU.mult, ALU.add, tag=f"cv{ti}_{db}_{k}")
                if ti < NTILE - 1:
                    copy(xcs.view(0, 128, base, base + 4),
                         xcs.view(0, 128, base + TT, base + 4 + TT), eng="pool",
                         tag=f"halo{ti}_{db}")
                act(xcs.view(0, 128, base + 4, base + 4 + TT), accv, AF.Silu,
                    bias=c_convb.view(0, 128, db, db + 1), tag=f"silu{ti}_{db}")

            if cfg["stop_after"] == "conv":
                debug_dump("conv", [xcs])
                break

            NW = DTR + 2 * NST
            wxs = wsl.view(0, 128, 0, DB * NW)
            for k in range(DB):
                dma("wx", wxs.view(0, 128, k * NW, (k + 1) * NW),
                    w_x[k * 128:(k + 1) * 128, :], tag=f"wx{ti}_{k}")
            for k in range(DB):
                mm(ps_mm[5].view(0, 96, 0, TT), wxs.view(0, 128, k * NW, (k + 1) * NW).view(0, 128, 0, NW),
                   xcs.view(0, 128, k * (TT + 4) + 4, k * (TT + 4) + 4 + TT),
                   start=(k == 0), stop=(k == DB - 1), group=f"dbc{ti}",
                   tag=f"dbc{ti}_{k}")
            dbc_sb = sr_rep.view(0, 96, 0, TT)
            copy(dbc_sb, ps_mm[5].view(0, 96, 0, TT), eng="act", tag=f"dbcev{ti}")
            dma_hbm("bc_st", scr_rows[0:2 * NST, 0:TT],
                    dbc_sb.view(DTR, DTR + 2 * NST, 0, TT), "scr_rows",
                    (0, 2 * NST, 0, TT), engine="pool", tag=f"bcst{ti}")

            wdt_sb = wsl.view(0, 64, 0, DI)
            dma("wdt", wdt_sb, w_dt[:, :], tag=f"wdt{ti}")
            for db in range(DB):
                pdt = ps_mm[4].view(0, 128, 0, TT)
                mm(pdt, wdt_sb.view(0, 64, db * 128, (db + 1) * 128),
                   dbc_sb.view(0, DTR, 0, TT), start=True, stop=True,
                   group=f"dt{ti}_{db}", tag=f"mmdt{ti}_{db}")
                bcol = c_bdt.view(0, 128, db, db + 1)
                t_abs = v_q.view(0, 128, 0, TT)
                t_ln = v_q.view(0, 128, TT, 2 * TT)
                act(t_abs, pdt, AF.Abs, bias=bcol, tag=f"sp{ti}_{db}_abs")
                act(t_abs, t_abs, AF.Exp, scale=-1.0, tag=f"sp{ti}_{db}_exp")
                act(t_abs, t_abs, AF.Ln, bias=1.0, tag=f"sp{ti}_{db}_ln")
                act(t_ln, pdt, AF.Relu, bias=bcol, tag=f"sp{ti}_{db}_relu")
                tt(dt_fm.view(0, 128, db * TT, (db + 1) * TT), t_abs, t_ln, ALU.add,
                   tag=f"sp{ti}_{db}_add")

            if cfg["stop_after"] == "dt":
                debug_dump("dt", [dt_fm])
                break

            for q in range(DB // 2):
                db0 = q * 2
                vq_s = v_q.view(0, 128, 0, 2 * TT)
                for j in range(2):
                    db = db0 + j
                    tt(vq_s.view(0, 128, j * TT, (j + 1) * TT),
                       dt_fm.view(0, 128, db * TT, (db + 1) * TT),
                       xcs.view(0, 128, db * (TT + 4) + 4, db * (TT + 4) + 4 + TT),
                       ALU.mult, tag=f"v{ti}_{q}_{j}", eng="pool" if j else "dve")
                for n in range(NST):
                    par = n % 2
                    bn = bc_rep.view(0, 128, par * 2 * TT, par * 2 * TT + TT)
                    cn = bc_rep.view(0, 128, par * 2 * TT + TT, par * 2 * TT + 2 * TT)
                    def ebc(eng, n=n, par=par):
                        srcap = scr_rows[n::NST, 0:TT][0:2, :]
                        srcap = srcap.rearrange("(o a) b -> o a b", o=1)
                        srcap = srcap.broadcast_to((128, 2, TT))
                        dstap = bc_rep.ap()[:, par * 2 * TT:(par + 1) * 2 * TT]
                        dstap = dstap.rearrange("p (a b) -> p a b", a=2)
                        return eng.dma_start(out=dstap, in_=srcap)
                    p.op("sp", ebc, reads=[("scr_rows", n, NST + n + 1, 0, TT)],
                         writes=[(bc_rep.name, 0, 128, par * 2 * TT, (par + 1) * 2 * TT)],
                         dma_ch=f"bcrep{par}", tag=f"ldBC{ti}_{q}_{n}")
                    uq = u_q.view(0, 128, par * 2 * TT, (par + 1) * 2 * TT)
                    hq = h_q.view(0, 128, par * 2 * TT, (par + 1) * 2 * TT)
                    tt_bcast(uq, vq_s, bn, ALU.mult, 2, tag=f"u{ti}_{q}_{n}")
                    for j in range(2):
                        db = db0 + j
                        da = dA_t.view(0, 128, par * TT, (par + 1) * TT)
                        act(da, dt_fm.view(0, 128, db * TT, (db + 1) * TT), AF.Exp,
                            scale=c_Aneg.view(0, 128, db * NST + n, db * NST + n + 1),
                            tag=f"dA{ti}_{q}_{n}_{j}")
                        scan_op(hq.view(0, 128, j * TT, (j + 1) * TT), da,
                                uq.view(0, 128, j * TT, (j + 1) * TT),
                                h_state.view(0, 128, db * NST + n, db * NST + n + 1),
                                tag=f"scan{ti}_{q}_{n}_{j}")
                    def esv(eng, q=q, n=n, db0=db0, hq=hq):
                        src2 = hq.ap().rearrange("p (j t) -> p j t", j=2)[:, :, TT - 1:TT]
                        src2 = src2.rearrange("p j t -> p (j t)")
                        dst = h_state.ap().rearrange("p (db n) -> p db n", n=NST)[
                            :, db0:db0 + 2, n:n + 1].rearrange("p j t -> p (j t)")
                        return eng.activation(out=dst, in_=src2, func=AF.Copy)
                    p.op("act", esv, reads=[(h_q.name, 0, 128, hq.c0, hq.c1)],
                         writes=[(h_state.name, 0, 128, db0 * NST + n, (db0 + 1) * NST + n + 1)],
                         tag=f"hs{ti}_{q}_{n}")
                    tt(uq.view(0, 128, 0, TT), hq.view(0, 128, 0, TT), cn,
                       ALU.mult, tag=f"hc0{ti}_{q}_{n}")
                    tt(uq.view(0, 128, TT, 2 * TT), hq.view(0, 128, TT, 2 * TT), cn,
                       ALU.mult, eng="pool", tag=f"hc1{ti}_{q}_{n}")
                    for j2 in range(2):
                        mm(ps_mm[4 + j2].view(0, 128, 0, TT), ident,
                           uq.view(0, 128, j2 * TT, (j2 + 1) * TT),
                           start=(n == 0), stop=(n == NST - 1),
                           group=f"yac{ti}_{q}_{j2}", tag=f"yac{ti}_{q}_{n}_{j2}")
                for j2 in range(2):
                    copy(y_acc.view(0, 128, (db0 + j2) * TT, (db0 + j2 + 1) * TT),
                         ps_mm[4 + j2].view(0, 128, 0, TT), eng="act",
                         tag=f"yev{ti}_{q}_{j2}")
                zbank = 6 + (q % 2)
                for j in range(2):
                    db = db0 + j
                    zw = wslb.view(0, 128, (j % 2) * 2048, (j % 2) * 2048 + 2048)
                    zwh = zw.view(0, 128, 0, 1024)
                    zwl = zw.view(0, 128, 1024, 2048)
                    for half, dst, co in ((0, zwh, 2048 + db * 128),
                                          (1, zwl, 2 * DI + 2048 + db * 128)):
                        def ezw(eng, dst=dst, co=co):
                            srcap = scr_wz[:, co:co + 128]
                            srcap = srcap.rearrange("(k p) c -> p k c", p=128)
                            dstap = dst.ap().rearrange("p (k c) -> p k c", k=MB_D)
                            return eng.dma_start(out=dstap, in_=srcap)
                        p.op("sp", ezw, reads=[("scr_wz", 0, D, co, co + 128)],
                             writes=[dst.reg()], dma_ch=f"wz{j % 2}",
                             tag=f"zw{ti}_{db}_{half}")
                    psz = ps_mm[zbank].view(0, 128, 0, TT)
                    for k in range(MB_D):
                        mm(psz, zwh.view(0, 128, k * 128, (k + 1) * 128), abf_h(k),
                           start=(k == 0), stop=False,
                           group=f"z{ti}_{db}", tag=f"zhh{ti}_{db}_{k}")
                        mm(psz, zwh.view(0, 128, k * 128, (k + 1) * 128), abf_l(k),
                           start=False, stop=False,
                           group=f"z{ti}_{db}", tag=f"zhl{ti}_{db}_{k}")
                        mm(psz, zwl.view(0, 128, k * 128, (k + 1) * 128), abf_h(k),
                           start=False, stop=(k == MB_D - 1),
                           group=f"z{ti}_{db}", tag=f"zlh{ti}_{db}_{k}")
                    act(psz, psz, AF.Silu, tag=f"gz{ti}_{db}")
                    yv = y_acc.view(0, 128, db * TT, (db + 1) * TT)
                    stt(yv, xcs.view(0, 128, db * (TT + 4) + 4,
                                     db * (TT + 4) + 4 + TT),
                        c_dskip.view(0, 128, db, db + 1), yv, ALU.mult, ALU.add,
                        tag=f"gd{ti}_{db}")
                    tt(dt_fm.view(0, 128, db * TT, (db + 1) * TT), yv, psz, ALU.mult,
                       tag=f"gm{ti}_{db}")

            if cfg["stop_after"] == "scan":
                debug_dump("scan", [y_acc])
                break

            for k in range(DB):
                wslot = wsl.view(0, 128, (k % 2) * 1024, (k % 2) * 1024 + D)
                dma(f"wo{k % 2}", wslot, w_out[k * 128:(k + 1) * 128, :],
                    tag=f"wo{ti}_{k}")
                for m in range(MB_D):
                    mm(ps_mm[m].view(0, 128, 0, TT),
                       wslot.view(0, 128, m * 128, (m + 1) * 128),
                       dt_fm.view(0, 128, k * TT, (k + 1) * TT),
                       start=(k == 0), stop=(k == DB - 1), group=f"wo{ti}_{m}",
                       tag=f"mmo{ti}_{m}_{k}")
            for m in range(MB_D):
                stg = u_q.view(0, 128, (m % 4) * TT, (m % 4) * TT + TT)
                copy(stg, ps_mm[m].view(0, 128, 0, TT), eng="act", tag=f"woev{ti}_{m}")
                for c in range(NCH):
                    tbk = m if (m == 0 or c % 2 == 0) else m - 1
                    tp = ps_mm[tbk].view(0, 128, 0, 128)
                    transpose(tp, stg.view(0, 128, c * 128, (c + 1) * 128),
                              tag=f"brt{ti}_{c}_{m}")
                    xd = x_tm.view(0, 128, c * D + m * 128, c * D + (m + 1) * 128)
                    tt(xd, xd, tp, ALU.add, tag=f"bra{ti}_{c}_{m}")

            if cfg["stop_after"] == "wout":
                debug_dump("wout", [x_tm])
                break

            ln_tm_to_fm(x_tm, a_bf, 2 * MB_D, 3 * MB_D, f"ln2_{ti}")

            if cfg["stop_after"] == "ln2":
                debug_dump("ln2", [a_bf], dt=BF16)
                break

            for mg in range(8):
                for k in range(MB_D):
                    wslot = wslb.view(0, 128, (k % 2) * 2048, (k % 2) * 2048 + 2048)
                    for half, off in ((0, 0), (1, 1024)):
                        def eld(eng, k=k, mg=mg, wslot=wslot, half=half, off=off):
                            return eng.dma_start(
                                out=wslot.ap()[:, off:off + 1024].rearrange(
                                    "p (h x) -> p h x", h=2),
                                in_=scr_f1[k * 128:(k + 1) * 128,
                                           half * FH:(half + 1) * FH].rearrange(
                                    "p (h x) -> p h x", h=2)[:, :, mg * 512:(mg + 1) * 512])
                        p.op("sp", eld, reads=[("scr_f1", k * 128, (k + 1) * 128,
                                                half * FH + mg * 512,
                                                half * FH + FH)],
                             writes=[(wslot.name, 0, 128, wslot.c0 + off,
                                      wslot.c0 + off + 1024)], dma_ch=f"wf{k % 2}",
                             tag=f"ff1ld{ti}_{mg}_{k}_{half}")
                    for i in range(4):
                        for bank, woff, gtag in ((i, 0, "a"), (4 + i, 512, "g")):
                            pso = ps_mm[bank].view(0, 128, 0, TT)
                            grp = f"f1{gtag}{ti}_{mg}_{i}"
                            mm(pso, wslot.view(0, 128, woff + i * 128,
                                               woff + (i + 1) * 128),
                               abf_h(k), start=(k == 0), stop=False, group=grp,
                               tag=f"f1{gtag}hh{ti}_{mg}_{i}_{k}")
                            mm(pso, wslot.view(0, 128, woff + i * 128,
                                               woff + (i + 1) * 128),
                               abf_l(k), start=False, stop=False, group=grp,
                               tag=f"f1{gtag}hl{ti}_{mg}_{i}_{k}")
                            mm(pso, wslot.view(0, 128, 1024 + woff + i * 128,
                                               1024 + woff + (i + 1) * 128),
                               abf_h(k), start=False, stop=(k == MB_D - 1), group=grp,
                               tag=f"f1{gtag}lh{ti}_{mg}_{i}_{k}")
                if cfg["stop_after"] == "ff1w" and mg == 1:
                    break
                for i in range(4):
                    ma = mg * 4 + i
                    aev = v_q.view(0, 128, (i % 2) * TT, (i % 2) * TT + TT)
                    gev = u_q.view(0, 128, (i % 2) * TT, (i % 2) * TT + TT)
                    act(aev, ps_mm[i].view(0, 128, 0, TT), AF.Identity,
                        bias=c_bff1.view(0, 128, ma, ma + 1), tag=f"f1ae{ti}_{ma}")
                    act(gev, ps_mm[4 + i].view(0, 128, 0, TT), AF.Gelu_apprx_tanh,
                        bias=c_bff1.view(0, 128, 32 + ma, 32 + ma + 1),
                        tag=f"f1ge{ti}_{ma}")
                    tt(aev, aev, gev, ALU.mult, tag=f"f1m{ti}_{ma}")
                    agh = TV(qbf.h, qbf.name, 0, 128, (i % 2) * TT, (i % 2) * TT + TT)
                    agl = TV(qbf.h, qbf.name, 0, 128, 2 * TT + (i % 2) * TT,
                             2 * TT + (i % 2) * TT + TT)
                    copy(agh, aev, eng="act", tag=f"agh{ti}_{ma}")
                    stt(agl, agh, -1.0, aev, ALU.mult, ALU.add, tag=f"agl{ti}_{ma}")
                    dma_hbm(f"agst{i % 2}", scr_ag[ma * 128:(ma + 1) * 128, 0:TT],
                            agh, "scr_ag", (ma * 128, (ma + 1) * 128, 0, TT),
                            engine="pool", tag=f"agsth{ti}_{ma}")
                    dma_hbm(f"agst{i % 2}", scr_ag[ma * 128:(ma + 1) * 128, TT:2 * TT],
                            agl, "scr_ag", (ma * 128, (ma + 1) * 128, TT, 2 * TT),
                            engine="pool", tag=f"agstl{ti}_{ma}")
            if cfg["stop_after"] == "ff1w":
                break
            if cfg["stop_after"] == "ff1":
                debug_dump_hbm("ff1", "scr_ag", scr_ag[:, :], (FI, 2 * TT), dt=BF16)
                break
            for k in range(FI // 128):
                wslot = wslb.view(0, 128, (k % 2) * 2048, (k % 2) * 2048 + 2 * D)
                dma(f"wf{k % 2}", wslot, scr_f2[k * 128:(k + 1) * 128, :],
                    tag=f"ff2ld{ti}_{k}")
                agk = TV(qbf.h, qbf.name, 0, 128, 4 * TT + (k % 2) * 2 * TT,
                         4 * TT + (k % 2) * 2 * TT + 2 * TT)
                dma_from_hbm(f"agld{k % 2}", agk, scr_ag[k * 128:(k + 1) * 128, :],
                             "scr_ag", (k * 128, (k + 1) * 128, 0, 2 * TT),
                             engine="sp", tag=f"agld{ti}_{k}")
                aghk = agk.view(0, 128, 0, TT)
                aglk = agk.view(0, 128, TT, 2 * TT)
                for m in range(MB_D):
                    pso = ps_mm[m].view(0, 128, 0, TT)
                    grp = f"f2{ti}_{m}"
                    KL = FI // 128 - 1
                    mm(pso, wslot.view(0, 128, m * 128, (m + 1) * 128), aghk,
                       start=(k == 0), stop=False, group=grp, tag=f"f2hh{ti}_{m}_{k}")
                    mm(pso, wslot.view(0, 128, m * 128, (m + 1) * 128), aglk,
                       start=False, stop=False, group=grp, tag=f"f2hl{ti}_{m}_{k}")
                    mm(pso, wslot.view(0, 128, D + m * 128, D + (m + 1) * 128), aghk,
                       start=False, stop=(k == KL), group=grp, tag=f"f2lh{ti}_{m}_{k}")
            for m in range(MB_D):
                stg = u_q.view(0, 128, (m % 4) * TT, (m % 4) * TT + TT)
                act(stg, ps_mm[m].view(0, 128, 0, TT), AF.Identity,
                    bias=c_bff2.view(0, 128, m, m + 1), tag=f"f2ev{ti}_{m}")
                for c in range(NCH):
                    tbk = m if (m == 0 or c % 2 == 0) else m - 1
                    tp = ps_mm[tbk].view(0, 128, 0, 128)
                    transpose(tp, stg.view(0, 128, c * 128, (c + 1) * 128),
                              tag=f"fft{ti}_{c}_{m}")
                    xd = x_tm.view(0, 128, c * D + m * 128, c * D + (m + 1) * 128)
                    tt(xd, xd, tp, ALU.add, tag=f"ffa{ti}_{c}_{m}")

            for c in range(NCH):
                x0c = v_q.view(0, 128, 0, D)
                dma(f"xr{c % 2}", x0c, x_in[tb + c * 128: tb + (c + 1) * 128, :],
                    tag=f"ldxr{ti}_{c}")
                xd = x_tm.view(0, 128, c * D, (c + 1) * D)
                tt(xd, xd, x0c, ALU.add, tag=f"pl1a{ti}_{c}")
                mu = srow.view(0, 128, 0, 1)
                r = srow.view(0, 128, 1, 2)
                tm_ln_stats(xd, c, mu, r, f"pl1_{ti}{c}")
                ts(xd, xd, mu, ALU.subtract, r, ALU.mult, tag=f"pl1n{ti}_{c}")

            if cfg["stop_after"] == "postln1":
                debug_dump("postln1", [x_tm])
                break

            HB = H // 128
            fg1rep = dA_t.view(0, 128, 0, D)
            fb1rep = bc_rep.view(0, 128, 1024, 1024 + D)
            dma("fgrep", fg1rep,
                fg1.rearrange("(o d) -> o d", o=1).broadcast_to((128, D)),
                tag=f"fg1l{ti}")
            dma("fbrep", fb1rep,
                fb1.rearrange("(o d) -> o d", o=1).broadcast_to((128, D)),
                tag=f"fb1l{ti}")
            for c in range(NCH):
                srcx = x_tm.view(0, 128, c * D, (c + 1) * D)
                mu = srow.view(0, 128, 0, 1)
                r = srow.view(0, 128, 1, 2)
                tm_ln_stats(srcx, c, mu, r, f"fl1_{ti}{c}")
                xa = sr_rep.view(0, 128, 0, D)
                ts(xa, srcx, mu, ALU.subtract, r, ALU.mult, tag=f"fl1n{ti}_{c}")
                tt(xa, xa, fg1rep, ALU.mult, tag=f"fl1g{ti}_{c}")
                tt(xa, xa, fb1rep, ALU.add, tag=f"fl1b{ti}_{c}")
                am = srow.view(0, 128, 3, 4)
                reduce(am, xa, ALU.max, absval=True, tag=f"q1m{ti}_{c}")
                ts(am, am, 1e-5, ALU.max, tag=f"q1c{ti}_{c}")
                sc = srow.view(0, 128, 4, 5)
                recip(sc, am, tag=f"q1r{ti}_{c}")
                ts(sc, sc, 127.0, ALU.mult, tag=f"q1s{ti}_{c}")
                iv = srow.view(0, 128, 5, 6)
                ts(iv, am, 1.0 / 127.0, ALU.mult, tag=f"q1i{ti}_{c}")
                dma_hbm("q1iv", scr_rows[32 + c:33 + c, 0:128].rearrange("a b -> b a"),
                        iv, "scr_rows", (32 + c, 33 + c, 0, 128), engine="pool",
                        tag=f"q1ivs{ti}_{c}")
                yq = xa
                ts(yq, xa, sc, ALU.mult, tag=f"q1y{ti}_{c}")
                AA = u_q.view(0, 128, 0, D)
                BB = u_q.view(0, 128, D, 2 * D)
                ts(AA, yq, C1, ALU.add, C1, ALU.subtract, tag=f"q1A{ti}_{c}")
                ts(BB, yq, C2, ALU.add, C2, ALU.subtract, tag=f"q1B{ti}_{c}")
                tt(BB, BB, AA, ALU.subtract, tag=f"q1t{ti}_{c}")
                stt(yq, BB, ind_pp, AA, ALU.mult, ALU.add, tag=f"q1b{ti}_{c}")
                ts(yq, yq, -128.0, ALU.max, 127.0, ALU.min, tag=f"q1cl{ti}_{c}")
                for j in range(MB_D):
                    tp = ps_mm[6 + (j % 2)].view(0, 128, 0, 128)
                    transpose(tp, yq.view(0, 128, j * 128, (j + 1) * 128),
                              tag=f"q1t{ti}_{c}_{j}")
                    copy(TV(qbf.h, qbf.name, 0, 128, j * TT + c * 128,
                            j * TT + (c + 1) * 128), tp, eng="dve",
                         tag=f"q1e{ti}_{c}_{j}")
            if cfg["stop_after"] == "q1":
                debug_dump("q1", [TV(qbf.h, qbf.name, 0, 128, 0, MB_D * TT)], dt=BF16)
                break
            iv_rep_f = dA_t.view(0, 128, 0, TT)
            def eivl(eng, iv_rep_f=iv_rep_f):
                srcap = scr_rows[32:32 + NCH, 0:128]
                srcap = srcap.rearrange("(o a) b -> o a b", o=1)
                srcap = srcap.broadcast_to((128, NCH, 128))
                dstap = iv_rep_f.ap().rearrange("p (a b) -> p a b", a=NCH)
                return eng.dma_start(out=dstap, in_=srcap)
            p.op("sp", eivl, reads=[("scr_rows", 32, 32 + NCH, 0, 128)],
                 writes=[iv_rep_f.reg()], dma_ch="ivrep", tag=f"q1ivl{ti}")
            if ti == 0:
                memset(tiny.view(0, 128, 8, 9), 1.0, eng="pool", tag="ones")
            onescol = tiny.view(0, 128, 8, 9)

            psS = ps_mm[6].view(0, 1, 0, TT)
            psQ = ps_mm[7].view(0, 1, 0, TT)
            for mg in range(HB // 4):
                for k in range(MB_D):
                    wslot = wslb.view(0, 128, (k % 2) * 1024, (k % 2) * 1024 + 512)
                    dma(f"w1q{k % 2}", wslot,
                        scr_w1q[k * 128:(k + 1) * 128, mg * 512:(mg + 1) * 512],
                        tag=f"w1l{ti}_{mg}_{k}")
                    for i in range(4):
                        mm(ps_mm[i].view(0, 128, 0, TT),
                           wslot.view(0, 128, i * 128, (i + 1) * 128),
                           TV(qbf.h, qbf.name, 0, 128, k * TT, (k + 1) * TT),
                           start=(k == 0), stop=(k == MB_D - 1),
                           group=f"m1{ti}_{mg}_{i}", tag=f"m1{ti}_{mg}_{i}_{k}")
                for i in range(4):
                    m = mg * 4 + i
                    g1m = v_q.view(0, 128, (i % 2) * TT, (i % 2) * TT + TT)
                    tt(g1m, ps_mm[i].view(0, 128, 0, TT), iv_rep_f, ALU.mult,
                       tag=f"m1e{ti}_{m}")
                    act(g1m, g1m, AF.Gelu, scale=c_sw.view(0, 128, 0, 1),
                        bias=c_b1.view(0, 128, m, m + 1), tag=f"m1g{ti}_{m}")
                    sq = u_q.view(0, 128, (i % 2) * TT, (i % 2) * TT + TT)
                    act(sq, g1m, AF.Square, tag=f"m1sq{ti}_{m}")
                    mm(psS, onescol, g1m, start=(m == 0), stop=(m == HB - 1),
                       group=f"lnS{ti}", tag=f"lnS{ti}_{m}")
                    mm(psQ, onescol, sq, start=(m == 0), stop=(m == HB - 1),
                       group=f"lnQ{ti}", tag=f"lnQm{ti}_{m}")
                    dma_hbm(f"g1st{i % 2}", scr_g1[m * 128:(m + 1) * 128, :], g1m,
                            "scr_g1", (m * 128, (m + 1) * 128, 0, TT),
                            engine="pool", tag=f"g1st{ti}_{m}")
            if cfg["stop_after"] == "mm1":
                debug_dump_hbm("mm1", "scr_g1", scr_g1[:, :], (H, TT))
                break
            mu1 = v_q.view(0, 1, 0, TT)
            vv = v_q.view(0, 1, TT, 2 * TT)
            m2r = u_q.view(0, 1, 0, TT)
            act(mu1, psS, AF.Identity, scale=1.0 / H, tag=f"lnmu{ti}")
            act(vv, psQ, AF.Identity, scale=1.0 / H, tag=f"lnq{ti}")
            tt(m2r, mu1, mu1, ALU.mult, tag=f"lnm2{ti}")
            tt(vv, vv, m2r, ALU.subtract, tag=f"lnv{ti}")
            act(vv, vv, AF.Sqrt, bias=eps40.view(0, 1, 1, 2), tag=f"lnr{ti}")
            recip(vv, vv, tag=f"lnrr{ti}")
            dma_hbm("lnmu", scr_rows[36:37, 0:TT], mu1, "scr_rows",
                    (36, 37, 0, TT), engine="pool", tag=f"lnmus{ti}")
            dma_hbm("lnr", scr_rows[37:38, 0:TT], vv, "scr_rows",
                    (37, 38, 0, TT), engine="pool", tag=f"lnrs{ti}")
            mu_rep = sr_rep.view(0, 128, 0, TT)
            r_rep = sr_rep.view(0, 128, TT, 2 * TT)
            dma_from_hbm("murep", mu_rep,
                         scr_rows[36:37, 0:TT].broadcast_to((128, TT)),
                         "scr_rows", (36, 37, 0, TT), engine="sp", tag=f"lnmul{ti}")
            dma_from_hbm("rrep", r_rep,
                         scr_rows[37:38, 0:TT].broadcast_to((128, TT)),
                         "scr_rows", (37, 38, 0, TT), engine="sp", tag=f"lnrl{ti}")

            mx = dA_t
            mx = u_q.view(0, 128, TT, 2 * TT)
            for grp in range(HB // 4):
                gchunk = y_acc.view(0, 128, (grp % 2) * 4 * TT, ((grp % 2) * 4 + 4) * TT)
                def eg1ldA(eng, grp=grp, gchunk=gchunk):
                    srcap = scr_g1[grp * 512:(grp + 1) * 512, :].rearrange(
                        "(j p) t -> p j t", p=128)
                    dstap = gchunk.ap().rearrange("p (j t) -> p j t", j=4)
                    return eng.dma_start(out=dstap, in_=srcap)
                p.op("sp", eg1ldA, reads=[("scr_g1", grp * 512, (grp + 1) * 512, 0, TT)],
                     writes=[gchunk.reg()], dma_ch=f"g1ldA{grp % 2}",
                     tag=f"g1ldA{ti}_{grp}")
                tt_bcast(gchunk, gchunk, mu_rep, ALU.subtract, 4, tag=f"pAn{ti}_{grp}")
                tt_bcast(gchunk, gchunk, r_rep, ALU.mult, 4, tag=f"pAr{ti}_{grp}")
                for i in range(4):
                    j = grp * 4 + i
                    gj = gchunk.view(0, 128, i * TT, (i + 1) * TT)
                    ts(gj, gj, c_fln2.view(0, 128, j, j + 1), ALU.mult,
                       c_fln2.view(0, 128, HB + j, HB + j + 1), ALU.add,
                       tag=f"pAa{ti}_{grp}_{i}")
                gm = u_q.view(0, 128, 0, TT)
                def ered(eng, gchunk=gchunk, gm=gm):
                    src3 = gchunk.ap().rearrange("p (j t) -> p t j", j=4)
                    return eng.tensor_reduce(out=gm.ap(), in_=src3, axis=AX.X,
                                             op=ALU.max, apply_absolute_value=True)
                p.op("dve", ered, reads=[gchunk.reg()], writes=[gm.reg()],
                     tag=f"pAm{ti}_{grp}")
                if grp == 0:
                    copy(mx, gm, eng="dve", tag=f"pAc{ti}_{grp}")
                else:
                    tt(mx, mx, gm, ALU.max, tag=f"pAx{ti}_{grp}")
            for c in range(NCH):
                tp = ps_mm[6].view(0, 128, 0, 128)
                transpose(tp, mx.view(0, 128, c * 128, (c + 1) * 128),
                          tag=f"q2tp{ti}_{c}")
                mcol = tiny.view(0, 128, 9, 10)
                reduce(mcol, tp.view(0, 128, 0, 128), ALU.max, tag=f"q2rd{ti}_{c}")
                ts(mcol, mcol, 1e-5, ALU.max, tag=f"q2cl{ti}_{c}")
                iv2 = tiny.view(0, 128, 10, 11)
                ts(iv2, mcol, 1.0 / 127.0, ALU.mult, tag=f"q2iv{ti}_{c}")
                sc2 = tiny.view(0, 128, 11, 12)
                recip(sc2, mcol, tag=f"q2rc{ti}_{c}")
                ts(sc2, sc2, 127.0, ALU.mult, tag=f"q2sc{ti}_{c}")
                cb = c * 128
                dma_hbm("q2s", scr_rows[38:39, cb:cb + 128].rearrange("a b -> b a"),
                        sc2, "scr_rows", (38, 39, cb, cb + 128),
                        engine="pool", tag=f"q2ss{ti}_{c}")
                dma_hbm("q2i", scr_rows[39:40, cb:cb + 128].rearrange("a b -> b a"),
                        iv2, "scr_rows", (39, 40, cb, cb + 128),
                        engine="pool", tag=f"q2is{ti}_{c}")
            s_rep = bc_rep.view(0, 128, 0, TT)
            i_rep2 = bc_rep.view(0, 128, TT, 2 * TT)
            dma_from_hbm("q2sr", s_rep,
                         scr_rows[38:39, 0:TT].broadcast_to((128, TT)),
                         "scr_rows", (38, 39, 0, TT), engine="sp", tag=f"q2srl{ti}")
            dma_from_hbm("q2ir", i_rep2,
                         scr_rows[39:40, 0:TT].broadcast_to((128, TT)),
                         "scr_rows", (39, 40, 0, TT), engine="sp", tag=f"q2irl{ti}")

            if cfg["stop_after"] == "q2m":
                debug_dump("q2m", [sr_rep.view(0, 128, 0, 2 * TT),
                                   bc_rep.view(0, 128, 0, 2 * TT)])
                break
            for grp in range(HB // 4):
                gchunk = y_acc.view(0, 128, (grp % 2) * 4 * TT, ((grp % 2) * 4 + 4) * TT)
                def eg1ldB(eng, grp=grp, gchunk=gchunk):
                    srcap = scr_g1[grp * 512:(grp + 1) * 512, :].rearrange(
                        "(j p) t -> p j t", p=128)
                    dstap = gchunk.ap().rearrange("p (j t) -> p j t", j=4)
                    return eng.dma_start(out=dstap, in_=srcap)
                p.op("sp", eg1ldB, reads=[("scr_g1", grp * 512, (grp + 1) * 512, 0, TT)],
                     writes=[gchunk.reg()], dma_ch=f"g1ldB{grp % 2}",
                     tag=f"g1ldB{ti}_{grp}")
                tt_bcast(gchunk, gchunk, mu_rep, ALU.subtract, 4, tag=f"pBn{ti}_{grp}")
                tt_bcast(gchunk, gchunk, r_rep, ALU.mult, 4, tag=f"pBr{ti}_{grp}")
                for i in range(4):
                    j = grp * 4 + i
                    gj = gchunk.view(0, 128, i * TT, (i + 1) * TT)
                    ts(gj, gj, c_fln2.view(0, 128, j, j + 1), ALU.mult,
                       c_fln2.view(0, 128, HB + j, HB + j + 1), ALU.add,
                       tag=f"pBa{ti}_{grp}_{i}")
                tt_bcast(gchunk, gchunk, s_rep, ALU.mult, 4, tag=f"q2y{ti}_{grp}")
                AA = u_q.view(0, 128, 0, 2 * TT)
                BB = u_q.view(0, 128, 2 * TT, 4 * TT)
                for hf in range(2):
                    gh = gchunk.view(0, 128, hf * 2 * TT, (hf + 1) * 2 * TT)
                    ts(AA, gh, C1, ALU.add, C1, ALU.subtract, tag=f"q2A{ti}_{grp}_{hf}")
                    ts(BB, gh, C2, ALU.add, C2, ALU.subtract, tag=f"q2B{ti}_{grp}_{hf}")
                    tt(BB, BB, AA, ALU.subtract, tag=f"q2d{ti}_{grp}_{hf}")
                    stt(gh, BB, ind_pp, AA, ALU.mult, ALU.add, tag=f"q2bl{ti}_{grp}_{hf}")
                    ts(gh, gh, -128.0, ALU.max, 127.0, ALU.min, tag=f"q2c{ti}_{grp}_{hf}")
                q2c = TV(qbf.h, qbf.name, 0, 128, MB_D * TT + (grp % 2) * 4 * TT,
                         MB_D * TT + (grp % 2) * 4 * TT + 4 * TT)
                copy(q2c, gchunk, eng="dve", tag=f"q2cast{ti}_{grp}")
                if grp == 0:
                    debug_dump("q2grp0", [gchunk])
                for kk in range(4):
                    k = grp * 4 + kk
                    wslot = wslb.view(0, 128, (k % 2) * 1024, (k % 2) * 1024 + D)
                    dma(f"w2q{k % 2}", wslot, scr_w2q[k * 128:(k + 1) * 128, :],
                        tag=f"w2l{ti}_{k}")
                    for m in range(MB_D):
                        mm(ps_mm[m].view(0, 128, 0, TT),
                           wslot.view(0, 128, m * 128, (m + 1) * 128),
                           q2c.view(0, 128, kk * TT, (kk + 1) * TT),
                           start=(k == 0), stop=(k == HB - 1),
                           group=f"m2{ti}_{m}", tag=f"m2{ti}_{m}_{k}")
            if cfg["stop_after"] == "mm2":
                pass

            for m in range(MB_D):
                ev = v_q.view(0, 128, 0, TT)
                tt(ev, ps_mm[m].view(0, 128, 0, TT), i_rep2, ALU.mult,
                   tag=f"m2e{ti}_{m}")
                stg = u_q.view(0, 128, (m % 2) * TT, (m % 2) * TT + TT)
                act(stg, ev, AF.Identity,
                    scale=c_sw.view(0, 128, 1, 2), bias=c_b2.view(0, 128, m, m + 1),
                    tag=f"m2b{ti}_{m}")
                for c in range(NCH):
                    tbk = m if (m == 0 or c % 2 == 0) else m - 1
                    tp = ps_mm[tbk].view(0, 128, 0, 128)
                    transpose(tp, stg.view(0, 128, c * 128, (c + 1) * 128),
                              tag=f"m2t{ti}_{c}_{m}")
                    xd = x_tm.view(0, 128, c * D + m * 128, c * D + (m + 1) * 128)
                    tt(xd, xd, tp, ALU.add, tag=f"m2a{ti}_{c}_{m}")
            for c in range(NCH):
                xd = x_tm.view(0, 128, c * D, (c + 1) * D)
                mu = srow.view(0, 128, 0, 1)
                r = srow.view(0, 128, 1, 2)
                tm_ln_stats(xd, c, mu, r, f"pl2_{ti}{c}")
                ts(xd, xd, mu, ALU.subtract, r, ALU.mult, tag=f"pl2n{ti}_{c}")
                dma_hbm(f"out{ti % 2}", out_hbm[tb + c * 128: tb + (c + 1) * 128, :],
                        xd, "out", (tb + c * 128, tb + (c + 1) * 128, 0, D),
                        engine="sp", tag=f"st{ti}_{c}")

        p.emit(nc, es)
    return nc, dbg_out


def kernel(**inputs):
    cfg = default_cfg()
    nc, _ = build_program(cfg)

    x_full = np.ascontiguousarray(np.asarray(inputs["L0_attn_x"], np.float32))
    B = x_full.shape[0]
    base = {
        "w_in": inputs["L0_attn_w_in"], "conv_w": inputs["L0_attn_conv_w"],
        "conv_b": inputs["L0_attn_conv_b"], "w_x": inputs["L0_attn_w_x"],
        "w_dt": inputs["L0_attn_w_dt"], "b_dt": inputs["L0_attn_b_dt"],
        "a_log": inputs["L0_attn_a_log"], "d_skip": inputs["L0_attn_d_skip"],
        "w_out": inputs["L0_attn_w_out"], "ln1_g": inputs["L0_attn_ln1_g"],
        "ln1_b": inputs["L0_attn_ln1_b"], "ln2_g": inputs["L0_attn_ln2_g"],
        "ln2_b": inputs["L0_attn_ln2_b"], "w_ff1": inputs["L0_attn_w_ff1"],
        "b_ff1": inputs["L0_attn_b_ff1"], "w_ff2": inputs["L0_attn_w_ff2"],
        "b_ff2": inputs["L0_attn_b_ff2"], "f_ln1_g": inputs["L0_ffn_ln1_g"],
        "f_ln1_b": inputs["L0_ffn_ln1_b"], "w1": inputs["L0_ffn_w1"],
        "b1": inputs["L0_ffn_b1"], "f_ln2_g": inputs["L0_ffn_ln2_g"],
        "f_ln2_b": inputs["L0_ffn_ln2_b"], "w2": inputs["L0_ffn_w2"],
        "b2": inputs["L0_ffn_b2"],
        "ident_hbm": np.eye(128, dtype=np.float32),
        "eps40_hbm": np.concatenate([np.full((128, 1), 2.0**-40, np.float32), np.full((128, 1), 1e-5, np.float32)], axis=1),
    }
    base = {k: np.ascontiguousarray(np.asarray(v, np.float32)) for k, v in base.items()}
    in_maps = []
    for b in range(B):
        m = dict(base)
        m["x"] = np.ascontiguousarray(x_full[b])
        in_maps.append(m)
    res = run_bass_kernel_spmd(nc, in_maps, core_ids=list(range(B)))
    out = np.stack([np.asarray(r["out"], np.float32) for r in res.results], axis=0)
    return out


# version 08 (final)
# speedup vs baseline: 1.0708x; 1.0126x over previous
import numpy as np
from contextlib import ExitStack

import concourse.bass as bass
import concourse.mybir as mybir
from concourse.bass_utils import run_bass_kernel_spmd

F32 = mybir.dt.float32
BF16 = mybir.dt.bfloat16
ALU = mybir.AluOpType
AF = mybir.ActivationFunctionType
AX = mybir.AxisListType

C1 = 1.5 * 2.0**23
C2 = 1.5 * 2.0**52

ENGINES = ("sp", "pe", "act", "dve", "pool")


class Op:
    __slots__ = ("idx", "engine", "emit", "deps", "need_inc", "inc_val", "dma_ch",
                 "dma_pos", "group", "tag")

    def __init__(self, idx, engine, emit, dma_ch, group, tag):
        self.idx = idx
        self.engine = engine
        self.emit = emit
        self.deps = []
        self.need_inc = False
        self.inc_val = None
        self.dma_ch = dma_ch
        self.dma_pos = None
        self.group = group
        self.tag = tag


class Prog:
    def __init__(self):
        self.ops = []
        self.access = {}
        self.chan_counts = {}

    def op(self, engine, emit, reads=(), writes=(), dma_ch=None, group=None, tag=""):
        idx = len(self.ops)
        o = Op(idx, engine, emit, dma_ch, group, tag)
        if dma_ch is not None:
            self.chan_counts[dma_ch] = self.chan_counts.get(dma_ch, 0) + 1
            o.dma_pos = self.chan_counts[dma_ch]
        deps = set()
        for r in reads:
            self._scan(r, False, idx, deps, group)
        for w in writes:
            self._scan(w, True, idx, deps, group)
        o.deps = sorted(deps)
        self.ops.append(o)
        for r in reads:
            self._record(r, idx, False)
        for w in writes:
            self._record(w, idx, True)
        return idx

    @staticmethod
    def _overlap(a, b):
        return a[0] < b[1] and b[0] < a[1] and a[2] < b[3] and b[2] < a[3]

    def _scan(self, region, is_write, idx, deps, group):
        name, p0, p1, c0, c1 = region
        lst = self.access.get(name)
        if not lst:
            return
        me = (p0, p1, c0, c1)
        for rec in reversed(lst):
            oidx, w, q0, q1, d0, d1 = rec
            if not (w or is_write):
                continue
            if self._overlap(me, (q0, q1, d0, d1)):
                o = self.ops[oidx]
                if group is not None and o.group == group:
                    continue
                deps.add(oidx)

    def _record(self, region, idx, is_write):
        name, p0, p1, c0, c1 = region
        lst = self.access.setdefault(name, [])
        if is_write and p0 == 0 and c0 == 0:
            keep = []
            for rec in lst:
                if not (rec[2] >= p0 and rec[3] <= p1 and rec[4] >= c0 and rec[5] <= c1):
                    keep.append(rec)
            if len(keep) != len(lst):
                self.access[name] = lst = keep
        lst.append([idx, is_write, p0, p1, c0, c1])
        if len(lst) > 4000:
            del lst[: len(lst) - 3000]

    def emit(self, nc, es):
        ops = self.ops
        for o in ops:
            for d in o.deps:
                p = ops[d]
                if p.dma_ch is None:
                    p.need_inc = True
        counters = {e: 0 for e in ENGINES}
        for o in ops:
            if o.dma_ch is None and o.need_inc:
                counters[o.engine] += 1
                o.inc_val = counters[o.engine]

        eng_sems = {e: es.enter_context(nc.semaphore(f"es_{e}")) for e in ENGINES}
        ch_sems = {c: es.enter_context(nc.semaphore(f"ch_{c}")) for c in self.chan_counts}
        import bisect as _bisect
        chan_idxs = {}
        for o in ops:
            if o.dma_ch is not None:
                chan_idxs.setdefault(o.dma_ch, []).append(o.idx)

        per_engine = {e: [o for o in ops if o.engine == e] for e in ENGINES}
        emitted_ok = []

        def make_body(ename):
            def body(eng):
                watermark = {}
                for o in per_engine[ename]:
                    needs = {}
                    for d in o.deps:
                        p = ops[d]
                        if p.dma_ch is not None:
                            key = ("ch", p.dma_ch)
                            if p.dma_ch.startswith("const"):
                                val = 16 * self.chan_counts[p.dma_ch]
                            else:
                                val = 16 * _bisect.bisect_left(chan_idxs[p.dma_ch], o.idx)
                            if needs.get(key, 0) < val:
                                needs[key] = val
                        else:
                            key = ("e", p.engine)
                            if needs.get(key, 0) < p.inc_val:
                                needs[key] = p.inc_val
                    for key, val in needs.items():
                        if watermark.get(key, 0) < val:
                            sem = ch_sems[key[1]] if key[0] == "ch" else eng_sems[key[1]]
                            eng.wait_ge(sem, val)
                            watermark[key] = val
                            self.emit_stats[(ename, "wait_" + key[0])] += 1
                    try:
                        inst = o.emit(eng)
                    except Exception as ex:
                        raise RuntimeError(f"emit failed at op {o.idx} tag={o.tag!r}") from ex
                    if o.dma_ch is not None:
                        inst.then_inc(ch_sems[o.dma_ch], 16)
                    elif o.need_inc:
                        inst.then_inc(eng_sems[o.engine], 1)
                    emitted_ok.append(o.idx)
                done = set()
                for o in per_engine[ename]:
                    if o.dma_ch is not None and o.dma_ch not in done:
                        done.add(o.dma_ch)
                for c in done:
                    eng.wait_ge(ch_sems[c], 16 * self.chan_counts[c])
            return body

        import collections as _c
        self.emit_stats = _c.Counter()
        block = es.enter_context(nc.Block())
        block.sync(make_body("sp"))
        block.tensor(make_body("pe"))
        block.scalar(make_body("act"))
        block.vector(make_body("dve"))
        block.gpsimd(make_body("pool"))
        assert len(emitted_ok) == len(ops)


class TV:

    __slots__ = ("h", "name", "p0", "p1", "c0", "c1")

    def __init__(self, h, name, p0, p1, c0, c1):
        self.h, self.name, self.p0, self.p1, self.c0, self.c1 = h, name, p0, p1, c0, c1

    def ap(self):
        return self.h.ap()[self.p0:self.p1, self.c0:self.c1]

    def reg(self):
        return (self.name, self.p0, self.p1, self.c0, self.c1)

    def view(self, p0, p1, c0, c1):
        return TV(self.h, self.name, self.p0 + p0, self.p0 + p1,
                  self.c0 + c0, self.c0 + c1)


def _tv(h, name, P, C):
    return TV(h, name, 0, P, 0, C)


def default_cfg():
    return dict(
        T=4096,
        TT=512,
        D=1024,
        DI=2048,
        NST=16,
        DTR=64,
        FH=8192,
        FI=4096,
        H=4096,
        EPS=1e-5,
        stop_after=None,
        debug=(),
    )


def build_program(cfg):
    T, TT, D, DI = cfg["T"], cfg["TT"], cfg["D"], cfg["DI"]
    NST, DTR, FH, FI, H = cfg["NST"], cfg["DTR"], cfg["FH"], cfg["FI"], cfg["H"]
    EPS = cfg["EPS"]
    NTILE = T // TT
    NCH = TT // 128
    DB = DI // 128
    MB_D = D // 128
    QD = 4
    NQ = DB // QD

    nc = bass.Bass("TRN2", target_bir_lowering=False)
    p = Prog()

    def din(name, shape):
        return nc.dram_tensor(name, list(shape), F32, kind="ExternalInput").ap()

    x_in = din("x", (T, D))
    w_in = din("w_in", (D, 2 * DI))
    conv_w = din("conv_w", (DI, 4))
    conv_b = din("conv_b", (DI,))
    w_x = din("w_x", (DI, DTR + 2 * NST))
    w_dt = din("w_dt", (DTR, DI))
    b_dt = din("b_dt", (DI,))
    a_log = din("a_log", (DI, NST))
    d_skip = din("d_skip", (DI,))
    w_out = din("w_out", (DI, D))
    ln1_g, ln1_b = din("ln1_g", (D,)), din("ln1_b", (D,))
    ln2_g, ln2_b = din("ln2_g", (D,)), din("ln2_b", (D,))
    w_ff1 = din("w_ff1", (D, FH))
    b_ff1 = din("b_ff1", (FH,))
    w_ff2 = din("w_ff2", (FI, D))
    b_ff2 = din("b_ff2", (D,))
    fg1, fb1 = din("f_ln1_g", (D,)), din("f_ln1_b", (D,))
    fg2, fb2 = din("f_ln2_g", (H,)), din("f_ln2_b", (H,))
    w1 = din("w1", (H, D))
    b1 = din("b1", (H,))
    w2 = din("w2", (D, H))
    b2 = din("b2", (D,))
    ident_in = din("ident_hbm", (128, 128))
    eps40_in = din("eps40_hbm", (128, 2))

    out_hbm = nc.dram_tensor("out", [T, D], F32, kind="ExternalOutput").ap()

    scr_w1q = nc.dram_tensor("scr_w1q", [D, H], BF16, kind="Internal").ap()
    scr_w2q = nc.dram_tensor("scr_w2q", [H, D], BF16, kind="Internal").ap()
    scr_rows = nc.dram_tensor("scr_rows", [64, TT], F32, kind="Internal").ap()
    scr_ag = nc.dram_tensor("scr_ag", [FI, 2 * TT], BF16, kind="Internal").ap()
    scr_wz = nc.dram_tensor("scr_wz", [D, 2 * 2 * DI], BF16, kind="Internal").ap()
    scr_f1 = nc.dram_tensor("scr_f1", [D, 2 * FH], BF16, kind="Internal").ap()
    scr_f2 = nc.dram_tensor("scr_f2", [FI, 2 * D], BF16, kind="Internal").ap()
    scr_g1 = nc.dram_tensor("scr_g1", [H, TT], F32, kind="Internal").ap()
    scr_g1n = nc.dram_tensor("scr_g1n", [H, TT], F32, kind="Internal").ap()
    scr_sw = nc.dram_tensor("scr_sw", [2], F32, kind="Internal").ap()

    dbg_out = {}
    for name in cfg["debug"]:
        pass

    es = ExitStack()

    def sb(name, P, C, dt=F32):
        h = es.enter_context(nc.sbuf_tensor(name, [P, C], dt))
        return _tv(h, name, P, C)

    def psum(name, P, C):
        h = es.enter_context(nc.psum_tensor(name, [P, C], F32))
        return _tv(h, name, P, C)

    with es:
        ident = sb("ident", 128, 128)
        eps40 = sb("eps40", 128, 2)
        ind_pp = sb("ind_pp", 128, 1)
        c_convw = sb("c_convw", 128, DB * 4)
        c_convb = sb("c_convb", 128, DB)
        c_bdt = sb("c_bdt", 128, DB)
        c_dskip = sb("c_dskip", 128, DB)
        c_Aneg = sb("c_Aneg", 128, DB * NST)
        c_ln = sb("c_ln", 128, 4 * (D // 128))
        c_fln2 = sb("c_fln2", 128, 2 * (H // 128))
        c_bff1 = sb("c_bff1", 128, FH // 128)
        c_bff2 = sb("c_bff2", 128, D // 128)
        c_b1 = sb("c_b1", 128, H // 128)
        c_b2 = sb("c_b2", 128, D // 128)
        c_sw = sb("c_sw", 128, 2)
        h_state = sb("h_state", 128, DB * NST)

        x_tm = sb("x_tm", 128, NCH * D)
        a_bf = sb("a_bf", 128, 2 * MB_D * TT, BF16)
        xcs = sb("xcs", 128, DB * (TT + 4))
        dt_fm = sb("dt_fm", 128, max(DB * TT, H))
        y_acc = sb("y_acc", 128, DB * TT)
        v_q = sb("v_q", 128, max(2 * TT, D))
        h_q = sb("h_q", 128, QD * TT)
        u_q = sb("u_q", 128, QD * TT)
        dA_t = sb("dA_t", 128, 2 * TT)
        bc_rep = sb("bc_rep", 128, 4 * TT)
        wsl = sb("wsl", 128, 2 * 1024)
        wslb = sb("wslb", 128, 2 * 2048, BF16)
        qbf = sb("qbf", 128, max(FH, 2 * D), BF16)
        srow = sb("srow", 128, 8)
        sr_rep = sb("sr_rep", 128, max(2 * TT, D))
        tiny = sb("tiny", 128, 16)

        ps_mm = [psum(f"ps{i}", 128, 512) for i in range(8)]

        def dma(ch, dst, src, engine="sp", tag="", slow=False):
            def e(eng, dst=dst, src=src, slow=slow):
                return eng.dma_start(out=dst.ap() if isinstance(dst, TV) else dst,
                                     in_=src.ap() if isinstance(src, TV) else src,
                                     allow_slow_non_contiguous=slow)
            reads = [src.reg()] if isinstance(src, TV) else []
            writes = [dst.reg()] if isinstance(dst, TV) else []
            p.op(engine, e, reads=reads, writes=writes, dma_ch=ch, tag=tag)

        def dma_hbm(ch, dst_ap, src, hbm_name, hr, engine="sp", tag=""):
            def e(eng, dst_ap=dst_ap, src=src):
                return eng.dma_start(out=dst_ap, in_=src.ap())
            p.op(engine, e, reads=[src.reg()],
                 writes=[(hbm_name,) + hr], dma_ch=ch, tag=tag)

        def dma_from_hbm(ch, dst, src_ap, hbm_name, hr, engine="sp", tag=""):
            def e(eng, dst=dst, src_ap=src_ap):
                return eng.dma_start(out=dst.ap(), in_=src_ap)
            p.op(engine, e, reads=[(hbm_name,) + hr], writes=[dst.reg()],
                 dma_ch=ch, tag=tag)

        def act(out, in_, func, scale=None, bias=None, extra_reads=(), eng="act", tag=""):
            def e(engine, out=out, in_=in_, func=func, scale=scale, bias=bias):
                kw = {}
                if scale is not None:
                    kw["scale"] = scale.ap() if isinstance(scale, TV) else scale
                if bias is not None:
                    kw["bias"] = bias.ap() if isinstance(bias, TV) else bias
                return engine.activation(out=out.ap(), in_=in_.ap(), func=func, **kw)
            reads = [in_.reg()]
            if isinstance(scale, TV):
                reads.append(scale.reg())
            if isinstance(bias, TV):
                reads.append(bias.reg())
            reads.extend(r.reg() for r in extra_reads)
            p.op(eng, e, reads=reads, writes=[out.reg()], tag=tag)

        def tt(out, in0, in1, op, eng="dve", tag=""):
            def e(engine, out=out, in0=in0, in1=in1, op=op):
                return engine.tensor_tensor(out=out.ap(), in0=in0.ap(), in1=in1.ap(), op=op)
            p.op(eng, e, reads=[in0.reg(), in1.reg()], writes=[out.reg()], tag=tag)

        def tt_bcast(out, in0, in1, op, reps, eng="dve", tag=""):
            def e(engine, out=out, in0=in0, in1=in1, op=op, reps=reps):
                w = in1.c1 - in1.c0
                ap1 = in1.ap().rearrange("p (r w) -> p r w", r=1).broadcast_to(
                    (128, reps, w))
                ap0 = in0.ap().rearrange("p (r w) -> p r w", r=reps)
                apo = out.ap().rearrange("p (r w) -> p r w", r=reps)
                return engine.tensor_tensor(out=apo, in0=ap0, in1=ap1, op=op)
            p.op(eng, e, reads=[in0.reg(), in1.reg()], writes=[out.reg()], tag=tag)

        def ts(out, in_, s1, op1, s2=None, op2=None, eng="dve", tag=""):
            def e(engine, out=out, in_=in_, s1=s1, op1=op1, s2=s2, op2=op2):
                a1 = s1.ap() if isinstance(s1, TV) else s1
                if s2 is None:
                    return engine.tensor_single_scalar(out=out.ap(), in_=in_.ap(),
                                                       scalar=a1, op=op1)
                a2 = s2.ap() if isinstance(s2, TV) else s2
                return engine.tensor_scalar(out=out.ap(), in0=in_.ap(), scalar1=a1,
                                            scalar2=a2, op0=op1, op1=op2)
            reads = [in_.reg()]
            for s in (s1, s2):
                if isinstance(s, TV):
                    reads.append(s.reg())
            p.op(eng, e, reads=reads, writes=[out.reg()], tag=tag)

        def stt(out, in0, scalar, in1, op0, op1, eng="dve", tag=""):
            def e(engine, out=out, in0=in0, scalar=scalar, in1=in1, op0=op0, op1=op1):
                s = scalar.ap() if isinstance(scalar, TV) else scalar
                return engine.scalar_tensor_tensor(out=out.ap(), in0=in0.ap(), scalar=s,
                                                   in1=in1.ap(), op0=op0, op1=op1)
            reads = [in0.reg(), in1.reg()]
            if isinstance(scalar, TV):
                reads.append(scalar.reg())
            p.op(eng, e, reads=reads, writes=[out.reg()], tag=tag)

        def scan_op(out, data0, data1, initial, tag=""):
            def e(engine, out=out, data0=data0, data1=data1, initial=initial):
                ini = initial.ap() if isinstance(initial, TV) else initial
                return engine.tensor_tensor_scan(out=out.ap(), data0=data0.ap(),
                                                 data1=data1.ap(), initial=ini,
                                                 op0=ALU.mult, op1=ALU.add)
            reads = [data0.reg(), data1.reg()]
            if isinstance(initial, TV):
                reads.append(initial.reg())
            p.op("dve", e, reads=reads, writes=[out.reg()], tag=tag)

        def reduce(out, in_, op, absval=False, eng="dve", tag=""):
            def e(engine, out=out, in_=in_, op=op, absval=absval):
                return engine.tensor_reduce(out=out.ap(), in_=in_.ap(), axis=AX.X,
                                            op=op, apply_absolute_value=absval)
            p.op(eng, e, reads=[in_.reg()], writes=[out.reg()], tag=tag)

        def recip(out, in_, tag=""):
            def e(engine, out=out, in_=in_):
                return engine.reciprocal(out=out.ap(), in_=in_.ap())
            p.op("dve", e, reads=[in_.reg()], writes=[out.reg()], tag=tag)

        def copy(out, in_, eng="act", tag=""):
            if eng == "act":
                act(out, in_, AF.Copy, tag=tag)
            else:
                def e(engine, out=out, in_=in_):
                    return engine.tensor_copy(out=out.ap(), in_=in_.ap())
                p.op(eng, e, reads=[in_.reg()], writes=[out.reg()], tag=tag)

        def mm(out, lhsT, rhs, start, stop, group, tag=""):
            def e(engine, out=out, lhsT=lhsT, rhs=rhs, start=start, stop=stop):
                return engine.matmul(out=out.ap(), lhsT=lhsT.ap(), rhs=rhs.ap(),
                                     start=start, stop=stop)
            p.op("pe", e, reads=[lhsT.reg(), rhs.reg()], writes=[out.reg()],
                 group=group, tag=tag)

        def transpose(out_ps, in_, tag=""):
            def e(engine, out_ps=out_ps, in_=in_):
                return engine.transpose(out=out_ps.ap(), in_=in_.ap(),
                                        identity=ident.ap())
            p.op("pe", e, reads=[in_.reg(), ident.reg()], writes=[out_ps.reg()],
                 tag=tag)

        def memset(buf, val, eng="dve", tag=""):
            def e(engine, buf=buf, val=val):
                return engine.memset(buf.ap(), val)
            p.op(eng, e, reads=[], writes=[buf.reg()], tag=tag)

        dbg_seq = [0]

        def debug_dump(name, tv_list, dt=F32):
            if name not in cfg["debug"]:
                return
            P = 128
            Ctot = sum(t.c1 - t.c0 for t in tv_list)
            d = nc.dram_tensor(f"dbg_{name}", [P, Ctot], dt, kind="ExternalOutput").ap()
            c = 0
            for t in tv_list:
                w = t.c1 - t.c0
                def e(eng, t=t, d=d, c=c, w=w):
                    return eng.dma_start(out=d[:, c:c + w], in_=t.ap())
                p.op("pool", e, reads=[t.reg()], writes=[], dma_ch="dbg", tag=f"dbg_{name}")
                c += w
            dbg_out[name] = d

        def debug_dump_hbm(name, src_name, src_hbm_ap, shape, dt=F32):
            if name not in cfg["debug"]:
                return
            d = nc.dram_tensor(f"dbg_{name}", list(shape), dt, kind="ExternalOutput").ap()
            def e(eng, d=d, src_hbm_ap=src_hbm_ap):
                return eng.dma_start(out=d, in_=src_hbm_ap)
            p.op("pool", e, reads=[(src_name, 0, shape[0], 0, shape[1])], writes=[],
                 dma_ch="dbg", tag=f"dbg_{name}")

        dma("const", ident, ident_in)
        dma("const", eps40, eps40_in)
        ts(ind_pp, eps40.view(0, 128, 0, 1), 1.0, ALU.add, 1.0, ALU.subtract, tag="ind1")
        ts(ind_pp, ind_pp, 2.0**40, ALU.mult, tag="ind2")

        def dma3(ch, dst, src3, a, b, tag=""):
            def e(eng, dst=dst, src3=src3, a=a, b=b):
                return eng.dma_start(
                    out=dst.ap().rearrange("p (a b) -> p a b", a=a),
                    in_=src3, allow_slow_non_contiguous=True)
            p.op("sp", e, reads=[], writes=[dst.reg()], dma_ch=ch, tag=tag)

        dma3("const", c_convw, conv_w.rearrange("(db p) k -> p db k", p=128),
             DB, 4)
        dma("const", c_convb, conv_b.rearrange("(db p) -> p db", p=128), slow=True)
        dma("const", c_bdt, b_dt.rearrange("(db p) -> p db", p=128), slow=True)
        dma("const", c_dskip, d_skip.rearrange("(db p) -> p db", p=128), slow=True)
        dma3("const", c_Aneg, a_log.rearrange("(db p) n -> p db n", p=128),
             DB, NST)
        act(c_Aneg, c_Aneg, AF.Exp, tag="Aexp")
        ts(c_Aneg, c_Aneg, -1.0, ALU.mult, tag="Aneg")
        for i, v in enumerate((ln1_g, ln1_b, ln2_g, ln2_b)):
            dma("const", c_ln.view(0, 128, i * MB_D, (i + 1) * MB_D),
                v.rearrange("(m p) -> p m", p=128), slow=True)
        dma("const", c_fln2.view(0, 128, 0, H // 128), fg2.rearrange("(m p) -> p m", p=128), slow=True)
        dma("const", c_fln2.view(0, 128, H // 128, 2 * H // 128),
            fb2.rearrange("(m p) -> p m", p=128), slow=True)
        dma("const", c_bff1, b_ff1.rearrange("(m p) -> p m", p=128), slow=True)
        dma("const", c_bff2, b_ff2.rearrange("(m p) -> p m", p=128), slow=True)
        dma("const", c_b1, b1.rearrange("(m p) -> p m", p=128), slow=True)
        dma("const", c_b2, b2.rearrange("(m p) -> p m", p=128), slow=True)
        memset(h_state, 0.0, tag="h0")

        def prep_w(w_hbm, name, rows, cols, scr, sw_col):
            RB = rows // 128
            CBLK = cols // 128
            acc = tiny.view(0, 128, 0, 1)
            t1 = tiny.view(0, 128, 1, 2)
            def get_slot(rb):
                if cols <= 1024:
                    return wsl.view(0, 128, (rb % 2) * cols, (rb % 2) * cols + cols)
                return dt_fm.view(0, 128, 0, cols)
            for rb in range(RB):
                slot = get_slot(rb)
                dma(f"pw_{name}{rb % 2}", slot,
                    w_hbm[rb * 128:(rb + 1) * 128, :], tag=f"prep_{name}_ld{rb}")
                reduce(t1, slot, ALU.add, absval=True, tag=f"prep_{name}_sum{rb}")
                if rb == 0:
                    copy(acc, t1, eng="dve")
                else:
                    tt(acc, acc, t1, ALU.add)
            tp = ps_mm[7].view(0, 1, 0, 128)
            transpose(tp, tiny.view(0, 128, 0, 1), tag=f"prep_{name}_tp")
            r0 = tiny.view(0, 1, 2, 3)
            reduce(r0, tp, ALU.add, tag=f"prep_{name}_tot")
            ts(r0, r0, 1.0 / (rows * cols), ALU.mult, 1e-5, ALU.max, eng="dve")
            dma_hbm("pw_sw", scr_sw[sw_col:sw_col + 1].rearrange("(a o) -> a o", o=1), r0,
                    "scr_sw", (0, 1, sw_col, sw_col + 1), engine="pool")
            dma_from_hbm(f"swbc{sw_col}", c_sw.view(0, 128, sw_col, sw_col + 1),
                         scr_sw[sw_col:sw_col + 1].rearrange("(o a) -> o a", o=1).broadcast_to((128, 1)),
                         "scr_sw", (0, 1, sw_col, sw_col + 1), engine="pool")
            swc = c_sw.view(0, 128, sw_col, sw_col + 1)
            for rb in range(RB):
                slot = get_slot(rb)
                dma(f"pw_{name}{rb % 2}", slot,
                    w_hbm[rb * 128:(rb + 1) * 128, :], tag=f"prep2_{name}_ld{rb}")
                qs = y_acc.view(0, 128, 0, cols)
                qs2 = xcs.view(0, 128, 0, cols)
                thr = tiny.view(0, 128, 12, 13)
                nthr = tiny.view(0, 128, 13, 14)
                if rb == 0:
                    ts(thr, swc, 0.5, ALU.mult, tag=f"prep_{name}_thr")
                    ts(nthr, thr, -1.0, ALU.mult, tag=f"prep_{name}_nthr")
                ts(qs2, slot, thr, ALU.is_ge, tag=f"prep_{name}_ge{rb}")
                ts(qs, slot, nthr, ALU.is_le, tag=f"prep_{name}_le{rb}")
                tt(qs, qs2, qs, ALU.subtract, tag=f"prep_{name}_q{rb}")
                for cb in range(CBLK):
                    tp2 = ps_mm[6].view(0, 128, 0, 128)
                    transpose(tp2, qs.view(0, 128, cb * 128, (cb + 1) * 128),
                              tag=f"prep_{name}_t{rb}_{cb}")
                    bslot = wslb.view(0, 128, 0, 128)
                    copy(bslot, tp2, eng="dve")
                    dma_hbm(f"pw_st_{name}", scr[cb * 128:(cb + 1) * 128,
                                                rb * 128:(rb + 1) * 128],
                            bslot, "scr_" + name,
                            (cb * 128, (cb + 1) * 128, rb * 128, (rb + 1) * 128),
                            engine="pool")

        prep_w(w1, "w1q", H, D, scr_w1q, 0)
        prep_w(w2, "w2q", D, H, scr_w2q, 1)

        def prep_split(w_hbm, name, rows, cols, scr):
            RB = rows // 128
            CH = max(1, cols // 4096)
            cw = cols // CH
            for rb in range(RB):
                for ch in range(CH):
                    slot = dt_fm.view(0, 128, 0, cw)
                    dma(f"psl_{name}", slot,
                        w_hbm[rb * 128:(rb + 1) * 128, ch * cw:(ch + 1) * cw],
                        tag=f"psl_{name}_{rb}_{ch}")
                    hi = TV(qbf.h, qbf.name, 0, 128, 0, cw)
                    lo = TV(qbf.h, qbf.name, 0, 128, cw, 2 * cw)
                    copy(hi, slot, eng="act", tag=f"psh_{name}_{rb}_{ch}")
                    stt(lo, hi, -1.0, slot, ALU.mult, ALU.add,
                        tag=f"pslo_{name}_{rb}_{ch}")
                    dma_hbm(f"pst_{name}", scr[rb * 128:(rb + 1) * 128,
                                               ch * cw:(ch + 1) * cw], hi,
                            "scr_" + name, (rb * 128, (rb + 1) * 128, ch * cw,
                                            (ch + 1) * cw), engine="pool",
                            tag=f"psth_{name}_{rb}_{ch}")
                    dma_hbm(f"pst_{name}", scr[rb * 128:(rb + 1) * 128,
                                               cols + ch * cw: cols + (ch + 1) * cw],
                            lo, "scr_" + name, (rb * 128, (rb + 1) * 128,
                                                cols + ch * cw, cols + (ch + 1) * cw),
                            engine="pool", tag=f"pstl_{name}_{rb}_{ch}")

        prep_split(w_in, "wz", D, 2 * DI, scr_wz)
        prep_split(w_ff1, "f1", D, FH, scr_f1)
        prep_split(w_ff2, "f2", FI, D, scr_f2)

        if cfg["stop_after"] == "prep":
            debug_dump("prep", [c_Aneg, c_sw])
            p.emit(nc, es)
            return nc, dbg_out

        for ti in range(NTILE):
            tb = ti * TT

            for c in range(NCH):
                dma(f"x{ti % 2}", x_tm.view(0, 128, c * D, (c + 1) * D),
                    x_in[tb + c * 128: tb + (c + 1) * 128, :], tag=f"ldx{ti}_{c}")

            def tm_ln_stats(src_tv, c, mu, r, tag):
                sums = tiny.view(0, 128, 4, 5)
                sq = tiny.view(0, 128, 5, 6)
                reduce(sums, src_tv, ALU.add, tag=f"{tag}_s")
                ts(mu, sums, 1.0 / (src_tv.c1 - src_tv.c0), ALU.mult)
                sqbuf = sr_rep.view(0, 128, 0, src_tv.c1 - src_tv.c0)
                act(sqbuf, src_tv, AF.Square, tag=f"{tag}_sq")
                reduce(sq, sqbuf, ALU.add, tag=f"{tag}_ss")
                ts(sq, sq, 1.0 / (src_tv.c1 - src_tv.c0), ALU.mult)
                mu2 = tiny.view(0, 128, 6, 7)
                tt(mu2, mu, mu, ALU.mult)
                var = tiny.view(0, 128, 7, 8)
                tt(var, sq, mu2, ALU.subtract)
                act(r, var, AF.Sqrt, bias=eps40.view(0, 128, 1, 2), tag=f"{tag}_r")
                recip(r, r, tag=f"{tag}_rr")

            def ln_tm_to_fm(src_tm, dst_bf, g_col, b_col, tag):
                for c in range(NCH):
                    mu = srow.view(0, 128, 0, 1)
                    r = srow.view(0, 128, 1, 2)
                    src = src_tm.view(0, 128, c * D, (c + 1) * D)
                    tm_ln_stats(src, c, mu, r, f"{tag}{c}")
                    xn = sr_rep.view(0, 128, 0, D)
                    ts(xn, src, mu, ALU.subtract, r, ALU.mult, tag=f"{tag}{c}_n")
                    for j in range(MB_D):
                        tp = ps_mm[6 + (j % 2)].view(0, 128, 0, 128)
                        transpose(tp, xn.view(0, 128, j * 128, (j + 1) * 128),
                                  tag=f"{tag}{c}_t{j}")
                        tmpf = v_q.view(0, 128, (j % 2) * 128, (j % 2) * 128 + 128)
                        ts(tmpf, tp, c_ln.view(0, 128, g_col + j, g_col + j + 1),
                           ALU.mult, c_ln.view(0, 128, b_col + j, b_col + j + 1),
                           ALU.add, tag=f"{tag}{c}_a{j}")
                        hi = TV(dst_bf.h, dst_bf.name, 0, 128,
                                2 * j * TT + c * 128, 2 * j * TT + (c + 1) * 128)
                        lo = TV(dst_bf.h, dst_bf.name, 0, 128,
                                (2 * j + 1) * TT + c * 128,
                                (2 * j + 1) * TT + (c + 1) * 128)
                        copy(hi, tmpf, eng="act", tag=f"{tag}{c}_h{j}")
                        stt(lo, hi, -1.0, tmpf, ALU.mult, ALU.add,
                            tag=f"{tag}{c}_l{j}")

            def abf_h(k):
                return TV(a_bf.h, a_bf.name, 0, 128, 2 * k * TT, 2 * k * TT + TT)

            def abf_l(k):
                return TV(a_bf.h, a_bf.name, 0, 128, (2 * k + 1) * TT,
                          (2 * k + 1) * TT + TT)

            ln_tm_to_fm(x_tm, a_bf, 0, MB_D, f"ln1_{ti}")

            if cfg["stop_after"] == "ln1":
                debug_dump("ln1", [a_bf], dt=BF16)
                break

            for mg in range(2):
                for k in range(MB_D):
                    wslot = wslb.view(0, 128, (k % 2) * 2048, (k % 2) * 2048 + 2048)
                    whi = wslot.view(0, 128, 0, 1024)
                    wlo = wslot.view(0, 128, 1024, 2048)
                    dma(f"wz{k % 2}", whi,
                        scr_wz[k * 128:(k + 1) * 128, mg * 1024:(mg + 1) * 1024],
                        tag=f"winh{ti}_{mg}_{k}")
                    dma(f"wz{k % 2}", wlo,
                        scr_wz[k * 128:(k + 1) * 128,
                               2 * DI + mg * 1024: 2 * DI + (mg + 1) * 1024],
                        tag=f"winl{ti}_{mg}_{k}")
                    for m8 in range(8):
                        m = mg * 8 + m8
                        pso = ps_mm[m8].view(0, 128, 0, TT)
                        mm(pso, whi.view(0, 128, m8 * 128, (m8 + 1) * 128),
                           abf_h(k), start=(k == 0), stop=False,
                           group=f"xz{ti}_{m}", tag=f"xzhh{ti}_{m}_{k}")
                        mm(pso, whi.view(0, 128, m8 * 128, (m8 + 1) * 128),
                           abf_l(k), start=False, stop=False,
                           group=f"xz{ti}_{m}", tag=f"xzhl{ti}_{m}_{k}")
                        mm(pso, wlo.view(0, 128, m8 * 128, (m8 + 1) * 128),
                           abf_h(k), start=False, stop=(k == MB_D - 1),
                           group=f"xz{ti}_{m}", tag=f"xzlh{ti}_{m}_{k}")
                for m8 in range(8):
                    m = mg * 8 + m8
                    dst = xcs.view(0, 128, m * (TT + 4) + 4, m * (TT + 4) + 4 + TT)
                    copy(dst, ps_mm[m8].view(0, 128, 0, TT), eng="act",
                         tag=f"xzev{ti}_{m}")

            if cfg["stop_after"] == "xz":
                debug_dump("xz", [xcs])
                break

            if ti == 0:
                for db in range(DB):
                    memset(xcs.view(0, 128, db * (TT + 4), db * (TT + 4) + 4), 0.0,
                           eng="pool", tag=f"halo0_{db}")
            for db in range(DB):
                base = db * (TT + 4)
                xc_t = lambda sh, db=db, base=base: xcs.view(0, 128, base + 4 - sh,
                                                             base + 4 + TT - sh)
                accv = v_q.view(0, 128, 0, TT)
                ts(accv, xc_t(3), c_convw.view(0, 128, db * 4, db * 4 + 1),
                   ALU.mult, tag=f"cv{ti}_{db}_0")
                for k in range(1, 4):
                    stt(accv, xc_t(3 - k),
                        c_convw.view(0, 128, db * 4 + k, db * 4 + k + 1),
                        accv, ALU.mult, ALU.add, tag=f"cv{ti}_{db}_{k}")
                if ti < NTILE - 1:
                    copy(xcs.view(0, 128, base, base + 4),
                         xcs.view(0, 128, base + TT, base + 4 + TT), eng="pool",
                         tag=f"halo{ti}_{db}")
                act(xcs.view(0, 128, base + 4, base + 4 + TT), accv, AF.Silu,
                    bias=c_convb.view(0, 128, db, db + 1), tag=f"silu{ti}_{db}")

            if cfg["stop_after"] == "conv":
                debug_dump("conv", [xcs])
                break

            NW = DTR + 2 * NST
            wxs = wsl.view(0, 128, 0, DB * NW)
            for k in range(DB):
                dma("wx", wxs.view(0, 128, k * NW, (k + 1) * NW),
                    w_x[k * 128:(k + 1) * 128, :], tag=f"wx{ti}_{k}")
            for k in range(DB):
                mm(ps_mm[5].view(0, 96, 0, TT), wxs.view(0, 128, k * NW, (k + 1) * NW).view(0, 128, 0, NW),
                   xcs.view(0, 128, k * (TT + 4) + 4, k * (TT + 4) + 4 + TT),
                   start=(k == 0), stop=(k == DB - 1), group=f"dbc{ti}",
                   tag=f"dbc{ti}_{k}")
            dbc_sb = sr_rep.view(0, 96, 0, TT)
            copy(dbc_sb, ps_mm[5].view(0, 96, 0, TT), eng="act", tag=f"dbcev{ti}")
            dma_hbm("bc_st", scr_rows[0:2 * NST, 0:TT],
                    dbc_sb.view(DTR, DTR + 2 * NST, 0, TT), "scr_rows",
                    (0, 2 * NST, 0, TT), engine="pool", tag=f"bcst{ti}")

            wdt_sb = wsl.view(0, 64, 0, DI)
            dma("wdt", wdt_sb, w_dt[:, :], tag=f"wdt{ti}")
            for db in range(DB):
                pdt = ps_mm[4].view(0, 128, 0, TT)
                mm(pdt, wdt_sb.view(0, 64, db * 128, (db + 1) * 128),
                   dbc_sb.view(0, DTR, 0, TT), start=True, stop=True,
                   group=f"dt{ti}_{db}", tag=f"mmdt{ti}_{db}")
                bcol = c_bdt.view(0, 128, db, db + 1)
                t_abs = v_q.view(0, 128, 0, TT)
                t_ln = v_q.view(0, 128, TT, 2 * TT)
                act(t_abs, pdt, AF.Abs, bias=bcol, tag=f"sp{ti}_{db}_abs")
                act(t_abs, t_abs, AF.Exp, scale=-1.0, tag=f"sp{ti}_{db}_exp")
                act(t_abs, t_abs, AF.Ln, bias=1.0, tag=f"sp{ti}_{db}_ln")
                act(t_ln, pdt, AF.Relu, bias=bcol, tag=f"sp{ti}_{db}_relu")
                tt(dt_fm.view(0, 128, db * TT, (db + 1) * TT), t_abs, t_ln, ALU.add,
                   tag=f"sp{ti}_{db}_add")

            if cfg["stop_after"] == "dt":
                debug_dump("dt", [dt_fm])
                break

            for q in range(DB // 2):
                db0 = q * 2
                vq_s = v_q.view(0, 128, 0, 2 * TT)
                for j in range(2):
                    db = db0 + j
                    tt(vq_s.view(0, 128, j * TT, (j + 1) * TT),
                       dt_fm.view(0, 128, db * TT, (db + 1) * TT),
                       xcs.view(0, 128, db * (TT + 4) + 4, db * (TT + 4) + 4 + TT),
                       ALU.mult, tag=f"v{ti}_{q}_{j}", eng="pool" if j else "dve")
                for n in range(NST):
                    par = n % 2
                    bn = bc_rep.view(0, 128, par * 2 * TT, par * 2 * TT + TT)
                    cn = bc_rep.view(0, 128, par * 2 * TT + TT, par * 2 * TT + 2 * TT)
                    def ebc(eng, n=n, par=par):
                        srcap = scr_rows[n::NST, 0:TT][0:2, :]
                        srcap = srcap.rearrange("(o a) b -> o a b", o=1)
                        srcap = srcap.broadcast_to((128, 2, TT))
                        dstap = bc_rep.ap()[:, par * 2 * TT:(par + 1) * 2 * TT]
                        dstap = dstap.rearrange("p (a b) -> p a b", a=2)
                        return eng.dma_start(out=dstap, in_=srcap)
                    p.op("sp", ebc, reads=[("scr_rows", n, NST + n + 1, 0, TT)],
                         writes=[(bc_rep.name, 0, 128, par * 2 * TT, (par + 1) * 2 * TT)],
                         dma_ch=f"bcrep{par}", tag=f"ldBC{ti}_{q}_{n}")
                    uq = u_q.view(0, 128, par * 2 * TT, (par + 1) * 2 * TT)
                    hq = h_q.view(0, 128, par * 2 * TT, (par + 1) * 2 * TT)
                    tt_bcast(uq, vq_s, bn, ALU.mult, 2, tag=f"u{ti}_{q}_{n}")
                    for j in range(2):
                        db = db0 + j
                        da = dA_t.view(0, 128, par * TT, (par + 1) * TT)
                        act(da, dt_fm.view(0, 128, db * TT, (db + 1) * TT), AF.Exp,
                            scale=c_Aneg.view(0, 128, db * NST + n, db * NST + n + 1),
                            tag=f"dA{ti}_{q}_{n}_{j}")
                        scan_op(hq.view(0, 128, j * TT, (j + 1) * TT), da,
                                uq.view(0, 128, j * TT, (j + 1) * TT),
                                h_state.view(0, 128, db * NST + n, db * NST + n + 1),
                                tag=f"scan{ti}_{q}_{n}_{j}")
                    def esv(eng, q=q, n=n, db0=db0, hq=hq):
                        src2 = hq.ap().rearrange("p (j t) -> p j t", j=2)[:, :, TT - 1:TT]
                        src2 = src2.rearrange("p j t -> p (j t)")
                        dst = h_state.ap().rearrange("p (db n) -> p db n", n=NST)[
                            :, db0:db0 + 2, n:n + 1].rearrange("p j t -> p (j t)")
                        return eng.activation(out=dst, in_=src2, func=AF.Copy)
                    p.op("act", esv, reads=[(h_q.name, 0, 128, hq.c0, hq.c1)],
                         writes=[(h_state.name, 0, 128, db0 * NST + n, (db0 + 1) * NST + n + 1)],
                         tag=f"hs{ti}_{q}_{n}")
                    tt(uq.view(0, 128, 0, TT), hq.view(0, 128, 0, TT), cn,
                       ALU.mult, tag=f"hc0{ti}_{q}_{n}")
                    tt(uq.view(0, 128, TT, 2 * TT), hq.view(0, 128, TT, 2 * TT), cn,
                       ALU.mult, eng="pool", tag=f"hc1{ti}_{q}_{n}")
                    for j2 in range(2):
                        mm(ps_mm[4 + j2].view(0, 128, 0, TT), ident,
                           uq.view(0, 128, j2 * TT, (j2 + 1) * TT),
                           start=(n == 0), stop=(n == NST - 1),
                           group=f"yac{ti}_{q}_{j2}", tag=f"yac{ti}_{q}_{n}_{j2}")
                for j2 in range(2):
                    copy(y_acc.view(0, 128, (db0 + j2) * TT, (db0 + j2 + 1) * TT),
                         ps_mm[4 + j2].view(0, 128, 0, TT), eng="act",
                         tag=f"yev{ti}_{q}_{j2}")
                zbank = 6 + (q % 2)
                for j in range(2):
                    db = db0 + j
                    zw = wslb.view(0, 128, (j % 2) * 2048, (j % 2) * 2048 + 2048)
                    zwh = zw.view(0, 128, 0, 1024)
                    zwl = zw.view(0, 128, 1024, 2048)
                    for half, dst, co in ((0, zwh, 2048 + db * 128),
                                          (1, zwl, 2 * DI + 2048 + db * 128)):
                        def ezw(eng, dst=dst, co=co):
                            srcap = scr_wz[:, co:co + 128]
                            srcap = srcap.rearrange("(k p) c -> p k c", p=128)
                            dstap = dst.ap().rearrange("p (k c) -> p k c", k=MB_D)
                            return eng.dma_start(out=dstap, in_=srcap)
                        p.op("sp", ezw, reads=[("scr_wz", 0, D, co, co + 128)],
                             writes=[dst.reg()], dma_ch=f"wz{j % 2}",
                             tag=f"zw{ti}_{db}_{half}")
                    psz = ps_mm[zbank].view(0, 128, 0, TT)
                    for k in range(MB_D):
                        mm(psz, zwh.view(0, 128, k * 128, (k + 1) * 128), abf_h(k),
                           start=(k == 0), stop=False,
                           group=f"z{ti}_{db}", tag=f"zhh{ti}_{db}_{k}")
                        mm(psz, zwh.view(0, 128, k * 128, (k + 1) * 128), abf_l(k),
                           start=False, stop=False,
                           group=f"z{ti}_{db}", tag=f"zhl{ti}_{db}_{k}")
                        mm(psz, zwl.view(0, 128, k * 128, (k + 1) * 128), abf_h(k),
                           start=False, stop=(k == MB_D - 1),
                           group=f"z{ti}_{db}", tag=f"zlh{ti}_{db}_{k}")
                    act(psz, psz, AF.Silu, tag=f"gz{ti}_{db}")
                    yv = y_acc.view(0, 128, db * TT, (db + 1) * TT)
                    stt(yv, xcs.view(0, 128, db * (TT + 4) + 4,
                                     db * (TT + 4) + 4 + TT),
                        c_dskip.view(0, 128, db, db + 1), yv, ALU.mult, ALU.add,
                        tag=f"gd{ti}_{db}")
                    tt(dt_fm.view(0, 128, db * TT, (db + 1) * TT), yv, psz, ALU.mult,
                       tag=f"gm{ti}_{db}")

            if cfg["stop_after"] == "scan":
                debug_dump("scan", [y_acc])
                break

            for k in range(DB):
                wslot = wsl.view(0, 128, (k % 2) * 1024, (k % 2) * 1024 + D)
                dma(f"wo{k % 2}", wslot, w_out[k * 128:(k + 1) * 128, :],
                    tag=f"wo{ti}_{k}")
                for m in range(MB_D):
                    mm(ps_mm[m].view(0, 128, 0, TT),
                       wslot.view(0, 128, m * 128, (m + 1) * 128),
                       dt_fm.view(0, 128, k * TT, (k + 1) * TT),
                       start=(k == 0), stop=(k == DB - 1), group=f"wo{ti}_{m}",
                       tag=f"mmo{ti}_{m}_{k}")
            for m in range(MB_D):
                stg = u_q.view(0, 128, (m % 4) * TT, (m % 4) * TT + TT)
                copy(stg, ps_mm[m].view(0, 128, 0, TT), eng="act", tag=f"woev{ti}_{m}")
                for c in range(NCH):
                    tbk = m if (m == 0 or c % 2 == 0) else m - 1
                    tp = ps_mm[tbk].view(0, 128, 0, 128)
                    transpose(tp, stg.view(0, 128, c * 128, (c + 1) * 128),
                              tag=f"brt{ti}_{c}_{m}")
                    xd = x_tm.view(0, 128, c * D + m * 128, c * D + (m + 1) * 128)
                    tt(xd, xd, tp, ALU.add, tag=f"bra{ti}_{c}_{m}")

            if cfg["stop_after"] == "wout":
                debug_dump("wout", [x_tm])
                break

            ln_tm_to_fm(x_tm, a_bf, 2 * MB_D, 3 * MB_D, f"ln2_{ti}")

            if cfg["stop_after"] == "ln2":
                debug_dump("ln2", [a_bf], dt=BF16)
                break

            for mg in range(8):
                for k in range(MB_D):
                    wslot = wslb.view(0, 128, (k % 2) * 2048, (k % 2) * 2048 + 2048)
                    for half, off in ((0, 0), (1, 1024)):
                        def eld(eng, k=k, mg=mg, wslot=wslot, half=half, off=off):
                            return eng.dma_start(
                                out=wslot.ap()[:, off:off + 1024].rearrange(
                                    "p (h x) -> p h x", h=2),
                                in_=scr_f1[k * 128:(k + 1) * 128,
                                           half * FH:(half + 1) * FH].rearrange(
                                    "p (h x) -> p h x", h=2)[:, :, mg * 512:(mg + 1) * 512])
                        p.op("sp", eld, reads=[("scr_f1", k * 128, (k + 1) * 128,
                                                half * FH + mg * 512,
                                                half * FH + FH)],
                             writes=[(wslot.name, 0, 128, wslot.c0 + off,
                                      wslot.c0 + off + 1024)], dma_ch=f"wf{k % 2}",
                             tag=f"ff1ld{ti}_{mg}_{k}_{half}")
                    for i in range(4):
                        for bank, woff, gtag in ((i, 0, "a"), (4 + i, 512, "g")):
                            pso = ps_mm[bank].view(0, 128, 0, TT)
                            grp = f"f1{gtag}{ti}_{mg}_{i}"
                            mm(pso, wslot.view(0, 128, woff + i * 128,
                                               woff + (i + 1) * 128),
                               abf_h(k), start=(k == 0), stop=False, group=grp,
                               tag=f"f1{gtag}hh{ti}_{mg}_{i}_{k}")
                            mm(pso, wslot.view(0, 128, woff + i * 128,
                                               woff + (i + 1) * 128),
                               abf_l(k), start=False, stop=False, group=grp,
                               tag=f"f1{gtag}hl{ti}_{mg}_{i}_{k}")
                            mm(pso, wslot.view(0, 128, 1024 + woff + i * 128,
                                               1024 + woff + (i + 1) * 128),
                               abf_h(k), start=False, stop=(k == MB_D - 1), group=grp,
                               tag=f"f1{gtag}lh{ti}_{mg}_{i}_{k}")
                if cfg["stop_after"] == "ff1w" and mg == 1:
                    break
                for i in range(4):
                    ma = mg * 4 + i
                    aev = v_q.view(0, 128, (i % 2) * TT, (i % 2) * TT + TT)
                    gev = u_q.view(0, 128, (i % 2) * TT, (i % 2) * TT + TT)
                    act(aev, ps_mm[i].view(0, 128, 0, TT), AF.Identity,
                        bias=c_bff1.view(0, 128, ma, ma + 1), tag=f"f1ae{ti}_{ma}")
                    act(gev, ps_mm[4 + i].view(0, 128, 0, TT), AF.Gelu_apprx_tanh,
                        bias=c_bff1.view(0, 128, 32 + ma, 32 + ma + 1),
                        tag=f"f1ge{ti}_{ma}")
                    tt(aev, aev, gev, ALU.mult, tag=f"f1m{ti}_{ma}")
                    agh = TV(qbf.h, qbf.name, 0, 128, (i % 2) * TT, (i % 2) * TT + TT)
                    agl = TV(qbf.h, qbf.name, 0, 128, 2 * TT + (i % 2) * TT,
                             2 * TT + (i % 2) * TT + TT)
                    copy(agh, aev, eng="act", tag=f"agh{ti}_{ma}")
                    stt(agl, agh, -1.0, aev, ALU.mult, ALU.add, tag=f"agl{ti}_{ma}")
                    dma_hbm(f"agst{i % 2}", scr_ag[ma * 128:(ma + 1) * 128, 0:TT],
                            agh, "scr_ag", (ma * 128, (ma + 1) * 128, 0, TT),
                            engine="pool", tag=f"agsth{ti}_{ma}")
                    dma_hbm(f"agst{i % 2}", scr_ag[ma * 128:(ma + 1) * 128, TT:2 * TT],
                            agl, "scr_ag", (ma * 128, (ma + 1) * 128, TT, 2 * TT),
                            engine="pool", tag=f"agstl{ti}_{ma}")
            if cfg["stop_after"] == "ff1w":
                break
            if cfg["stop_after"] == "ff1":
                debug_dump_hbm("ff1", "scr_ag", scr_ag[:, :], (FI, 2 * TT), dt=BF16)
                break
            for k in range(FI // 128):
                wslot = wslb.view(0, 128, (k % 2) * 2048, (k % 2) * 2048 + 2 * D)
                dma(f"wf{k % 2}", wslot, scr_f2[k * 128:(k + 1) * 128, :],
                    tag=f"ff2ld{ti}_{k}")
                agk = TV(qbf.h, qbf.name, 0, 128, 4 * TT + (k % 2) * 2 * TT,
                         4 * TT + (k % 2) * 2 * TT + 2 * TT)
                dma_from_hbm(f"agld{k % 2}", agk, scr_ag[k * 128:(k + 1) * 128, :],
                             "scr_ag", (k * 128, (k + 1) * 128, 0, 2 * TT),
                             engine="sp", tag=f"agld{ti}_{k}")
                aghk = agk.view(0, 128, 0, TT)
                aglk = agk.view(0, 128, TT, 2 * TT)
                for m in range(MB_D):
                    pso = ps_mm[m].view(0, 128, 0, TT)
                    grp = f"f2{ti}_{m}"
                    KL = FI // 128 - 1
                    mm(pso, wslot.view(0, 128, m * 128, (m + 1) * 128), aghk,
                       start=(k == 0), stop=False, group=grp, tag=f"f2hh{ti}_{m}_{k}")
                    mm(pso, wslot.view(0, 128, m * 128, (m + 1) * 128), aglk,
                       start=False, stop=False, group=grp, tag=f"f2hl{ti}_{m}_{k}")
                    mm(pso, wslot.view(0, 128, D + m * 128, D + (m + 1) * 128), aghk,
                       start=False, stop=(k == KL), group=grp, tag=f"f2lh{ti}_{m}_{k}")
            for m in range(MB_D):
                stg = u_q.view(0, 128, (m % 4) * TT, (m % 4) * TT + TT)
                act(stg, ps_mm[m].view(0, 128, 0, TT), AF.Identity,
                    bias=c_bff2.view(0, 128, m, m + 1), tag=f"f2ev{ti}_{m}")
                for c in range(NCH):
                    tbk = m if (m == 0 or c % 2 == 0) else m - 1
                    tp = ps_mm[tbk].view(0, 128, 0, 128)
                    transpose(tp, stg.view(0, 128, c * 128, (c + 1) * 128),
                              tag=f"fft{ti}_{c}_{m}")
                    xd = x_tm.view(0, 128, c * D + m * 128, c * D + (m + 1) * 128)
                    tt(xd, xd, tp, ALU.add, tag=f"ffa{ti}_{c}_{m}")

            for c in range(NCH):
                x0c = v_q.view(0, 128, 0, D)
                dma(f"xr{c % 2}", x0c, x_in[tb + c * 128: tb + (c + 1) * 128, :],
                    tag=f"ldxr{ti}_{c}")
                xd = x_tm.view(0, 128, c * D, (c + 1) * D)
                tt(xd, xd, x0c, ALU.add, tag=f"pl1a{ti}_{c}")
                mu = srow.view(0, 128, 0, 1)
                r = srow.view(0, 128, 1, 2)
                tm_ln_stats(xd, c, mu, r, f"pl1_{ti}{c}")
                ts(xd, xd, mu, ALU.subtract, r, ALU.mult, tag=f"pl1n{ti}_{c}")

            if cfg["stop_after"] == "postln1":
                debug_dump("postln1", [x_tm])
                break

            HB = H // 128
            fg1rep = dA_t.view(0, 128, 0, D)
            fb1rep = bc_rep.view(0, 128, 1024, 1024 + D)
            dma("fgrep", fg1rep,
                fg1.rearrange("(o d) -> o d", o=1).broadcast_to((128, D)),
                tag=f"fg1l{ti}")
            dma("fbrep", fb1rep,
                fb1.rearrange("(o d) -> o d", o=1).broadcast_to((128, D)),
                tag=f"fb1l{ti}")
            for c in range(NCH):
                srcx = x_tm.view(0, 128, c * D, (c + 1) * D)
                mu = srow.view(0, 128, 0, 1)
                r = srow.view(0, 128, 1, 2)
                tm_ln_stats(srcx, c, mu, r, f"fl1_{ti}{c}")
                xa = sr_rep.view(0, 128, 0, D)
                ts(xa, srcx, mu, ALU.subtract, r, ALU.mult, tag=f"fl1n{ti}_{c}")
                tt(xa, xa, fg1rep, ALU.mult, tag=f"fl1g{ti}_{c}")
                tt(xa, xa, fb1rep, ALU.add, tag=f"fl1b{ti}_{c}")
                am = srow.view(0, 128, 3, 4)
                reduce(am, xa, ALU.max, absval=True, tag=f"q1m{ti}_{c}")
                ts(am, am, 1e-5, ALU.max, tag=f"q1c{ti}_{c}")
                sc = srow.view(0, 128, 4, 5)
                recip(sc, am, tag=f"q1r{ti}_{c}")
                ts(sc, sc, 127.0, ALU.mult, tag=f"q1s{ti}_{c}")
                iv = srow.view(0, 128, 5, 6)
                ts(iv, am, 1.0 / 127.0, ALU.mult, tag=f"q1i{ti}_{c}")
                dma_hbm("q1iv", scr_rows[32 + c:33 + c, 0:128].rearrange("a b -> b a"),
                        iv, "scr_rows", (32 + c, 33 + c, 0, 128), engine="pool",
                        tag=f"q1ivs{ti}_{c}")
                yq = xa
                ts(yq, xa, sc, ALU.mult, tag=f"q1y{ti}_{c}")
                AA = u_q.view(0, 128, 0, D)
                BB = u_q.view(0, 128, D, 2 * D)
                ts(AA, yq, C1, ALU.add, C1, ALU.subtract, tag=f"q1A{ti}_{c}")
                ts(BB, yq, C2, ALU.add, C2, ALU.subtract, tag=f"q1B{ti}_{c}")
                tt(BB, BB, AA, ALU.subtract, tag=f"q1t{ti}_{c}")
                stt(yq, BB, ind_pp, AA, ALU.mult, ALU.add, tag=f"q1b{ti}_{c}")
                ts(yq, yq, -128.0, ALU.max, 127.0, ALU.min, tag=f"q1cl{ti}_{c}")
                for j in range(MB_D):
                    tp = ps_mm[6 + (j % 2)].view(0, 128, 0, 128)
                    transpose(tp, yq.view(0, 128, j * 128, (j + 1) * 128),
                              tag=f"q1t{ti}_{c}_{j}")
                    copy(TV(qbf.h, qbf.name, 0, 128, j * TT + c * 128,
                            j * TT + (c + 1) * 128), tp, eng="dve",
                         tag=f"q1e{ti}_{c}_{j}")
            if cfg["stop_after"] == "q1":
                debug_dump("q1", [TV(qbf.h, qbf.name, 0, 128, 0, MB_D * TT)], dt=BF16)
                break
            iv_rep_f = dA_t.view(0, 128, 0, TT)
            def eivl(eng, iv_rep_f=iv_rep_f):
                srcap = scr_rows[32:32 + NCH, 0:128]
                srcap = srcap.rearrange("(o a) b -> o a b", o=1)
                srcap = srcap.broadcast_to((128, NCH, 128))
                dstap = iv_rep_f.ap().rearrange("p (a b) -> p a b", a=NCH)
                return eng.dma_start(out=dstap, in_=srcap)
            p.op("sp", eivl, reads=[("scr_rows", 32, 32 + NCH, 0, 128)],
                 writes=[iv_rep_f.reg()], dma_ch="ivrep", tag=f"q1ivl{ti}")
            if ti == 0:
                memset(tiny.view(0, 128, 8, 9), 1.0, eng="pool", tag="ones")
            onescol = tiny.view(0, 128, 8, 9)

            psS = ps_mm[6].view(0, 1, 0, TT)
            psQ = ps_mm[7].view(0, 1, 0, TT)
            for mg in range(HB // 4):
                for k in range(MB_D):
                    wslot = wslb.view(0, 128, (k % 2) * 1024, (k % 2) * 1024 + 512)
                    dma(f"w1q{k % 2}", wslot,
                        scr_w1q[k * 128:(k + 1) * 128, mg * 512:(mg + 1) * 512],
                        tag=f"w1l{ti}_{mg}_{k}")
                    for i in range(4):
                        mm(ps_mm[i].view(0, 128, 0, TT),
                           wslot.view(0, 128, i * 128, (i + 1) * 128),
                           TV(qbf.h, qbf.name, 0, 128, k * TT, (k + 1) * TT),
                           start=(k == 0), stop=(k == MB_D - 1),
                           group=f"m1{ti}_{mg}_{i}", tag=f"m1{ti}_{mg}_{i}_{k}")
                for i in range(4):
                    m = mg * 4 + i
                    g1m = v_q.view(0, 128, (i % 2) * TT, (i % 2) * TT + TT)
                    tt(g1m, ps_mm[i].view(0, 128, 0, TT), iv_rep_f, ALU.mult,
                       tag=f"m1e{ti}_{m}")
                    act(g1m, g1m, AF.Gelu, scale=c_sw.view(0, 128, 0, 1),
                        bias=c_b1.view(0, 128, m, m + 1), tag=f"m1g{ti}_{m}")
                    sq = u_q.view(0, 128, (i % 2) * TT, (i % 2) * TT + TT)
                    act(sq, g1m, AF.Square, tag=f"m1sq{ti}_{m}")
                    mm(psS, onescol, g1m, start=(m == 0), stop=(m == HB - 1),
                       group=f"lnS{ti}", tag=f"lnS{ti}_{m}")
                    mm(psQ, onescol, sq, start=(m == 0), stop=(m == HB - 1),
                       group=f"lnQ{ti}", tag=f"lnQm{ti}_{m}")
                    dma_hbm(f"g1st{i % 2}", scr_g1[m * 128:(m + 1) * 128, :], g1m,
                            "scr_g1", (m * 128, (m + 1) * 128, 0, TT),
                            engine="pool", tag=f"g1st{ti}_{m}")
            if cfg["stop_after"] == "mm1":
                debug_dump_hbm("mm1", "scr_g1", scr_g1[:, :], (H, TT))
                break
            mu1 = v_q.view(0, 1, 0, TT)
            vv = v_q.view(0, 1, TT, 2 * TT)
            m2r = u_q.view(0, 1, 0, TT)
            act(mu1, psS, AF.Identity, scale=1.0 / H, tag=f"lnmu{ti}")
            act(vv, psQ, AF.Identity, scale=1.0 / H, tag=f"lnq{ti}")
            tt(m2r, mu1, mu1, ALU.mult, tag=f"lnm2{ti}")
            tt(vv, vv, m2r, ALU.subtract, tag=f"lnv{ti}")
            act(vv, vv, AF.Sqrt, bias=eps40.view(0, 1, 1, 2), tag=f"lnr{ti}")
            recip(vv, vv, tag=f"lnrr{ti}")
            dma_hbm("lnmu", scr_rows[36:37, 0:TT], mu1, "scr_rows",
                    (36, 37, 0, TT), engine="pool", tag=f"lnmus{ti}")
            dma_hbm("lnr", scr_rows[37:38, 0:TT], vv, "scr_rows",
                    (37, 38, 0, TT), engine="pool", tag=f"lnrs{ti}")
            mu_rep = sr_rep.view(0, 128, 0, TT)
            r_rep = sr_rep.view(0, 128, TT, 2 * TT)
            dma_from_hbm("murep", mu_rep,
                         scr_rows[36:37, 0:TT].broadcast_to((128, TT)),
                         "scr_rows", (36, 37, 0, TT), engine="sp", tag=f"lnmul{ti}")
            dma_from_hbm("rrep", r_rep,
                         scr_rows[37:38, 0:TT].broadcast_to((128, TT)),
                         "scr_rows", (37, 38, 0, TT), engine="sp", tag=f"lnrl{ti}")

            mx = dA_t
            mx = u_q.view(0, 128, TT, 2 * TT)
            for grp in range(HB // 4):
                gchunk = y_acc.view(0, 128, (grp % 2) * 4 * TT, ((grp % 2) * 4 + 4) * TT)
                def eg1ldA(eng, grp=grp, gchunk=gchunk):
                    srcap = scr_g1[grp * 512:(grp + 1) * 512, :].rearrange(
                        "(j p) t -> p j t", p=128)
                    dstap = gchunk.ap().rearrange("p (j t) -> p j t", j=4)
                    return eng.dma_start(out=dstap, in_=srcap)
                p.op("sp", eg1ldA, reads=[("scr_g1", grp * 512, (grp + 1) * 512, 0, TT)],
                     writes=[gchunk.reg()], dma_ch=f"g1ldA{grp % 2}",
                     tag=f"g1ldA{ti}_{grp}")
                tt_bcast(gchunk, gchunk, mu_rep, ALU.subtract, 4, tag=f"pAn{ti}_{grp}")
                tt_bcast(gchunk, gchunk, r_rep, ALU.mult, 4, tag=f"pAr{ti}_{grp}")
                for i in range(4):
                    j = grp * 4 + i
                    gj = gchunk.view(0, 128, i * TT, (i + 1) * TT)
                    ts(gj, gj, c_fln2.view(0, 128, j, j + 1), ALU.mult,
                       c_fln2.view(0, 128, HB + j, HB + j + 1), ALU.add,
                       tag=f"pAa{ti}_{grp}_{i}")
                def eg1n(eng, grp=grp, gchunk=gchunk):
                    dstap = scr_g1n[grp * 512:(grp + 1) * 512, :].rearrange(
                        "(j p) t -> p j t", p=128)
                    srcap = gchunk.ap().rearrange("p (j t) -> p j t", j=4)
                    return eng.dma_start(out=dstap, in_=srcap)
                p.op("pool", eg1n, reads=[gchunk.reg()],
                     writes=[("scr_g1n", grp * 512, (grp + 1) * 512, 0, TT)],
                     dma_ch=f"g1n{grp % 2}", tag=f"g1n{ti}_{grp}")
                gm = u_q.view(0, 128, 0, TT)
                def ered(eng, gchunk=gchunk, gm=gm):
                    src3 = gchunk.ap().rearrange("p (j t) -> p t j", j=4)
                    return eng.tensor_reduce(out=gm.ap(), in_=src3, axis=AX.X,
                                             op=ALU.max, apply_absolute_value=True)
                p.op("dve", ered, reads=[gchunk.reg()], writes=[gm.reg()],
                     tag=f"pAm{ti}_{grp}")
                if grp == 0:
                    copy(mx, gm, eng="dve", tag=f"pAc{ti}_{grp}")
                else:
                    tt(mx, mx, gm, ALU.max, tag=f"pAx{ti}_{grp}")
            for c in range(NCH):
                tp = ps_mm[6].view(0, 128, 0, 128)
                transpose(tp, mx.view(0, 128, c * 128, (c + 1) * 128),
                          tag=f"q2tp{ti}_{c}")
                mcol = tiny.view(0, 128, 9, 10)
                reduce(mcol, tp.view(0, 128, 0, 128), ALU.max, tag=f"q2rd{ti}_{c}")
                ts(mcol, mcol, 1e-5, ALU.max, tag=f"q2cl{ti}_{c}")
                iv2 = tiny.view(0, 128, 10, 11)
                ts(iv2, mcol, 1.0 / 127.0, ALU.mult, tag=f"q2iv{ti}_{c}")
                sc2 = tiny.view(0, 128, 11, 12)
                recip(sc2, mcol, tag=f"q2rc{ti}_{c}")
                ts(sc2, sc2, 127.0, ALU.mult, tag=f"q2sc{ti}_{c}")
                cb = c * 128
                dma_hbm("q2s", scr_rows[38:39, cb:cb + 128].rearrange("a b -> b a"),
                        sc2, "scr_rows", (38, 39, cb, cb + 128),
                        engine="pool", tag=f"q2ss{ti}_{c}")
                dma_hbm("q2i", scr_rows[39:40, cb:cb + 128].rearrange("a b -> b a"),
                        iv2, "scr_rows", (39, 40, cb, cb + 128),
                        engine="pool", tag=f"q2is{ti}_{c}")
            s_rep = bc_rep.view(0, 128, 0, TT)
            i_rep2 = bc_rep.view(0, 128, TT, 2 * TT)
            dma_from_hbm("q2sr", s_rep,
                         scr_rows[38:39, 0:TT].broadcast_to((128, TT)),
                         "scr_rows", (38, 39, 0, TT), engine="sp", tag=f"q2srl{ti}")
            dma_from_hbm("q2ir", i_rep2,
                         scr_rows[39:40, 0:TT].broadcast_to((128, TT)),
                         "scr_rows", (39, 40, 0, TT), engine="sp", tag=f"q2irl{ti}")

            if cfg["stop_after"] == "q2m":
                debug_dump("q2m", [sr_rep.view(0, 128, 0, 2 * TT),
                                   bc_rep.view(0, 128, 0, 2 * TT)])
                break
            for grp in range(HB // 4):
                gchunk = y_acc.view(0, 128, (grp % 2) * 4 * TT, ((grp % 2) * 4 + 4) * TT)
                def eg1ldB(eng, grp=grp, gchunk=gchunk):
                    srcap = scr_g1n[grp * 512:(grp + 1) * 512, :].rearrange(
                        "(j p) t -> p j t", p=128)
                    dstap = gchunk.ap().rearrange("p (j t) -> p j t", j=4)
                    return eng.dma_start(out=dstap, in_=srcap)
                p.op("sp", eg1ldB, reads=[("scr_g1n", grp * 512, (grp + 1) * 512, 0, TT)],
                     writes=[gchunk.reg()], dma_ch=f"g1ldB{grp % 2}",
                     tag=f"g1ldB{ti}_{grp}")
                tt_bcast(gchunk, gchunk, s_rep, ALU.mult, 4, tag=f"q2y{ti}_{grp}")
                AA = u_q.view(0, 128, 0, 2 * TT)
                BB = u_q.view(0, 128, 2 * TT, 4 * TT)
                for hf in range(2):
                    gh = gchunk.view(0, 128, hf * 2 * TT, (hf + 1) * 2 * TT)
                    ts(AA, gh, C1, ALU.add, C1, ALU.subtract, tag=f"q2A{ti}_{grp}_{hf}")
                    ts(BB, gh, C2, ALU.add, C2, ALU.subtract, tag=f"q2B{ti}_{grp}_{hf}")
                    tt(BB, BB, AA, ALU.subtract, tag=f"q2d{ti}_{grp}_{hf}")
                    stt(gh, BB, ind_pp, AA, ALU.mult, ALU.add, tag=f"q2bl{ti}_{grp}_{hf}")
                    ts(gh, gh, -128.0, ALU.max, 127.0, ALU.min, tag=f"q2c{ti}_{grp}_{hf}")
                q2c = TV(qbf.h, qbf.name, 0, 128, MB_D * TT + (grp % 2) * 4 * TT,
                         MB_D * TT + (grp % 2) * 4 * TT + 4 * TT)
                copy(q2c, gchunk, eng="dve", tag=f"q2cast{ti}_{grp}")
                if grp == 0:
                    debug_dump("q2grp0", [gchunk])
                for kk in range(4):
                    k = grp * 4 + kk
                    wslot = wslb.view(0, 128, (k % 2) * 1024, (k % 2) * 1024 + D)
                    dma(f"w2q{k % 2}", wslot, scr_w2q[k * 128:(k + 1) * 128, :],
                        tag=f"w2l{ti}_{k}")
                    for m in range(MB_D):
                        mm(ps_mm[m].view(0, 128, 0, TT),
                           wslot.view(0, 128, m * 128, (m + 1) * 128),
                           q2c.view(0, 128, kk * TT, (kk + 1) * TT),
                           start=(k == 0), stop=(k == HB - 1),
                           group=f"m2{ti}_{m}", tag=f"m2{ti}_{m}_{k}")
            if cfg["stop_after"] == "mm2":
                pass

            for m in range(MB_D):
                ev = v_q.view(0, 128, 0, TT)
                tt(ev, ps_mm[m].view(0, 128, 0, TT), i_rep2, ALU.mult,
                   tag=f"m2e{ti}_{m}")
                stg = u_q.view(0, 128, (m % 2) * TT, (m % 2) * TT + TT)
                act(stg, ev, AF.Identity,
                    scale=c_sw.view(0, 128, 1, 2), bias=c_b2.view(0, 128, m, m + 1),
                    tag=f"m2b{ti}_{m}")
                for c in range(NCH):
                    tbk = m if (m == 0 or c % 2 == 0) else m - 1
                    tp = ps_mm[tbk].view(0, 128, 0, 128)
                    transpose(tp, stg.view(0, 128, c * 128, (c + 1) * 128),
                              tag=f"m2t{ti}_{c}_{m}")
                    xd = x_tm.view(0, 128, c * D + m * 128, c * D + (m + 1) * 128)
                    tt(xd, xd, tp, ALU.add, tag=f"m2a{ti}_{c}_{m}")
            for c in range(NCH):
                xd = x_tm.view(0, 128, c * D, (c + 1) * D)
                mu = srow.view(0, 128, 0, 1)
                r = srow.view(0, 128, 1, 2)
                tm_ln_stats(xd, c, mu, r, f"pl2_{ti}{c}")
                ts(xd, xd, mu, ALU.subtract, r, ALU.mult, tag=f"pl2n{ti}_{c}")
                dma_hbm(f"out{ti % 2}", out_hbm[tb + c * 128: tb + (c + 1) * 128, :],
                        xd, "out", (tb + c * 128, tb + (c + 1) * 128, 0, D),
                        engine="sp", tag=f"st{ti}_{c}")

        p.emit(nc, es)
    return nc, dbg_out


def kernel(**inputs):
    cfg = default_cfg()
    nc, _ = build_program(cfg)

    x_full = np.ascontiguousarray(np.asarray(inputs["L0_attn_x"], np.float32))
    B = x_full.shape[0]
    base = {
        "w_in": inputs["L0_attn_w_in"], "conv_w": inputs["L0_attn_conv_w"],
        "conv_b": inputs["L0_attn_conv_b"], "w_x": inputs["L0_attn_w_x"],
        "w_dt": inputs["L0_attn_w_dt"], "b_dt": inputs["L0_attn_b_dt"],
        "a_log": inputs["L0_attn_a_log"], "d_skip": inputs["L0_attn_d_skip"],
        "w_out": inputs["L0_attn_w_out"], "ln1_g": inputs["L0_attn_ln1_g"],
        "ln1_b": inputs["L0_attn_ln1_b"], "ln2_g": inputs["L0_attn_ln2_g"],
        "ln2_b": inputs["L0_attn_ln2_b"], "w_ff1": inputs["L0_attn_w_ff1"],
        "b_ff1": inputs["L0_attn_b_ff1"], "w_ff2": inputs["L0_attn_w_ff2"],
        "b_ff2": inputs["L0_attn_b_ff2"], "f_ln1_g": inputs["L0_ffn_ln1_g"],
        "f_ln1_b": inputs["L0_ffn_ln1_b"], "w1": inputs["L0_ffn_w1"],
        "b1": inputs["L0_ffn_b1"], "f_ln2_g": inputs["L0_ffn_ln2_g"],
        "f_ln2_b": inputs["L0_ffn_ln2_b"], "w2": inputs["L0_ffn_w2"],
        "b2": inputs["L0_ffn_b2"],
        "ident_hbm": np.eye(128, dtype=np.float32),
        "eps40_hbm": np.concatenate([np.full((128, 1), 2.0**-40, np.float32), np.full((128, 1), 1e-5, np.float32)], axis=1),
    }
    base = {k: np.ascontiguousarray(np.asarray(v, np.float32)) for k, v in base.items()}
    in_maps = []
    for b in range(B):
        m = dict(base)
        m["x"] = np.ascontiguousarray(x_full[b])
        in_maps.append(m)
    res = run_bass_kernel_spmd(nc, in_maps, core_ids=list(range(B)))
    out = np.stack([np.asarray(r["out"], np.float32) for r in res.results], axis=0)
    return out
